# version 06
# baseline (speedup 1.0000x reference)
import numpy as np
import concourse.bass as bass
import concourse.mybir as mybir
from concourse.bass_utils import run_bass_kernel_spmd
from contextlib import ExitStack

F32 = mybir.dt.float32
BF16 = mybir.dt.bfloat16
I32 = mybir.dt.int32
U32 = mybir.dt.uint32
AF = mybir.ActivationFunctionType
ALU = mybir.AluOpType
AX = mybir.AxisListType

BB, TT_, DD = 4, 2048, 2048
NST, RK = 16, 64
EE, IH, JH, CAP = 8, 1408, 5632, 2560
NCORE, TOK, DSL, NDT = 8, 1024, 256, 2
TC, NCH = 512, 4
CLOC = 320
NTOKT = TOK // 128
NJT = JH // 128
NIT = IH // 128
RG = [list(range(NCORE))]
EXP23 = 8388608.0
EXPBITS = 0x4B000000
MANT = 0x7FFFFF
DUMP = EE * CLOC


class Sems:
    def __init__(self):
        self.c, self.h = {}, {}

    def make(self, ctx, nc, *names):
        for n in names:
            self.h[n] = ctx.enter_context(nc.semaphore(n))
            self.c[n] = 0

    def inc(self, instr, name, n=1):
        instr.then_inc(self.h[name], n)
        self.c[name] += n
        return instr

    def cnt(self, name):
        return self.c[name]

    def wait(self, eng, name, val=None):
        eng.wait_ge(self.h[name], self.c[name] if val is None else val)


def build(stages=("mamba", "ffn"), taps=()):
    nc = bass.Bass("TRN2", target_bir_lowering=False,
                   detect_race_conditions=False)
    do_ffn = "ffn" in stages

    def ein(name, shape, dt=F32):
        return nc.dram_tensor(name, shape, dt, kind="ExternalInput").ap()

    i_x_dsl = ein("x_dsl", [BB, TT_, DSL])
    i_wx_sl = ein("wx_sl", [96, DSL])
    i_wdt_sl = ein("wdt_sl", [DSL, RK])
    i_bdt_sl = ein("bdt_sl", [DSL, 1])
    i_alog_sl = ein("alog_sl", [DSL, NST])
    i_d_sl = ein("d_sl", [DSL, 1])
    i_rout_sl = ein("rout_sl", [DSL, EE])
    i_rout = ein("rout", [DD, EE])
    i_wug = ein("wug", [EE, 2 * NIT, 128, 16 * 128])
    i_wdn = ein("wdn", [EE, IH, DD])
    i_swug = ein("swug", [2 * NJT, 128, 16 * 128])
    i_swdn = ein("swdn", [16, 128, NJT * 128])
    i_ident = ein("identf", [128, 128])
    i_identb = ein("identb", [128, 128], BF16)
    i_ones = ein("onesf", [128, 1])
    i_xyrl = ein("xyrl", [128, 16], I32)
    i_strl_a = ein("strl_a", [8, 1], I32)
    i_strl_b = ein("strl_b", [2, 1], I32)
    i_strl_c = ein("strl_c", [2, 1], I32)
    i_roh = ein("roh", [1, 8])
    i_iota8c = ein("iota8c", [8, 1])
    i_onesr = ein("onesr", [1, 128])
    i_sel16 = ein("sel16", [16, 2048])

    o_out = nc.dram_tensor("out_tok", [TOK, DD], F32, kind="ExternalOutput").ap()
    tap_aps = {}
    if "xy" in taps:
        tap_aps["t_xy"] = nc.dram_tensor("t_xy", [DSL, BB * TT_], F32,
                                         kind="ExternalOutput").ap()
    if "y1" in taps:
        tap_aps["t_y1"] = nc.dram_tensor("t_y1", [DD, TOK], BF16,
                                         kind="ExternalOutput").ap()
    if "route" in taps:
        tap_aps["t_route"] = nc.dram_tensor("t_route", [8, 2 * TOK], F32,
                                            kind="ExternalOutput").ap()
    if "st" in taps:
        tap_aps["t_st"] = nc.dram_tensor("t_st", [80, TOK], F32,
                                         kind="ExternalOutput").ap()
        tap_aps["t_raw8"] = nc.dram_tensor("t_raw8", [8, TOK], F32,
                                           kind="ExternalOutput").ap()
        tap_aps["t_st2"] = nc.dram_tensor("t_st2", [2, TOK], F32,
                                          kind="ExternalOutput").ap()
        tap_aps["t_xyg"] = nc.dram_tensor("t_xyg", [128, TOK], BF16,
                                          kind="ExternalOutput").ap()
        tap_aps["t_mr"] = nc.dram_tensor("t_mr", [2, TOK], F32,
                                         kind="ExternalOutput").ap()
        tap_aps["t_mb"] = nc.dram_tensor("t_mb", [128, TOK], F32,
                                         kind="ExternalOutput").ap()
        tap_aps["t_rb"] = nc.dram_tensor("t_rb", [128, TOK], F32,
                                         kind="ExternalOutput").ap()
    if "ag" in taps:
        tap_aps["t_ag"] = nc.dram_tensor("t_ag", [DD, 1024], BF16,
                                         kind="ExternalOutput").ap()
    if "dbg" in taps:
        tap_aps["t_xpr"] = nc.dram_tensor("t_xpr", [96, BB * TT_], F32,
                                          kind="ExternalOutput").ap()
        tap_aps["t_delta"] = nc.dram_tensor("t_delta", [128, TC], F32,
                                            kind="ExternalOutput").ap()
        tap_aps["t_u"] = nc.dram_tensor("t_u", [128, TC], F32,
                                        kind="ExternalOutput").ap()
        tap_aps["t_bx"] = nc.dram_tensor("t_bx", [128, 8 * TC], F32,
                                         kind="ExternalOutput").ap()
        tap_aps["t_hh"] = nc.dram_tensor("t_hh", [128, 8 * TC], F32,
                                         kind="ExternalOutput").ap()

    d_xp_in = nc.dram_tensor("xp_in", [96, BB * TT_], F32)
    d_xpr = nc.dram_tensor("xpr_out", [96, BB * TT_], F32)
    d_st_in = nc.dram_tensor("st_in", [10, BB * TT_], F32)
    d_st = nc.dram_tensor("st_out", [10 * NCORE, TOK], F32)
    d_xy_in = nc.dram_tensor("xy_in", [DSL, BB * TT_], BF16)
    d_xy = nc.dram_tensor("xy_out", [DD * NCORE, TOK], BF16)
    d_cnt_in = nc.dram_tensor("cnt_in", [1, 8], F32)
    d_cnt = nc.dram_tensor("cnt_out", [8, 8], F32)
    d_sel = nc.dram_tensor("sel_d", [TOK, 2], F32)
    d_rows = nc.dram_tensor("rows_d", [8, 2048], F32)
    d_y1t = nc.dram_tensor("y1t_d", [TOK, DD], BF16)
    d_gath = nc.dram_tensor("gath_d", [DUMP + 128, DD], BF16)
    d_outb = nc.dram_tensor("outb_d", [DUMP + 128, DD], BF16)
    d_ysh = nc.dram_tensor("ysh_d", [DD, TOK], BF16)

    with ExitStack() as ctx:
        S = Sems()
        S.make(ctx, nc, "ls", "lg", "la", "cc", "ct", "cs", "cv", "st")

        def sb(c, name, shape, dt=F32):
            return c.enter_context(nc.sbuf_tensor(name, shape, dt))

        s_ident = sb(ctx, "s_ident", [128, 128])
        s_identb = sb(ctx, "s_identb", [128, 128], BF16)
        s_ones = sb(ctx, "s_ones", [128, 1])
        s_io8c = sb(ctx, "s_io8c", [8, 1])
        s_onesr = sb(ctx, "s_onesr", [1, 128])


        ps1 = ctx.enter_context(nc.psum_tensor("ps1", [128, 512], F32))
        ps2 = ctx.enter_context(nc.psum_tensor("ps2", [128, 512], F32))
        ps3 = ctx.enter_context(nc.psum_tensor("ps3", [128, 512], F32))
        ps4 = ctx.enter_context(nc.psum_tensor("ps4", [128, 512], F32))
        pse = ctx.enter_context(nc.psum_tensor("pse", [128, 1024], F32))
        pst = ctx.enter_context(nc.psum_tensor("pst", [128, 1024], BF16))
        psf = ctx.enter_context(nc.psum_tensor("psf", [128, 512], F32))

        block = ctx.enter_context(nc.Block())

        keep = dict(nc=nc, block=block, S=S, sb=sb,
                    s_ident=s_ident, s_identb=s_identb, s_ones=s_ones,
                    s_onesr=s_onesr,
                    s_io8c=s_io8c, ps1=ps1, ps2=ps2, ps3=ps3, ps4=ps4,
                    pse=pse, pst=pst, psf=psf,
                    d_xy=d_xy, d_st=d_st, d_cnt_in=d_cnt_in, d_cnt=d_cnt,
                    d_sel=d_sel, d_rows=d_rows, d_y1t=d_y1t, d_gath=d_gath,
                    d_outb=d_outb, d_ysh=d_ysh, i_xyrl=i_xyrl,
                    i_strl_a=i_strl_a, i_strl_b=i_strl_b, i_strl_c=i_strl_c,
                    i_rout=i_rout, i_swug=i_swug, i_swdn=i_swdn, i_wug=i_wug,
                    i_wdn=i_wdn, i_roh=i_roh, o_out=o_out, tap_aps=tap_aps,
                    taps=taps, stages=stages)

        with ExitStack() as mctx:
            s_wxt = [sb(mctx, f"s_wxt{i}", [128, 96], BF16) for i in range(NDT)]
            s_wdtt = [sb(mctx, f"s_wdtt{i}", [128, 128], BF16) for i in range(NDT)]
            s_bdt = sb(mctx, "s_bdt", [128, NDT])
            s_negA = sb(mctx, "s_negA", [128, NST * NDT])
            s_onepD = sb(mctx, "s_onepD", [128, NDT])
            s_routsl = sb(mctx, "s_routsl", [128, EE * NDT])
            s_carry = sb(mctx, "s_carry", [128, NDT * BB * 2 * 8])
            s_wxf = sb(mctx, "s_wxf", [96, 256])
            s_wxb = sb(mctx, "s_wxb", [96, 256], BF16)
            s_wdtf = sb(mctx, "s_wdtf", [128, 128])
            s_wdtb = sb(mctx, "s_wdtb", [128, 128], BF16)
            s_xraw = [sb(mctx, f"s_xraw{i}", [128, 1024]) for i in range(2)]
            s_xdm = [sb(mctx, f"s_xdm{i}", [128, TC]) for i in range(NDT)]
            s_xbf = [sb(mctx, f"s_xbf{i}", [128, TC], BF16) for i in range(NDT)]
            s_dpf = sb(mctx, "s_dpf", [128, TC])
            s_dpb = sb(mctx, "s_dpb", [128, TC], BF16)
            s_delta = [sb(mctx, f"s_delta{i}", [128, TC]) for i in range(NDT)]
            s_expz = sb(mctx, "s_expz", [128, TC])
            s_u = [sb(mctx, f"s_u{i}", [128, TC]) for i in range(NDT)]
            s_xpre = [sb(mctx, f"s_xpre{i}", [128, TC]) for i in range(NDT)]
            s_sq = sb(mctx, "s_sq", [128, TC])
            s_Bb = [sb(mctx, f"s_Bb{i}", [128, 8 * TC]) for i in range(2)]
            s_Cb = [sb(mctx, f"s_Cb{i}", [128, 8 * TC]) for i in range(2)]
            s_ab = [sb(mctx, f"s_ab{i}", [128, 8 * TC]) for i in range(2)]
            s_bx1 = sb(mctx, "s_bx1", [128, 8 * TC])
            s_hh1 = sb(mctx, "s_hh1", [128, 8 * TC])
            s_corr = sb(mctx, "s_corr", [128, 8])
            s_xy = [sb(mctx, f"s_xy{i}", [128, TC]) for i in range(NDT)]
            s_stev = sb(mctx, "s_stev", [128, TC])
            s_bcr = sb(mctx, "s_bcr", [16, TC], BF16)
            s_sel16b = sb(mctx, "s_sel16b", [16, 2048], BF16)
            s_sel16 = sb(mctx, "s_sel16", [16, 2048])


            def _setup_sync(sync):
                S.inc(sync.dma_start(out=s_ident[:, :], in_=i_ident), "ls", 16)
                S.inc(sync.dma_start(out=s_identb[:, :], in_=i_identb), "ls", 16)
                S.inc(sync.dma_start(out=s_ones[:, :], in_=i_ones), "ls", 16)
                S.inc(sync.dma_start(out=s_io8c[:, :], in_=i_iota8c), "ls", 16)
                S.inc(sync.dma_start(out=s_onesr[:, :], in_=i_onesr), "ls", 16)
                S.inc(sync.dma_start(out=s_sel16[:, :], in_=i_sel16), "ls", 16)
                S.inc(sync.dma_start(out=s_wxf[:, :], in_=i_wx_sl), "ls", 16)
                for i in range(NDT):
                    S.inc(sync.dma_start(out=s_wdtf[:, 64 * i:64 * (i + 1)],
                                         in_=i_wdt_sl[128 * i:128 * (i + 1), :]),
                          "ls", 16)
                    S.inc(sync.dma_start(out=s_bdt[:, i:i + 1],
                                         in_=i_bdt_sl[128 * i:128 * (i + 1), :]),
                          "ls", 16)
                    S.inc(sync.dma_start(out=s_onepD[:, i:i + 1],
                                         in_=i_d_sl[128 * i:128 * (i + 1), :]),
                          "ls", 16)
                    S.inc(sync.dma_start(out=s_negA[:, NST * i:NST * (i + 1)],
                                         in_=i_alog_sl[128 * i:128 * (i + 1), :]),
                          "ls", 16)
                    S.inc(sync.dma_start(out=s_routsl[:, EE * i:EE * (i + 1)],
                                         in_=i_rout_sl[128 * i:128 * (i + 1), :]),
                          "ls", 16)
            block.sync(_setup_sync)
            ls0 = S.cnt("ls")

            def _setup_vec(vector):
                S.wait(vector, "ls", ls0)
                S.inc(vector.tensor_copy(s_wxb[:, :], s_wxf[:, :]), "cv")
                S.inc(vector.tensor_copy(s_wdtb[:, :], s_wdtf[:, :]), "cv")
                S.inc(vector.tensor_copy(s_sel16b[:, :], s_sel16[:, :]), "cv")
                S.inc(vector.memset(s_carry[:, :], 0.0), "cv")
                S.inc(vector.tensor_scalar(s_onepD[:, :], s_onepD[:, :], 1.0,
                                           None, ALU.add), "cv")
            block.vector(_setup_vec)
            cv0 = S.cnt("cv")

            def _setup_sca(scalar):
                S.wait(scalar, "ls", ls0)
                S.inc(scalar.activation(s_negA[:, :], s_negA[:, :], AF.Exp), "cs")
                S.wait(scalar, "cs", S.cnt("cs"))
                S.inc(scalar.mul(s_negA[:, :], s_negA[:, :], -1.0), "cs")
            block.scalar(_setup_sca)

            for i in range(NDT):
                def _t(tensor, i=i):
                    S.wait(tensor, "cv", cv0)
                    S.wait(tensor, "cs", S.cnt("cs"))
                    S.inc(tensor.transpose(pst[:, 0:96],
                                           s_wxb[0:96, 128 * i:128 * (i + 1)],
                                           s_identb[0:96, 0:96]), "ct")
                block.tensor(_t)

                def _s(scalar, i=i):
                    S.wait(scalar, "ct", S.cnt("ct"))
                    S.inc(scalar.copy(s_wxt[i][:, 0:96], pst[:, 0:96]), "cs")
                block.scalar(_s)
            for i in range(NDT):
                def _t(tensor, i=i):
                    S.wait(tensor, "cs", S.cnt("cs"))
                    S.inc(tensor.transpose(pst[0:64, 0:128],
                                           s_wdtb[:, 64 * i:64 * (i + 1)],
                                           s_identb[:, :]), "ct")
                block.tensor(_t)

                def _s(scalar, i=i):
                    S.wait(scalar, "ct", S.cnt("ct"))
                    S.inc(scalar.copy(s_wdtt[i][0:64, 0:128], pst[0:64, 0:128]),
                          "cs")
                block.scalar(_s)

            lastcs_xraw = {0: 0, 1: 0}
            lastct_bcr = {0: 0}
            lastcv_ab = {0: 0, 1: 0}
            lastcv_bb = {0: 0, 1: 0}
            lg_after_xy = [0]

            def load_x_chunk(b, t0, par):
                def _f(sync):
                    S.wait(sync, "cs", lastcs_xraw[par])
                    for g in range(4):
                        S.inc(sync.dma_start(
                            out=s_xraw[par][:, 256 * g:256 * (g + 1)],
                            in_=i_x_dsl[b, t0 + 128 * g:t0 + 128 * (g + 1), :]),
                            "ls", 16)
                block.sync(_f)
                return S.cnt("ls")

            def transpose_x_chunk(ls_x, par):
                def _t(tensor):
                    S.wait(tensor, "ls", ls_x)
                    S.wait(tensor, "cs", S.cnt("cs"))
                    for i in range(NDT):
                        ps = ps1 if i == 0 else ps2
                        for g in range(4):
                            S.inc(tensor.transpose(
                                ps[:, 128 * g:128 * (g + 1)],
                                s_xraw[par][:, 256 * g + 128 * i:256 * g + 128 * (i + 1)],
                                s_ident[:, :]), "ct")
                block.tensor(_t)
                ct_tp = S.cnt("ct")

                def _s(scalar):
                    S.wait(scalar, "ct", ct_tp)
                    for i in range(NDT):
                        ps = ps1 if i == 0 else ps2
                        S.inc(scalar.copy(s_xdm[i][:, :], ps[:, :]), "cs")
                block.scalar(_s)
                lastcs_xraw[par] = S.cnt("cs")
                return S.cnt("cs")

            for b in range(BB):
                for c in range(NCH):
                    t0 = c * TC
                    col0 = b * TT_ + t0
                    par = (b * NCH + c) % 2
                    ls_x = load_x_chunk(b, t0, par)
                    cs_xdm = transpose_x_chunk(ls_x, par)

                    def _v(vector):
                        S.wait(vector, "cs", cs_xdm)
                        S.wait(vector, "ct", S.cnt("ct"))
                        for i in range(NDT):
                            S.inc(vector.tensor_copy(s_xbf[i][:, :],
                                                     s_xdm[i][:, :]), "cv")
                    block.vector(_v)
                    cv_xbf = S.cnt("cv")

                    def _t(tensor):
                        S.wait(tensor, "cv", cv_xbf)
                        S.wait(tensor, "cs", S.cnt("cs"))
                        for i in range(NDT):
                            S.inc(tensor.matmul(ps3[0:96, :], s_wxt[i][:, 0:96],
                                                s_xbf[i][:, :], start=(i == 0),
                                                stop=(i == NDT - 1)), "ct")
                    block.tensor(_t)
                    ct_xp = S.cnt("ct")

                    def _s(scalar):
                        S.wait(scalar, "ct", ct_xp)
                        S.wait(scalar, "la", S.cnt("la"))
                        S.inc(scalar.copy(s_stev[0:96, :], ps3[0:96, :]), "cs")
                        S.wait(scalar, "cs", S.cnt("cs"))
                        S.inc(scalar.dma_start(out=d_xp_in[0:96, col0:col0 + TC],
                                               in_=s_stev[0:96, :]), "la", 16)
                    block.scalar(_s)

            def _cc(gpsimd):
                S.wait(gpsimd, "la", S.cnt("la"))
                S.inc(gpsimd.collective_compute(
                    "AllReduce", ALU.add, replica_groups=RG,
                    ins=[d_xp_in.ap().opt()], outs=[d_xpr.ap().opt()]), "cc", 1)
            block.gpsimd(_cc)
            cc_xp = S.cnt("cc")

            for b in range(BB):
                for c in range(NCH):
                    t0 = c * TC
                    col0 = b * TT_ + t0
                    par = (b * NCH + c) % 2
                    ls_x = load_x_chunk(b, t0, par)
                    cs_xdm = transpose_x_chunk(ls_x, par)

                    def _s(scalar):
                        S.wait(scalar, "cv", S.cnt("cv"))
                        for i in range(NDT):
                            S.inc(scalar.activation(s_xpre[i][:, :], s_xdm[i][:, :],
                                                    AF.Copy,
                                                    scale=s_onepD[:, i:i + 1]),
                                  "cs")
                    block.scalar(_s)
                    cs_xpre = S.cnt("cs")

                    def _g(gpsimd, b=b, c=c):
                        S.wait(gpsimd, "cc", cc_xp)
                        S.wait(gpsimd, "cv", S.cnt("cv"))
                        if "dbg" in taps and b == 0 and c == 0:
                            S.inc(gpsimd.dma_start(out=tap_aps["t_xpr"],
                                                   in_=d_xpr[:, :]), "lg", 16)
                        S.inc(gpsimd.dma_start(out=s_dpf[0:64, :],
                                               in_=d_xpr[0:64, col0:col0 + TC]),
                              "lg", 16)
                    block.gpsimd(_g)
                    lg_dp = S.cnt("lg")

                    def _v(vector):
                        S.wait(vector, "lg", lg_dp)
                        S.inc(vector.tensor_copy(s_dpb[0:64, :], s_dpf[0:64, :]),
                              "cv")
                    block.vector(_v)
                    cv_dpb = S.cnt("cv")

                    def _t(tensor):
                        S.wait(tensor, "cv", cv_dpb)
                        S.wait(tensor, "cs", S.cnt("cs"))
                        for i in range(NDT):
                            ps = ps3 if i == 0 else ps4
                            S.inc(tensor.matmul(ps[:, :], s_wdtt[i][0:64, :],
                                                s_dpb[0:64, :], start=True,
                                                stop=True), "ct")
                    block.tensor(_t)
                    ct_delta = S.cnt("ct")

                    def _s(scalar):
                        S.wait(scalar, "ct", ct_delta)
                        S.wait(scalar, "cv", S.cnt("cv"))
                        for i in range(NDT):
                            ps = ps3 if i == 0 else ps4
                            S.inc(scalar.activation(s_expz[:, :], ps[:, :], AF.Exp,
                                                    bias=s_bdt[:, i:i + 1]), "cs")
                            S.wait(scalar, "cs", S.cnt("cs"))
                            S.inc(scalar.activation(s_delta[i][:, :], s_expz[:, :],
                                                    AF.Ln, bias=1.0), "cs")
                    block.scalar(_s)
                    cs_delta = S.cnt("cs")

                    def _v(vector):
                        S.wait(vector, "cs", max(cs_delta, cs_xdm))
                        for i in range(NDT):
                            S.inc(vector.tensor_mul(s_u[i][:, :], s_delta[i][:, :],
                                                    s_xdm[i][:, :]), "cv")
                    block.vector(_v)
                    cv_u = S.cnt("cv")
                    if "dbg" in taps and b == 0 and c == 0:
                        def _gd(gpsimd):
                            S.wait(gpsimd, "cv", cv_u)
                            S.inc(gpsimd.dma_start(out=tap_aps["t_delta"],
                                                   in_=s_delta[0][:, :]), "lg", 16)
                            S.inc(gpsimd.dma_start(out=tap_aps["t_u"],
                                                   in_=s_u[0][:, :]), "lg", 16)
                        block.gpsimd(_gd)

                    for h in range(2):
                        hpar = (b * NCH + c) * 2 + h
                        def _g(gpsimd, h=h, hpar=hpar):
                            S.wait(gpsimd, "cs", S.cnt("cs"))
                            S.inc(gpsimd.dma_start(
                                out=s_bcr[0:8, :],
                                in_=d_xpr[64 + 8 * h:72 + 8 * h,
                                          col0:col0 + TC]), "lg", 16)
                            S.inc(gpsimd.dma_start(
                                out=s_bcr[8:16, :],
                                in_=d_xpr[80 + 8 * h:88 + 8 * h,
                                          col0:col0 + TC]), "lg", 16)
                        block.gpsimd(_g)
                        lg_bcr = S.cnt("lg")

                        cs_bc0 = S.cnt("cs")

                        def _tb(tensor, lg_bcr=lg_bcr, cs_bc0=cs_bc0,
                                hpar=hpar):
                            S.wait(tensor, "lg", lg_bcr)
                            S.wait(tensor, "cv", S.cnt("cv"))
                            S.wait(tensor, "cs", cs_bc0)
                            for w in range(16):
                                if w >= 2:
                                    S.wait(tensor, "cs", cs_bc0 + w - 1)
                                ps = (ps1, ps2)[w % 2]
                                S.inc(tensor.matmul(
                                    ps[:, :], s_sel16b[:, 128 * w:128 * (w + 1)],
                                    s_bcr[:, :], start=True, stop=True),
                                    "ct")
                        block.tensor(_tb)
                        ct_bc0 = S.cnt("ct") - 16
                        lastct_bcr[0] = S.cnt("ct")

                        def _sb2(scalar, ct_bc0=ct_bc0, hpar=hpar):
                            S.wait(scalar, "cv", lastcv_bb[hpar % 2])
                            for w in range(16):
                                S.wait(scalar, "ct", ct_bc0 + w + 1)
                                dst = s_Bb[hpar % 2] if w < 8 else s_Cb[hpar % 2]
                                n = w % 8
                                ps = (ps1, ps2)[w % 2]
                                S.inc(scalar.copy(
                                    dst[:, n * TC:(n + 1) * TC], ps[:, :]),
                                    "cs")
                        block.scalar(_sb2)
                        lg_bc = S.cnt("cs")

                        for i in range(NDT):
                            un = 2 * h + i
                            ab = s_ab[un % 2]
                            bx = s_bx1
                            hh = s_hh1
                            ccol = ((i * BB + b) * 2 + h) * 8

                            def _s(scalar, i=i, h=h, ab=ab, un=un):
                                S.wait(scalar, "cs", cs_delta)
                                S.wait(scalar, "cv", lastcv_ab[un % 2])
                                for n in range(8):
                                    S.inc(scalar.activation(
                                        ab[:, n * TC:(n + 1) * TC],
                                        s_delta[i][:, :], AF.Exp,
                                        scale=s_negA[:, NST * i + 8 * h + n:
                                                     NST * i + 8 * h + n + 1]),
                                        "cs")
                            block.scalar(_s)
                            cs_ab = S.cnt("cs")

                            def _v(vector, i=i, h=h, ab=ab, bx=bx, hh=hh,
                                   ccol=ccol, cs_ab=cs_ab, lg_bc=lg_bc, b=b,
                                   c=c, hpar=hpar):
                                S.wait(vector, "cs", max(cs_ab, lg_bc))
                                if h == 0:
                                    S.wait(vector, "lg", lg_after_xy[0])
                                for n in range(8):
                                    S.inc(vector.tensor_mul(
                                        bx[:, n * TC:(n + 1) * TC], s_u[i][:, :],
                                        s_Bb[hpar % 2][:, n * TC:(n + 1) * TC]),
                                        "cv")
                                abst = ab[:, :].rearrange("p (n t) -> p n t",
                                                          t=TC)[:, :, 0]
                                bxst = bx[:, :].rearrange("p (n t) -> p n t",
                                                          t=TC)[:, :, 0]
                                S.inc(vector.tensor_mul(
                                    s_corr[:, :], abst,
                                    s_carry[:, ccol:ccol + 8]), "cv")
                                S.wait(vector, "cv", S.cnt("cv"))
                                S.inc(vector.tensor_add(bxst, bxst,
                                                        s_corr[:, :]), "cv")
                                S.wait(vector, "cv", S.cnt("cv"))
                                S.inc(vector.memset(abst, 0.0), "cv")
                                S.wait(vector, "cv", S.cnt("cv"))
                                S.inc(vector.tensor_tensor_scan(
                                    hh[:, :], ab[:, :], bx[:, :], 0.0,
                                    ALU.mult, ALU.add), "cv")
                                hhe = hh[:, :].rearrange("p (n t) -> p n t",
                                                         t=TC)[:, :, TC - 1]
                                S.wait(vector, "cv", S.cnt("cv"))
                                S.inc(vector.tensor_copy(
                                    s_carry[:, ccol:ccol + 8], hhe), "cv")
                                S.wait(vector, "cv", S.cnt("cv"))
                                S.inc(vector.tensor_mul(hh[:, :], hh[:, :],
                                                        s_Cb[hpar % 2][:, :]),
                                      "cv")
                                S.inc(vector.tensor_add(
                                    hh[:, 0:4 * TC], hh[:, 0:4 * TC],
                                    hh[:, 4 * TC:8 * TC]), "cv")
                                S.inc(vector.tensor_add(
                                    hh[:, 0:2 * TC], hh[:, 0:2 * TC],
                                    hh[:, 2 * TC:4 * TC]), "cv")
                                S.inc(vector.tensor_add(
                                    hh[:, 0:TC], hh[:, 0:TC],
                                    hh[:, TC:2 * TC]), "cv")
                                S.wait(vector, "cs", cs_xpre)
                                if h == 0:
                                    S.inc(vector.tensor_add(
                                        s_xy[i][:, :], s_xpre[i][:, :],
                                        hh[:, 0:TC]), "cv")
                                else:
                                    S.inc(vector.tensor_add(
                                        s_xy[i][:, :], s_xy[i][:, :],
                                        hh[:, 0:TC]), "cv")
                            block.vector(_v)
                            if ("dbg" in taps and b == 0 and c == 0
                                    and h == 0 and i == 0):
                                cv_u0 = S.cnt("cv")

                                def _gd2(gpsimd):
                                    S.wait(gpsimd, "cv", cv_u0)
                                    S.inc(gpsimd.dma_start(
                                        out=tap_aps["t_bx"],
                                        in_=bx[:, :]), "lg", 16)
                                    S.inc(gpsimd.dma_start(
                                        out=tap_aps["t_hh"],
                                        in_=hh[:, :]), "lg", 16)
                                block.gpsimd(_gd2)
                                lg_d0 = S.cnt("lg")

                                def _vw(vector):
                                    S.wait(vector, "lg", lg_d0)
                                block.vector(_vw)

                                def _sw(scalar):
                                    S.wait(scalar, "lg", lg_d0)
                                block.scalar(_sw)
                    cv_xy = S.cnt("cv")

                    def _g(gpsimd, b=b, c=c):
                        S.wait(gpsimd, "cv", cv_xy)
                        for i in range(NDT):
                            S.inc(gpsimd.dma_start(
                                out=d_xy_in[128 * i:128 * (i + 1),
                                            col0:col0 + TC],
                                in_=s_xy[i][:, :]), "lg", 16)
                            if "xy" in taps:
                                S.inc(gpsimd.dma_start(
                                    out=tap_aps["t_xy"][128 * i:128 * (i + 1),
                                                        col0:col0 + TC],
                                    in_=s_xy[i][:, :]), "lg", 16)
                    block.gpsimd(_g)
                    lg_after_xy[0] = S.cnt("lg")

                    def _s(scalar):
                        S.wait(scalar, "cv", cv_xy)
                        S.inc(scalar.activation(s_sq[:, :], s_xy[0][:, :],
                                                AF.Square), "cs")
                    block.scalar(_s)
                    cs_q0 = S.cnt("cs")

                    def _t(tensor):
                        S.wait(tensor, "cv", cv_xy)
                        S.wait(tensor, "la", S.cnt("la"))
                        for i in range(NDT):
                            S.inc(tensor.matmul(psf[0:8, :],
                                                s_routsl[:, EE * i:EE * (i + 1)],
                                                s_xy[i][:, :], start=(i == 0),
                                                stop=(i == 1)), "ct")
                        for i in range(NDT):
                            S.inc(tensor.matmul(psf[32:33, :], s_ones[:, 0:1],
                                                s_xy[i][:, :], start=(i == 0),
                                                stop=(i == 1)), "ct")
                        S.wait(tensor, "cs", cs_q0)
                        S.inc(tensor.matmul(psf[64:65, :], s_ones[:, 0:1],
                                            s_sq[:, :], start=True, stop=False),
                              "ct")
                    block.tensor(_t)
                    ct_q0 = S.cnt("ct")

                    def _s(scalar):
                        S.wait(scalar, "ct", ct_q0)
                        S.inc(scalar.activation(s_sq[:, :], s_xy[1][:, :],
                                                AF.Square), "cs")
                    block.scalar(_s)
                    cs_q1 = S.cnt("cs")

                    def _t(tensor):
                        S.wait(tensor, "cs", cs_q1)
                        S.inc(tensor.matmul(psf[64:65, :], s_ones[:, 0:1],
                                            s_sq[:, :], start=False, stop=True),
                              "ct")
                    block.tensor(_t)
                    ct_stats = S.cnt("ct")

                    def _s(scalar):
                        S.wait(scalar, "ct", ct_stats)
                        S.wait(scalar, "la", S.cnt("la"))
                        S.inc(scalar.copy(s_stev[0:8, :], psf[0:8, :]), "cs")
                        S.inc(scalar.copy(s_stev[32:33, :], psf[32:33, :]), "cs")
                        S.inc(scalar.copy(s_stev[64:65, :], psf[64:65, :]), "cs")
                        S.wait(scalar, "cs", S.cnt("cs"))
                        S.inc(scalar.dma_start(out=d_st_in[0:8, col0:col0 + TC],
                                               in_=s_stev[0:8, :]), "la", 16)
                        S.inc(scalar.dma_start(out=d_st_in[8:9, col0:col0 + TC],
                                               in_=s_stev[32:33, :]), "la", 16)
                        S.inc(scalar.dma_start(out=d_st_in[9:10, col0:col0 + TC],
                                               in_=s_stev[64:65, :]), "la", 16)
                    block.scalar(_s)

            def _cc2(gpsimd):
                S.wait(gpsimd, "lg", S.cnt("lg"))
                S.wait(gpsimd, "la", S.cnt("la"))
                S.inc(gpsimd.collective_compute(
                    "AllGather", ALU.bypass, replica_groups=RG,
                    ins=[d_xy_in.ap().opt()], outs=[d_xy.ap().opt()]), "cc", 1)
                S.inc(gpsimd.collective_compute(
                    "AllReduce", ALU.add, replica_groups=RG,
                    ins=[d_st_in.ap().opt()], outs=[d_st.ap().opt()]), "cc", 1)
            block.gpsimd(_cc2)

            bar = {k: S.cnt(k) for k in ("ls", "lg", "la", "cc", "ct", "cs", "cv")}
            for sec in (block.sync, block.vector, block.scalar, block.tensor,
                        block.gpsimd):
                def _barr(eng):
                    for k, v in bar.items():
                        eng.wait_ge(S.h[k], v)
                sec(_barr)

        if not do_ffn:
            def _out(gpsimd):
                for tt in range(NTOKT):
                    S.inc(gpsimd.dma_start(
                        out=o_out[128 * tt:128 * (tt + 1), :],
                        in_=d_xy.ap().rearrange("(d r) t -> d (r t)",
                                                r=8)[0:128, 0:DD]), "st", 16)
                S.wait(gpsimd, "st", S.cnt("st"))
            block.gpsimd(_out)
            return nc, tap_aps

        _ffn(keep)
    return nc, tap_aps


def _ffn(E):
    nc, block, S, sb = E["nc"], E["block"], E["S"], E["sb"]
    s_ident, s_identb, s_ones, s_io8c, s_onesr = (E[k] for k in
        ("s_ident", "s_identb", "s_ones", "s_io8c", "s_onesr"))
    ps1, ps2, ps3, ps4, pse, pst, psf = (E[k] for k in
        ("ps1", "ps2", "ps3", "ps4", "pse", "pst", "psf"))
    d_xy, d_st, d_cnt_in, d_cnt, d_sel, d_rows = (E[k] for k in
        ("d_xy", "d_st", "d_cnt_in", "d_cnt", "d_sel", "d_rows"))
    d_y1t, d_gath, d_outb, d_ysh = (E[k] for k in
        ("d_y1t", "d_gath", "d_outb", "d_ysh"))
    i_xyrl, i_strl_a, i_strl_b, i_strl_c = (E[k] for k in
        ("i_xyrl", "i_strl_a", "i_strl_b", "i_strl_c"))
    i_rout, i_swug, i_swdn, i_wug, i_wdn, i_roh = (E[k] for k in
        ("i_rout", "i_swug", "i_swdn", "i_wug", "i_wdn", "i_roh"))
    o_out = E["o_out"]
    tap_aps, taps = E["tap_aps"], E["taps"]
    stages = E.get("stages", ())

    with ExitStack() as fctx:
        s_y1tst = sb(fctx, "s_y1tst", [128, DD], BF16)
        s_zero = sb(fctx, "s_zero", [128, 512], BF16)
        s_xyrl = sb(fctx, "s_xyrl", [128, 16], I32)
        s_sia = sb(fctx, "s_sia", [8, 1], I32)
        s_sib = sb(fctx, "s_sib", [2, 1], I32)
        s_sic = sb(fctx, "s_sic", [2, 1], I32)
        s_soffc = sb(fctx, "s_soffc", [128, 16], I32)
        s_goffc = sb(fctx, "s_goffc", [128, 16], I32)
        s_w1c = sb(fctx, "s_w1c", [128, 8])
        s_w2c = sb(fctx, "s_w2c", [128, 8])

        def _idx(sync):
            S.inc(sync.dma_start(out=s_xyrl[:, :], in_=i_xyrl), "ls", 16)
            S.inc(sync.dma_start(out=s_sia[:, :], in_=i_strl_a), "ls", 16)
            S.inc(sync.dma_start(out=s_sib[:, :], in_=i_strl_b), "ls", 16)
            S.inc(sync.dma_start(out=s_sic[:, :], in_=i_strl_c), "ls", 16)
        block.sync(_idx)
        ls_idx = S.cnt("ls")

        def _z(vector):
            S.inc(vector.memset(s_zero[:, :], 0.0), "cv")
        block.vector(_z)
        cv_zero = S.cnt("cv")

        def _zg(gpsimd):
            S.wait(gpsimd, "cv", cv_zero)
            nrow = DUMP + 128
            for r0 in range(0, nrow, 128):
                nr = min(128, nrow - r0)
                for c0 in range(0, DD, 512):
                    S.inc(gpsimd.dma_start(
                        out=d_gath[r0:r0 + nr, c0:c0 + 512],
                        in_=s_zero[0:nr, :]), "lg", 16)
                    S.inc(gpsimd.dma_start(
                        out=d_outb[r0:r0 + nr, c0:c0 + 512],
                        in_=s_zero[0:nr, :]), "lg", 16)
        block.gpsimd(_zg)

        with ExitStack() as actx:
            s_y1 = [sb(fctx, f"s_y1_{j}", [128, TOK], BF16) for j in range(16)]

            def _gxy(gpsimd):
                S.wait(gpsimd, "ls", ls_idx)
                if "ag" in taps:
                    S.inc(gpsimd.dma_start(
                        out=tap_aps["t_ag"],
                        in_=d_xy.ap().rearrange("(d r) t -> d (r t)",
                                                r=8)[:, 0:1024]), "lg", 16)
                xyv = d_xy.ap()
                for j in range(16):
                    S.inc(gpsimd.indirect_dma_start(
                        s_y1[j][:, :], None, xyv,
                        bass.IndirectOffsetOnAxis(ap=s_xyrl[:, j:j + 1], axis=0)),
                        "lg", 16)
            block.gpsimd(_gxy)
            lg_xy = S.cnt("lg")

            with ExitStack() as bctx:
                s_t1 = sb(bctx, "s_t1", [128, TOK])
                s_meanb = sb(bctx, "s_meanb", [128, TOK])
                s_rstdb = sb(bctx, "s_rstdb", [128, TOK])
                s_raw8 = sb(bctx, "s_raw8", [8, TOK])
                s_st2 = sb(bctx, "s_st2", [2, TOK])
                s_ssq2 = sb(bctx, "s_ssq2", [2, TOK])
                s_mrow = sb(bctx, "s_mrow", [1, TOK])
                s_rrow = sb(bctx, "s_rrow", [1, TOK])
                s_rsr = sb(bctx, "s_rsr", [1, 8])
                s_cntr = sb(bctx, "s_cntr", [1, 8])
                s_rt = sb(bctx, "s_rt", [128, 128])
                s_rsumb = sb(bctx, "s_rsumb", [128, 8])
                s_nmc = sb(bctx, "s_nmc", [128, 8])
                s_rsc = sb(bctx, "s_rsc", [128, 8])
                s_lg = sb(bctx, "s_lg", [128, 8])
                s_srt = sb(bctx, "s_srt", [128, 8])
                s_idxs = sb(bctx, "s_idxs", [128, 8], U32)
                s_tmpu = sb(bctx, "s_tmpu", [128, 2], U32)
                s_ekl = sb(bctx, "s_ekl", [128, 2])
                s_c1 = sb(bctx, "s_c1", [128, 1])
                s_c2 = sb(bctx, "s_c2", [128, 1])
                s_c3 = sb(bctx, "s_c3", [128, 1])
                s_eb8 = sb(bctx, "s_eb8", [8, 2 * TOK])
                s_ind8 = sb(bctx, "s_ind8", [8, 2 * TOK])
                s_cum8 = sb(bctx, "s_cum8", [8, 2 * TOK])
                s_z8 = sb(bctx, "s_z8", [8, 2 * TOK])
                s_lpos = sb(bctx, "s_lpos", [1, 2 * TOK])
                s_gbrow = sb(bctx, "s_gbrow", [1, 2 * TOK])
                s_ml = sb(bctx, "s_ml", [1, 2 * TOK])
                s_mg = sb(bctx, "s_mg", [1, 2 * TOK])
                s_offr = sb(bctx, "s_offr", [1, 2 * TOK])
                s_dmpr = sb(bctx, "s_dmpr", [1, 2 * TOK])
                s_sofr = sb(bctx, "s_sofr", [1, 2 * TOK])
                s_gofr = sb(bctx, "s_gofr", [1, 2 * TOK])
                s_ef = sb(bctx, "s_ef", [1, 2 * TOK])
                s_mycnt = sb(bctx, "s_mycnt", [8, 1])
                s_c88 = sb(bctx, "s_c88", [8, 8])
                s_cT = sb(bctx, "s_cT", [8, 8])
                s_cumr = sb(bctx, "s_cumr", [8, 8])
                s_rohb = sb(bctx, "s_rohb", [8, 8])
                s_rohr = sb(bctx, "s_rohr", [1, 8])
                s_incl = sb(bctx, "s_incl", [8, 1])
                s_gb8 = sb(bctx, "s_gb8", [8, 1])
                s_sm = sb(bctx, "s_sm", [128, 16])

                def _gst(gpsimd):
                    S.wait(gpsimd, "ls", ls_idx)
                    stv = d_st.ap()
                    S.inc(gpsimd.indirect_dma_start(
                        s_raw8[:, :], None, stv,
                        bass.IndirectOffsetOnAxis(ap=s_sia[:, 0:1], axis=0)),
                        "lg", 16)
                    S.inc(gpsimd.indirect_dma_start(
                        s_st2[:, :], None, stv,
                        bass.IndirectOffsetOnAxis(ap=s_sib[:, 0:1], axis=0)),
                        "lg", 16)
                    S.inc(gpsimd.indirect_dma_start(
                        s_ssq2[:, :], None, stv,
                        bass.IndirectOffsetOnAxis(ap=s_sic[:, 0:1], axis=0)),
                        "lg", 16)
                block.gpsimd(_gst)
                lg_st = S.cnt("lg")

                if "st" in taps:
                    def _gtap(gpsimd):
                        S.wait(gpsimd, "lg", lg_st)
                        S.wait(gpsimd, "cc", S.cnt("cc"))
                        S.inc(gpsimd.dma_start(out=tap_aps["t_st"],
                                               in_=d_st[:, :]), "lg", 16)
                        S.inc(gpsimd.dma_start(out=tap_aps["t_raw8"],
                                               in_=s_raw8[:, :]), "lg", 16)
                        S.inc(gpsimd.dma_start(out=tap_aps["t_st2"],
                                               in_=s_st2[:, :]), "lg", 16)
                        S.inc(gpsimd.dma_start(out=tap_aps["t_xyg"],
                                               in_=s_y1[0][:, :]), "lg", 16)
                    block.gpsimd(_gtap)
                    lg_tap = S.cnt("lg")

                    def _vtap(vector):
                        S.wait(vector, "lg", lg_tap)
                    block.vector(_vtap)

                def _rt(sync):
                    S.inc(sync.dma_start(
                        out=s_rt[:, :].rearrange("p (j e) -> p j e", e=8),
                        in_=i_rout.rearrange("(j p) e -> p j e", p=128)),
                        "ls", 16)
                block.sync(_rt)
                ls_rt = S.cnt("ls")

                def _t(tensor):
                    S.wait(tensor, "ls", ls_rt)
                    for j in range(16):
                        S.inc(tensor.matmul(ps3[0:8, 0:1],
                                            s_rt[:, 8 * j:8 * (j + 1)],
                                            s_ones[:, 0:1], start=(j == 0),
                                            stop=(j == 15)), "ct")
                block.tensor(_t)
                ct_rs = S.cnt("ct")

                def _s(scalar):
                    S.wait(scalar, "ct", ct_rs)
                    S.inc(scalar.copy(s_gb8[:, :], ps3[0:8, 0:1]), "cs")
                block.scalar(_s)
                cs_rs = S.cnt("cs")

                def _t(tensor):
                    S.wait(tensor, "cs", cs_rs)
                    S.inc(tensor.transpose(ps4[0:1, 0:8], s_gb8[:, 0:1],
                                           s_ident[0:8, 0:8]), "ct")
                block.tensor(_t)
                ct_rs2 = S.cnt("ct")

                def _s(scalar):
                    S.wait(scalar, "ct", ct_rs2)
                    S.inc(scalar.copy(s_rsr[0:1, 0:8], ps4[0:1, 0:8]), "cs")
                block.scalar(_s)

                def _t2(tensor):
                    S.wait(tensor, "cs", S.cnt("cs"))
                    S.inc(tensor.matmul(ps1[:, 0:8], s_onesr[0:1, :],
                                        s_rsr[0:1, 0:8], start=True,
                                        stop=True), "ct")
                block.tensor(_t2)
                ct_rsb = S.cnt("ct")

                def _s2(scalar):
                    S.wait(scalar, "ct", ct_rsb)
                    S.inc(scalar.copy(s_rsumb[:, :], ps1[:, 0:8]), "cs")
                block.scalar(_s2)
                lg_rsb = S.cnt("cs")

                def _v(vector):
                    S.wait(vector, "lg", lg_st)
                    S.inc(vector.tensor_scalar(s_mrow[:, :], s_st2[0:1, :],
                                               1.0 / DD, None, ALU.mult), "cv")
                    S.wait(vector, "cv", S.cnt("cv"))
                    S.inc(vector.tensor_scalar(s_rrow[:, :], s_ssq2[0:1, :],
                                               1.0 / DD, None, ALU.mult), "cv")
                    S.wait(vector, "cv", S.cnt("cv"))
                    S.inc(vector.tensor_mul(s_t1[0:1, :], s_mrow[:, :],
                                            s_mrow[:, :]), "cv")
                    S.wait(vector, "cv", S.cnt("cv"))
                    S.inc(vector.tensor_sub(s_rrow[:, :], s_rrow[:, :],
                                            s_t1[0:1, :]), "cv")
                block.vector(_v)
                cv_var = S.cnt("cv")

                def _v2(vector):
                    S.wait(vector, "cv", cv_var)
                    S.inc(vector.tensor_scalar(s_rrow[:, :], s_rrow[:, :],
                                               1e-5, None, ALU.add), "cv")
                    S.wait(vector, "cv", S.cnt("cv"))
                    S.inc(vector.tensor_copy(s_t1[0:1, :], s_rrow[:, :]), "cv")
                block.vector(_v2)
                cv_eps = S.cnt("cv")

                def _s(scalar):
                    S.wait(scalar, "cv", cv_eps)
                    S.inc(scalar.activation(s_rrow[:, :], s_rrow[:, :],
                                            AF.Ln), "cs")
                    S.wait(scalar, "cs", S.cnt("cs"))
                    S.inc(scalar.mul(s_rrow[:, :], s_rrow[:, :], -0.5), "cs")
                    S.wait(scalar, "cs", S.cnt("cs"))
                    S.inc(scalar.activation(s_rrow[:, :], s_rrow[:, :],
                                            AF.Exp), "cs")
                block.scalar(_s)
                la_mr = S.cnt("cs")

                cs_mb0 = S.cnt("cs")

                def _t3(tensor):
                    S.wait(tensor, "cs", cs_mb0)
                    for w in range(4):
                        if w >= 2:
                            S.wait(tensor, "cs", cs_mb0 + w - 1)
                        src = s_mrow if w < 2 else s_rrow
                        q = w % 2
                        ps = ps1 if w % 2 == 0 else ps2
                        S.inc(tensor.matmul(
                            ps[:, :], s_onesr[0:1, :],
                            src[0:1, 512 * q:512 * (q + 1)], start=True,
                            stop=True), "ct")
                block.tensor(_t3)
                ct_mb0 = S.cnt("ct") - 4

                def _s3(scalar):
                    for w in range(4):
                        S.wait(scalar, "ct", ct_mb0 + w + 1)
                        dst = s_meanb if w < 2 else s_rstdb
                        q = w % 2
                        ps = ps1 if w % 2 == 0 else ps2
                        S.inc(scalar.copy(dst[:, 512 * q:512 * (q + 1)],
                                          ps[:, :]), "cs")
                block.scalar(_s3)
                lg_mrb = S.cnt("cs")

                if "st" in taps:
                    def _gtap2(gpsimd):
                        S.wait(gpsimd, "cs", lg_mrb)
                        S.inc(gpsimd.dma_start(out=tap_aps["t_mr"][0:1, :],
                                               in_=s_mrow[:, :]), "lg", 16)
                        S.inc(gpsimd.dma_start(out=tap_aps["t_mr"][1:2, :],
                                               in_=s_rrow[:, :]), "lg", 16)
                        S.inc(gpsimd.dma_start(out=tap_aps["t_mb"],
                                               in_=s_meanb[:, :]), "lg", 16)
                        S.inc(gpsimd.dma_start(out=tap_aps["t_rb"],
                                               in_=s_rstdb[:, :]), "lg", 16)
                    block.gpsimd(_gtap2)
                    lg_tap2 = S.cnt("lg")

                    def _vtap2(vector):
                        S.wait(vector, "lg", lg_tap2)
                    block.vector(_vtap2)

                    def _stap2(scalar):
                        S.wait(scalar, "lg", lg_tap2)
                    block.scalar(_stap2)

                def _t(tensor):
                    S.wait(tensor, "cs", S.cnt("cs"))
                    for tt in range(8):
                        S.inc(tensor.transpose(
                            ps3[:, tt:tt + 1],
                            s_mrow[0:1, 128 * tt:128 * (tt + 1)],
                            s_ident[0:1, 0:1]), "ct")
                    for tt in range(8):
                        S.inc(tensor.transpose(
                            ps4[:, tt:tt + 1],
                            s_rrow[0:1, 128 * tt:128 * (tt + 1)],
                            s_ident[0:1, 0:1]), "ct")
                block.tensor(_t)
                ct_cols = S.cnt("ct")

                def _s(scalar):
                    S.wait(scalar, "ct", ct_cols)
                    S.inc(scalar.copy(s_nmc[:, :], ps3[:, 0:8]), "cs")
                    S.inc(scalar.copy(s_rsc[:, :], ps4[:, 0:8]), "cs")
                block.scalar(_s)
                cs_cols = S.cnt("cs")

                def _v(vector):
                    S.wait(vector, "cs", cs_cols)
                    S.inc(vector.tensor_scalar(s_nmc[:, :], s_nmc[:, :], -1.0,
                                               None, ALU.mult), "cv")
                block.vector(_v)
                cv_nmc = S.cnt("cv")

                def _v(vector):
                    S.wait(vector, "lg", lg_xy)
                    S.wait(vector, "cs", lg_mrb)
                    for j in range(16):
                        S.inc(vector.tensor_sub(s_t1[:, :], s_y1[j][:, :],
                                                s_meanb[:, :]), "cv")
                        S.wait(vector, "cv", S.cnt("cv"))
                        S.inc(vector.tensor_mul(s_y1[j][:, :], s_t1[:, :],
                                                s_rstdb[:, :]), "cv")
                block.vector(_v)
                cv_ln1 = S.cnt("cv")

                if "y1" in taps:
                    def _g(gpsimd):
                        S.wait(gpsimd, "cv", cv_ln1)
                        for j in range(16):
                            S.inc(gpsimd.dma_start(
                                out=tap_aps["t_y1"][128 * j:128 * (j + 1), :],
                                in_=s_y1[j][:, :]), "lg", 16)
                    block.gpsimd(_g)

                for tt in range(8):
                    def _t(tensor, tt=tt):
                        S.wait(tensor, "lg", lg_st)
                        S.wait(tensor, "cs", S.cnt("cs"))
                        S.inc(tensor.transpose(
                            ps1[:, 0:8], s_raw8[:, 128 * tt:128 * (tt + 1)],
                            s_ident[0:8, 0:8]), "ct")
                    block.tensor(_t)
                    ct_l = S.cnt("ct")

                    def _s(scalar, ct_l=ct_l):
                        S.wait(scalar, "ct", ct_l)
                        S.wait(scalar, "cv", S.cnt("cv"))
                        S.inc(scalar.copy(s_lg[:, :], ps1[:, 0:8]), "cs")
                    block.scalar(_s)
                    cs_l = S.cnt("cs")

                    def _v(vector, tt=tt, cs_l=cs_l):
                        S.wait(vector, "cs", cs_l)
                        S.wait(vector, "cs", lg_rsb)
                        S.wait(vector, "cv", cv_nmc)
                        S.inc(vector.scalar_tensor_tensor(
                            s_lg[:, :], s_rsumb[:, :], s_nmc[:, tt:tt + 1],
                            s_lg[:, :], ALU.mult, ALU.add), "cv")
                        S.wait(vector, "cv", S.cnt("cv"))
                        S.inc(vector.tensor_scalar(s_lg[:, :], s_lg[:, :],
                                                   s_rsc[:, tt:tt + 1], None,
                                                   ALU.mult), "cv")
                        S.wait(vector, "cv", S.cnt("cv"))
                        S.inc(vector.max(s_srt[:, :], s_lg[:, :]), "cv")
                        S.wait(vector, "cv", S.cnt("cv"))
                        S.inc(vector.max_index(s_idxs[:, :], s_srt[:, :],
                                               s_lg[:, :]), "cv")
                        S.wait(vector, "cv", S.cnt("cv"))
                        S.inc(vector.tensor_scalar(s_tmpu[:, 0:2],
                                                   s_idxs[:, 0:2], EXPBITS,
                                                   None, ALU.bitwise_or), "cv")
                        S.wait(vector, "cv", S.cnt("cv"))
                        S.inc(vector.tensor_scalar(s_ekl[:, :],
                                                   s_tmpu[:, 0:2].bitcast(F32),
                                                   EXP23, None, ALU.subtract),
                              "cv")
                        S.inc(vector.tensor_sub(s_c1[:, :], s_srt[:, 1:2],
                                                s_srt[:, 0:1]), "cv")
                        S.wait(vector, "cv", S.cnt("cv"))
                        S.inc(vector.tensor_copy(s_c3[:, :], s_c1[:, :]), "cv")
                    block.vector(_v)
                    cv_g = S.cnt("cv")

                    def _s(scalar, cv_g=cv_g):
                        S.wait(scalar, "cv", cv_g)
                        S.inc(scalar.activation(s_c2[:, :], s_c1[:, :], AF.Exp),
                              "cs")
                    block.scalar(_s)
                    cs_eg = S.cnt("cs")

                    def _v(vector, tt=tt, cs_eg=cs_eg):
                        S.wait(vector, "cs", cs_eg)
                        S.inc(vector.tensor_scalar(s_c3[:, :], s_c2[:, :], 1.0,
                                                   None, ALU.add), "cv")
                        S.wait(vector, "cv", S.cnt("cv"))
                        S.inc(vector.reciprocal(s_w1c[:, tt:tt + 1],
                                                s_c3[:, :]), "cv")
                        S.wait(vector, "cv", S.cnt("cv"))
                        S.inc(vector.tensor_scalar(s_w2c[:, tt:tt + 1],
                                                   s_w1c[:, tt:tt + 1], -1.0,
                                                   1.0, ALU.mult, ALU.add), "cv")
                        S.wait(vector, "cv", S.cnt("cv"))
                    block.vector(_v)
                    cv_w = S.cnt("cv")

                    def _vd(vector, cv_w=cv_w):
                        S.wait(vector, "cv", cv_w)
                        S.inc(vector.tensor_copy(s_c3[:, :], s_ekl[:, 0:1]),
                              "cv")
                        S.inc(vector.tensor_copy(s_c3[:, :], s_ekl[:, 1:2]),
                              "cv")
                    block.vector(_vd)
                    cv_wd = S.cnt("cv")

                    def _s(scalar, tt=tt, cv_wd=cv_wd):
                        S.wait(scalar, "cv", cv_wd)
                        S.inc(scalar.dma_start(
                            out=d_sel[128 * tt:128 * (tt + 1), :],
                            in_=s_ekl[:, :]), "la", 16)
                    block.scalar(_s)
                la_sel = S.cnt("la")

                def _g(gpsimd):
                    S.wait(gpsimd, "la", la_sel)
                    flat = d_sel.ap().rearrange("t k -> (t k)").unsqueeze(0)
                    S.inc(gpsimd.dma_start(out=s_ef[:, :], in_=flat), "lg", 16)
                block.gpsimd(_g)
                lg_ef = S.cnt("lg")

                cs_eb0 = S.cnt("cs")

                def _t4(tensor):
                    S.wait(tensor, "lg", lg_ef)
                    S.wait(tensor, "cs", cs_eb0)
                    for q in range(4):
                        if q >= 2:
                            S.wait(tensor, "cs", cs_eb0 + q - 1)
                        ps = ps1 if q % 2 == 0 else ps2
                        S.inc(tensor.matmul(
                            ps[0:8, :], s_onesr[0:1, 0:8],
                            s_ef[0:1, 512 * q:512 * (q + 1)], start=True,
                            stop=True), "ct")
                block.tensor(_t4)
                ct_eb0 = S.cnt("ct") - 4

                def _s4(scalar):
                    for q in range(4):
                        S.wait(scalar, "ct", ct_eb0 + q + 1)
                        ps = ps1 if q % 2 == 0 else ps2
                        S.inc(scalar.copy(s_eb8[:, 512 * q:512 * (q + 1)],
                                          ps[0:8, :]), "cs")
                block.scalar(_s4)
                cs_eb = S.cnt("cs")

                def _v(vector):
                    S.wait(vector, "lg", lg_ef)
                    S.wait(vector, "cs", cs_eb)
                    S.inc(vector.memset(s_z8[:, :], 0.0), "cv")
                    S.wait(vector, "cv", S.cnt("cv"))
                    S.inc(vector.tensor_scalar(s_ind8[:, :], s_eb8[:, :],
                                               s_io8c[:, 0:1], None,
                                               ALU.is_equal), "cv")
                    S.wait(vector, "cv", S.cnt("cv"))
                    S.inc(vector.tensor_tensor_scan(
                        s_cum8[:, :], s_ind8[:, :], s_z8[:, :], 0.0,
                        ALU.add, ALU.add), "cv")
                    S.wait(vector, "cv", S.cnt("cv"))
                    S.inc(vector.tensor_copy(
                        s_mycnt[:, :], s_cum8[:, 2 * TOK - 1:2 * TOK]), "cv")
                    S.wait(vector, "cv", S.cnt("cv"))
                    S.inc(vector.tensor_scalar(s_cum8[:, :], s_cum8[:, :], 1.0,
                                               None, ALU.subtract), "cv")
                    S.wait(vector, "cv", S.cnt("cv"))
                    S.inc(vector.tensor_mul(s_cum8[:, :], s_cum8[:, :],
                                            s_ind8[:, :]), "cv")
                    S.wait(vector, "cv", S.cnt("cv"))
                    S.inc(vector.tensor_copy(s_gb8[:, :], s_mycnt[:, :]), "cv")
                block.vector(_v)
                cv_cum = S.cnt("cv")

                cs_base = S.cnt("cs")

                def _t(tensor):
                    S.wait(tensor, "cv", cv_cum)
                    for q in range(4):
                        if q > 0:
                            S.wait(tensor, "cs", cs_base + q)
                        S.inc(tensor.matmul(ps1[0:1, 0:512], s_ones[0:8, 0:1],
                                            s_cum8[:, 512 * q:512 * (q + 1)],
                                            start=True, stop=True), "ct")
                block.tensor(_t)
                ct_lp0 = S.cnt("ct") - 4

                def _s(scalar):
                    for q in range(4):
                        S.wait(scalar, "ct", ct_lp0 + q + 1)
                        S.inc(scalar.copy(s_lpos[0:1, 512 * q:512 * (q + 1)],
                                          ps1[0:1, 0:512]), "cs")
                block.scalar(_s)
                cs_lp = S.cnt("cs")

                def _t(tensor):
                    S.wait(tensor, "cv", cv_cum)
                    S.wait(tensor, "cs", S.cnt("cs"))
                    S.inc(tensor.transpose(ps2[0:1, 0:8], s_mycnt[:, 0:1],
                                           s_ident[0:8, 0:8]), "ct")
                block.tensor(_t)
                ct_cnt = S.cnt("ct")

                def _s(scalar):
                    S.wait(scalar, "ct", ct_cnt)
                    S.inc(scalar.copy(s_cntr[0:1, 0:8], ps2[0:1, 0:8]), "cs")
                    S.wait(scalar, "cs", S.cnt("cs"))
                    S.inc(scalar.dma_start(out=d_cnt_in[0:1, :],
                                           in_=s_cntr[0:1, 0:8]), "la", 16)
                block.scalar(_s)
                la_cnt = S.cnt("la")

                def _g(gpsimd):
                    S.wait(gpsimd, "la", la_cnt)
                    S.inc(gpsimd.collective_compute(
                        "AllGather", ALU.bypass, replica_groups=RG,
                        ins=[d_cnt_in.ap().opt()], outs=[d_cnt.ap().opt()]),
                        "cc", 1)
                    S.wait(gpsimd, "cc", S.cnt("cc"))
                    S.inc(gpsimd.dma_start(out=s_c88[:, :], in_=d_cnt[:, :]),
                          "lg", 16)
                    S.inc(gpsimd.dma_start(out=s_rohr[:, :], in_=i_roh),
                          "lg", 16)
                block.gpsimd(_g)
                lg_c88 = S.cnt("lg")

                def _t5(tensor):
                    S.wait(tensor, "lg", lg_c88)
                    S.wait(tensor, "cs", S.cnt("cs"))
                    S.inc(tensor.matmul(ps3[0:8, 0:8], s_onesr[0:1, 0:8],
                                        s_rohr[0:1, :], start=True,
                                        stop=True), "ct")
                block.tensor(_t5)
                ct_rohb = S.cnt("ct")

                def _s5(scalar):
                    S.wait(scalar, "ct", ct_rohb)
                    S.inc(scalar.copy(s_rohb[:, :], ps3[0:8, 0:8]), "cs")
                block.scalar(_s5)
                cs_rohb = S.cnt("cs")

                def _t(tensor):
                    S.wait(tensor, "lg", lg_c88)
                    S.wait(tensor, "cs", S.cnt("cs"))
                    S.inc(tensor.transpose(ps2[0:8, 0:8], s_c88[:, :],
                                           s_ident[0:8, 0:8]), "ct")
                block.tensor(_t)
                ct_c88 = S.cnt("ct")

                def _s(scalar):
                    S.wait(scalar, "ct", ct_c88)
                    S.inc(scalar.copy(s_cT[:, :], ps2[0:8, 0:8]), "cs")
                block.scalar(_s)
                cs_c88 = S.cnt("cs")

                def _v(vector):
                    S.wait(vector, "cs", max(cs_c88, cs_rohb))
                    S.inc(vector.tensor_tensor_scan(
                        s_cumr[:, :], s_cT[:, :], s_z8[:, 0:8], 0.0,
                        ALU.add, ALU.add), "cv")
                    S.wait(vector, "cv", S.cnt("cv"))
                    S.inc(vector.tensor_mul(s_cumr[:, :], s_cumr[:, :],
                                            s_rohb[:, :]), "cv")
                    S.wait(vector, "cv", S.cnt("cv"))
                    S.inc(vector.reduce_sum(s_incl[:, :], s_cumr[:, :],
                                            axis=AX.X), "cv")
                    S.wait(vector, "cv", S.cnt("cv"))
                    S.inc(vector.tensor_sub(s_gb8[:, :], s_incl[:, :],
                                            s_mycnt[:, :]), "cv")
                    S.wait(vector, "cv", S.cnt("cv"))
                    S.inc(vector.tensor_scalar(s_ind8[:, :], s_ind8[:, :],
                                               s_gb8[:, 0:1], None, ALU.mult),
                          "cv")
                    S.wait(vector, "cv", S.cnt("cv"))
                    S.inc(vector.tensor_copy(s_incl[:, :], s_gb8[:, :]), "cv")
                block.vector(_v)
                cv_gb = S.cnt("cv")

                cs_base2 = S.cnt("cs")

                def _t(tensor):
                    S.wait(tensor, "cv", cv_gb)
                    for q in range(4):
                        if q > 0:
                            S.wait(tensor, "cs", cs_base2 + q)
                        S.inc(tensor.matmul(ps1[0:1, 0:512], s_ones[0:8, 0:1],
                                            s_ind8[:, 512 * q:512 * (q + 1)],
                                            start=True, stop=True), "ct")
                block.tensor(_t)
                ct_gb0 = S.cnt("ct") - 4

                def _s(scalar):
                    for q in range(4):
                        S.wait(scalar, "ct", ct_gb0 + q + 1)
                        S.inc(scalar.copy(s_gbrow[0:1, 512 * q:512 * (q + 1)],
                                          ps1[0:1, 0:512]), "cs")
                block.scalar(_s)
                cs_gb = S.cnt("cs")

                def _v(vector):
                    S.wait(vector, "cs", max(cs_lp, cs_gb))
                    S.inc(vector.tensor_add(s_gbrow[:, :], s_gbrow[:, :],
                                            s_lpos[:, :]), "cv")
                    S.wait(vector, "cv", S.cnt("cv"))
                    S.inc(vector.tensor_scalar(s_ml[:, :], s_lpos[:, :],
                                               float(CLOC), None, ALU.is_lt),
                          "cv")
                    S.wait(vector, "cv", S.cnt("cv"))
                    S.inc(vector.tensor_scalar(s_mg[:, :], s_gbrow[:, :],
                                               float(CAP), None, ALU.is_lt),
                          "cv")
                    S.wait(vector, "cv", S.cnt("cv"))
                    S.inc(vector.tensor_mul(s_mg[:, :], s_mg[:, :],
                                            s_ml[:, :]), "cv")
                    S.wait(vector, "cv", S.cnt("cv"))
                    S.inc(vector.tensor_scalar(s_offr[:, :], s_ef[:, :],
                                               float(CLOC), None, ALU.mult),
                          "cv")
                    S.wait(vector, "cv", S.cnt("cv"))
                    S.inc(vector.tensor_add(s_offr[:, :], s_offr[:, :],
                                            s_lpos[:, :]), "cv")
                    S.wait(vector, "cv", S.cnt("cv"))
                    S.inc(vector.tensor_scalar(s_dmpr[:, :], s_offr[:, :],
                                               float(DUMP), None,
                                               ALU.subtract), "cv")
                    S.wait(vector, "cv", S.cnt("cv"))
                    S.inc(vector.tensor_mul(s_sofr[:, :], s_ml[:, :],
                                            s_dmpr[:, :]), "cv")
                    S.inc(vector.tensor_mul(s_gofr[:, :], s_mg[:, :],
                                            s_dmpr[:, :]), "cv")
                    S.wait(vector, "cv", S.cnt("cv"))
                    S.inc(vector.tensor_scalar(s_sofr[:, :], s_sofr[:, :],
                                               float(DUMP), None, ALU.add),
                          "cv")
                    S.inc(vector.tensor_scalar(s_gofr[:, :], s_gofr[:, :],
                                               float(DUMP), None, ALU.add),
                          "cv")
                    S.wait(vector, "cv", S.cnt("cv"))
                    S.inc(vector.tensor_copy(s_dmpr[:, :], s_sofr[:, :]), "cv")
                    S.inc(vector.tensor_copy(s_dmpr[:, :], s_gofr[:, :]), "cv")
                block.vector(_v)
                cv_offs = S.cnt("cv")

                if "route" in taps:
                    def _s(scalar):
                        S.wait(scalar, "cv", cv_offs)
                        for ri, row in enumerate((s_sofr, s_gofr, s_ef, s_lpos,
                                                  s_gbrow, s_ml, s_mg)):
                            S.inc(scalar.dma_start(
                                out=tap_aps["t_route"][ri:ri + 1, :],
                                in_=row[:, :]), "la", 16)
                    block.scalar(_s)

                for src_row, dst in ((s_sofr, s_soffc), (s_gofr, s_goffc)):
                    def _t(tensor, src_row=src_row):
                        S.wait(tensor, "cv", cv_offs)
                        S.wait(tensor, "cs", S.cnt("cs"))
                        v = src_row[0:1, :].rearrange("p (t k) -> p k t", k=2)
                        for col in range(16):
                            tt, k = col // 2, col % 2
                            S.inc(tensor.transpose(
                                ps2[:, col:col + 1],
                                v[:, k, 128 * tt:128 * (tt + 1)],
                                s_ident[0:1, 0:1]), "ct")
                    block.tensor(_t)
                    ct_o = S.cnt("ct")

                    def _s(scalar, ct_o=ct_o):
                        S.wait(scalar, "ct", ct_o)
                        S.inc(scalar.copy(s_sm[:, :], ps2[:, 0:16]), "cs")
                    block.scalar(_s)
                    cs_o = S.cnt("cs")

                    def _v(vector, dst=dst, cs_o=cs_o):
                        S.wait(vector, "cs", cs_o)
                        S.inc(vector.tensor_scalar(dst[:, :].bitcast(F32),
                                                   s_sm[:, :], EXP23, None,
                                                   ALU.add), "cv")
                        S.wait(vector, "cv", S.cnt("cv"))
                        S.inc(vector.tensor_scalar(dst[:, :], dst[:, :], MANT,
                                                   None, ALU.bitwise_and), "cv")
                        S.wait(vector, "cv", S.cnt("cv"))
                        S.inc(vector.tensor_copy(s_sm[:, :],
                                                 dst[:, :].bitcast(F32)), "cv")
                    block.vector(_v)
                cv_offc = S.cnt("cv")

                if "route_only" in stages:
                    def _out(gpsimd):
                        S.wait(gpsimd, "cv", cv_offc)
                        for tt in range(NTOKT):
                            S.inc(gpsimd.dma_start(
                                out=o_out[128 * tt:128 * (tt + 1), :],
                                in_=d_xy.ap().rearrange(
                                    "(d r) t -> d (r t)", r=8)[0:128, 0:DD]),
                                  "st", 16)
                        S.wait(gpsimd, "st", S.cnt("st"))
                    block.gpsimd(_out)
                    return

            lg_zero_done = S.cnt("lg")
            for tt in range(NTOKT):
                for half in range(2):
                    def _t(tensor, tt=tt, half=half):
                        S.wait(tensor, "cv", cv_offc)
                        S.wait(tensor, "cs", S.cnt("cs"))
                        for j in range(8):
                            dt = 8 * half + j
                            S.inc(tensor.transpose(
                                pst[:, 128 * j:128 * (j + 1)],
                                s_y1[dt][:, 128 * tt:128 * (tt + 1)],
                                s_identb[:, :]), "ct")
                    block.tensor(_t)
                    ct_y1t = S.cnt("ct")

                    def _s(scalar, half=half, ct_y1t=ct_y1t):
                        S.wait(scalar, "ct", ct_y1t)
                        S.wait(scalar, "lg", S.cnt("lg"))
                        S.inc(scalar.copy(
                            s_y1tst[:, 1024 * half:1024 * (half + 1)],
                            pst[:, :]), "cs")
                    block.scalar(_s)
                cs_y1t = S.cnt("cs")

                def _g(gpsimd, tt=tt, cs_y1t=cs_y1t):
                    S.wait(gpsimd, "cs", cs_y1t)
                    S.inc(gpsimd.dma_start(
                        out=d_y1t[128 * tt:128 * (tt + 1), :],
                        in_=s_y1tst[:, :]), "lg", 16)
                    for k in range(2):
                        S.inc(gpsimd.indirect_dma_start(
                            d_gath.ap(),
                            bass.IndirectOffsetOnAxis(
                                ap=s_soffc[:, 2 * tt + k:2 * tt + k + 1],
                                axis=0),
                            s_y1tst[:, :], None), "lg", 16)
                block.gpsimd(_g)
            lg_scat = S.cnt("lg")

            s_h = [sb(fctx, f"s_h{i}", [128, TOK], BF16) for i in range(NJT)]
            with ExitStack() as uctx:
                s_wg = [sb(uctx, f"s_wg{p}", [128, DD], BF16) for p in range(2)]
                s_wu = [sb(uctx, f"s_wu{p}", [128, DD], BF16) for p in range(2)]
                s_g2 = [sb(uctx, f"s_g2{i}", [128, 512]) for i in range(2)]
                lastcv_ps, lastcs_ps = {}, {}
                lastcs_wu = {0: 0, 1: 0}

                for p in range(NJT):
                    pb = p % 2

                    def _g(gpsimd, p=p, pb=pb):
                        S.wait(gpsimd, "cs", lastcs_wu[pb])
                        S.inc(gpsimd.dma_start(
                            out=s_wg[pb][:, :], in_=i_swug[p]), "lg", 16)
                        S.inc(gpsimd.dma_start(
                            out=s_wu[pb][:, :], in_=i_swug[NJT + p]),
                              "lg", 16)
                    block.gpsimd(_g)
                    lg_w = S.cnt("lg")

                    for c2 in range(2):
                        ub = (2 * p + c2) % 2
                        pg, pu = ((ps1, ps2), (ps3, ps4))[ub]
                        gk = "sup%d" % ub

                        def _t(tensor, pb=pb, lg_w=lg_w, c2=c2, pg=pg, pu=pu,
                               gk=gk):
                            S.wait(tensor, "lg", lg_w)
                            S.wait(tensor, "cv", lastcv_ps.get(gk, 0))
                            S.wait(tensor, "cs", lastcs_ps.get(gk, 0))
                            for j in range(16):
                                S.inc(tensor.matmul(
                                    pg[:, :], s_wg[pb][:, 128 * j:128 * (j + 1)],
                                    s_y1[j][:, 512 * c2:512 * (c2 + 1)],
                                    start=(j == 0), stop=(j == 15)), "ct")
                            for j in range(16):
                                S.inc(tensor.matmul(
                                    pu[:, :], s_wu[pb][:, 128 * j:128 * (j + 1)],
                                    s_y1[j][:, 512 * c2:512 * (c2 + 1)],
                                    start=(j == 0), stop=(j == 15)), "ct")
                        block.tensor(_t)
                        ct_up = S.cnt("ct")

                        def _s(scalar, ct_up=ct_up, pg=pg):
                            S.wait(scalar, "ct", ct_up)
                            S.wait(scalar, "cv", S.cnt("cv"))
                            S.inc(scalar.activation(s_g2[(2 * p + c2) % 2][:, :],
                                                    pg[:, :], AF.Silu), "cs")
                        block.scalar(_s)
                        cs_si = S.cnt("cs")
                        lastcs_ps[gk] = cs_si

                        def _v(vector, c2=c2, p=p, cs_si=cs_si, pu=pu):
                            S.wait(vector, "cs", cs_si)
                            S.inc(vector.tensor_mul(
                                s_h[p][:, 512 * c2:512 * (c2 + 1)],
                                s_g2[(2 * p + c2) % 2][:, :], pu[:, :]), "cv")
                        block.vector(_v)
                        lastcv_ps[gk] = S.cnt("cv")
                    lastcs_wu[pb] = S.cnt("cs")
                cv_h = S.cnt("cv")

                pass

        with ExitStack() as dctx:
            s_wdh = [sb(dctx, f"s_wdh{p}", [128, JH // 2], BF16)
                     for p in range(2)]
            s_yst = sb(dctx, "s_yst", [128, TOK], BF16)
            s_gT = [sb(dctx, f"s_gT{j}", [128, CLOC], BF16)
                    for j in range(16)]
            s_he = [[sb(dctx, f"s_he{par}_{i}", [128, CLOC], BF16)
                     for i in range(NIT)] for par in range(2)]
            s_wg2 = [sb(dctx, f"s_wg2{p}", [128, DD], BF16) for p in range(2)]
            s_wu2 = [sb(dctx, f"s_wu2{p}", [128, DD], BF16) for p in range(2)]
            s_wd2q = [[sb(dctx, f"s_wd2q{par}_{p}", [128, 512], BF16)
                       for p in range(NIT)] for par in range(2)]
            s_ge2 = [sb(dctx, "s_ge2", [128, CLOC])]
            s_ost512 = [sb(dctx, f"s_ost512{i}", [128, 512], BF16)
                        for i in range(2)]


            cs_ph0 = S.cnt("cs")
            lastcs_sdn = [cs_ph0]
            lastcs_w2 = {0: cs_ph0, 1: cs_ph0}
            lastcs_gt = [cs_ph0]
            lastcs_ps = [cs_ph0]
            lastcv_ps = [0]
            lastcs_dnev = [cs_ph0]
            cs_dn_done = {0: cs_ph0, 1: cs_ph0}
            lg_wdset = {}
            la_ostp = {0: S.cnt("la"), 1: S.cnt("la")}
            ostp = [0]
            moe_state = {}
            NSET = 4 * EE

            def _wdset_load(gpsimd, si):
                e2, q2 = si // 4, si % 4
                for p in range(NIT):
                    S.inc(gpsimd.dma_start(
                        out=s_wd2q[si % 2][p][:, :],
                        in_=i_wdn[e2][128 * p:128 * (p + 1),
                                      512 * q2:512 * (q2 + 1)]), "lg", 16)
                lg_wdset[si] = S.cnt("lg")

            def _g0(gpsimd):
                for si in (0, 1):
                    _wdset_load(gpsimd, si)
            block.gpsimd(_g0)



            NJH = NJT // 2

            def emit_sdown(dt):
                for half in range(2):
                    def _l(gpsimd, dt=dt, half=half):
                        S.wait(gpsimd, "cs", lastcs_sdn[0])
                        S.inc(gpsimd.dma_start(
                            out=s_wdh[half][:, :],
                            in_=i_swdn[dt][:, (JH // 2) * half:
                                           (JH // 2) * (half + 1)]),
                              "lg", 16)
                    block.gpsimd(_l)
                    moe_state[("lgwd", half)] = S.cnt("lg")

                for half in range(2):
                    lg_wd = moe_state[("lgwd", half)]

                    def _t(tensor, half=half, lg_wd=lg_wd):
                        S.wait(tensor, "lg", lg_wd)
                        S.wait(tensor, "cv", cv_h)
                        if half == 0:
                            S.wait(tensor, "cs", lastcs_sdn[0])
                        for i in range(NJH):
                            ig = NJH * half + i
                            for c2 in range(2):
                                S.inc(tensor.matmul(
                                    pse[:, 512 * c2:512 * (c2 + 1)],
                                    s_wdh[half][:, 128 * i:128 * (i + 1)],
                                    s_h[ig][:, 512 * c2:512 * (c2 + 1)],
                                    start=(ig == 0),
                                    stop=(ig == NJT - 1)), "ct")
                    block.tensor(_t)
                    ct_dn = S.cnt("ct")

                def _s(scalar, dt=dt, ct_dn=ct_dn):
                    S.wait(scalar, "ct", ct_dn)
                    S.wait(scalar, "la", S.cnt("la"))
                    S.inc(scalar.copy(s_yst[:, 0:512], pse[:, 0:512]), "cs")
                    S.inc(scalar.copy(s_yst[:, 512:1024],
                                      pse[:, 512:1024]), "cs")
                    S.wait(scalar, "cs", S.cnt("cs"))
                    S.inc(scalar.dma_start(
                        out=d_ysh[128 * dt:128 * (dt + 1), :],
                        in_=s_yst[:, :]), "la", 16)
                block.scalar(_s)
                lastcs_sdn[0] = S.cnt("cs")

            def emit_moe_gt(e):
                def _ls(sync, e=e):
                    S.wait(sync, "lg", lg_scat)
                    S.wait(sync, "cs", lastcs_gt[0])
                    for j in range(16):
                        S.inc(sync.dma_start(
                            out=s_gT[j][:, :],
                            in_=d_gath[CLOC * e:CLOC * (e + 1),
                                       128 * j:128 * (j + 1)],
                            transpose=True), "ls", 16)
                block.sync(_ls)
                moe_state[("lsg", e)] = S.cnt("ls")

            def emit_moe_up(e, p):
                pb = p % 2
                par = e % 2

                def _g(gpsimd, e=e, p=p, pb=pb):
                    S.wait(gpsimd, "cs", lastcs_w2[pb])
                    S.inc(gpsimd.dma_start(
                        out=s_wg2[pb][:, :], in_=i_wug[e][p]), "lg", 16)
                    S.inc(gpsimd.dma_start(
                        out=s_wu2[pb][:, :], in_=i_wug[e][NIT + p]),
                          "lg", 16)
                block.gpsimd(_g)
                lg_w2 = S.cnt("lg")
                ls_g = moe_state[("lsg", e)]

                def _t(tensor, pb=pb, lg_w2=lg_w2, ls_g=ls_g):
                    S.wait(tensor, "lg", lg_w2)
                    S.wait(tensor, "ls", ls_g)
                    S.wait(tensor, "cv", lastcv_ps[0])
                    S.wait(tensor, "cs", lastcs_ps[0])
                    for j in range(16):
                        S.inc(tensor.matmul(
                            ps1[:, 0:CLOC],
                            s_wg2[pb][:, 128 * j:128 * (j + 1)],
                            s_gT[j][:, :], start=(j == 0),
                            stop=(j == 15)), "ct")
                    for j in range(16):
                        S.inc(tensor.matmul(
                            ps2[:, 0:CLOC],
                            s_wu2[pb][:, 128 * j:128 * (j + 1)],
                            s_gT[j][:, :], start=(j == 0),
                            stop=(j == 15)), "ct")
                block.tensor(_t)
                ct_up2 = S.cnt("ct")

                def _s(scalar, ct_up2=ct_up2, pb=pb):
                    S.wait(scalar, "ct", ct_up2)
                    S.wait(scalar, "cv", S.cnt("cv"))
                    S.inc(scalar.activation(s_ge2[0][:, 0:CLOC],
                                            ps1[:, 0:CLOC], AF.Silu), "cs")
                block.scalar(_s)
                cs_si2 = S.cnt("cs")
                lastcs_ps[0] = cs_si2
                lastcs_w2[pb] = cs_si2

                def _v(vector, p=p, cs_si2=cs_si2, pb=pb, par=par):
                    S.wait(vector, "cs", cs_si2)
                    S.wait(vector, "cs", cs_dn_done[par])
                    S.inc(vector.tensor_mul(s_he[par][p][:, :],
                                            s_ge2[0][:, 0:CLOC],
                                            ps2[:, 0:CLOC]), "cv")
                block.vector(_v)
                lastcv_ps[0] = S.cnt("cv")
                if p == NIT - 1:
                    moe_state[("cvhe", e)] = S.cnt("cv")
                    lastcs_gt[0] = S.cnt("cs")

            STB = ((0, 128, "ps3"), (128, 128, "ps4"), (256, 64, "psf"))

            def emit_moe_dn(e, q):
                si = 4 * e + q
                par = si % 2
                hpar = e % 2
                lg_set = lg_wdset[si]
                cv_he = moe_state[("cvhe", e)]
                psb = {"ps3": ps3, "ps4": ps4, "psf": psf}

                def _t(tensor, e=e, par=par, hpar=hpar,
                       lg_set=lg_set, cv_he=cv_he):
                    S.wait(tensor, "lg", lg_set)
                    S.wait(tensor, "cv", cv_he)
                    S.wait(tensor, "cs", lastcs_dnev[0])
                    for p in range(NIT):
                        for (r0, nr, pn) in STB:
                            S.inc(tensor.matmul(
                                psb[pn][0:nr, :],
                                s_he[hpar][p][:, r0:r0 + nr],
                                s_wd2q[par][p][:, :],
                                start=(p == 0), stop=(p == NIT - 1)), "ct")
                block.tensor(_t)
                ct_g = S.cnt("ct")

                def _s(scalar, e=e, q=q, si=si, ct_g=ct_g):
                    S.wait(scalar, "ct", ct_g)
                    for (r0, nr, pn) in STB:
                        op = ostp[0]
                        ostp[0] ^= 1
                        S.wait(scalar, "la", la_ostp[op])
                        S.inc(scalar.copy(s_ost512[op][0:nr, :],
                                          psb[pn][0:nr, :]), "cs")
                        S.wait(scalar, "cs", S.cnt("cs"))
                        S.inc(scalar.dma_start(
                            out=d_outb[CLOC * e + r0:CLOC * e + r0 + nr,
                                       512 * q:512 * (q + 1)],
                            in_=s_ost512[op][0:nr, :]), "la", 16)
                        la_ostp[op] = S.cnt("la")
                block.scalar(_s)
                lastcs_dnev[0] = S.cnt("cs")
                if q == 3:
                    cs_dn_done[e % 2] = S.cnt("cs")
                if si + 2 < NSET:
                    def _gw(gpsimd, si=si):
                        S.wait(gpsimd, "cs", lastcs_dnev[0])
                        _wdset_load(gpsimd, si + 2)
                    block.gpsimd(_gw)

            moe_steps = []
            for e in range(EE):
                moe_steps.append(("gt", e))
                ups = [("up", e, p) for p in range(NIT)]
                dns = ([("dn", e - 1, q) for q in range(4)]
                       if e > 0 else [])
                mix = []
                ui = di = 0
                for slot_i in range(len(ups) + len(dns)):
                    if di < len(dns) and slot_i % 4 == 2:
                        mix.append(dns[di]); di += 1
                    elif ui < len(ups):
                        mix.append(ups[ui]); ui += 1
                    else:
                        mix.append(dns[di]); di += 1
                moe_steps.extend(mix)
            moe_steps.extend([("dn", EE - 1, q) for q in range(4)])

            k = 0

            def emit_one():
                nonlocal k
                st_ = moe_steps[k]
                if st_[0] == "gt":
                    emit_moe_gt(st_[1])
                elif st_[0] == "up":
                    emit_moe_up(st_[1], st_[2])
                else:
                    emit_moe_dn(st_[1], st_[2])
                k += 1

            per_dt = (len(moe_steps) + 15) // 16
            for dt in range(16):
                emit_sdown(dt)
                for _ in range(per_dt):
                    if k < len(moe_steps):
                        emit_one()
            while k < len(moe_steps):
                emit_one()
            la_moe = S.cnt("la")
            la_ysh = la_moe

        with ExitStack() as octx:
            s_ta = sb(octx, "s_ta", [128, DD], BF16)
            s_gA = sb(octx, "s_gA", [128, DD], BF16)
            s_gB = sb(octx, "s_gB", [128, DD], BF16)
            s_sh = sb(octx, "s_sh", [128, DD], BF16)
            s_z = sb(octx, "s_z", [128, DD])
            s_sq2 = sb(octx, "s_sq2", [128, DD])
            s_m2 = sb(octx, "s_m2", [128, 1])
            s_ss2 = sb(octx, "s_ss2", [128, 1])
            s_msq = sb(octx, "s_msq", [128, 1])
            s_v2 = sb(octx, "s_v2", [128, 1])
            s_nm2 = sb(octx, "s_nm2", [128, 1])
            s_orow = sb(octx, "s_orow", [128, DD])

            for tt in range(NTOKT):
                def _g(gpsimd, tt=tt):
                    S.wait(gpsimd, "la", la_moe)
                    S.wait(gpsimd, "cv", S.cnt("cv"))
                    S.inc(gpsimd.dma_start(
                        out=s_ta[:, :],
                        in_=d_y1t[128 * tt:128 * (tt + 1), :]), "lg", 16)
                    S.inc(gpsimd.indirect_dma_start(
                        s_gA[:, :], None, d_outb.ap(),
                        bass.IndirectOffsetOnAxis(
                            ap=s_goffc[:, 2 * tt:2 * tt + 1], axis=0)),
                        "lg", 16)
                    S.inc(gpsimd.indirect_dma_start(
                        s_gB[:, :], None, d_outb.ap(),
                        bass.IndirectOffsetOnAxis(
                            ap=s_goffc[:, 2 * tt + 1:2 * tt + 2], axis=0)),
                        "lg", 16)
                block.gpsimd(_g)
                lg_cmb = S.cnt("lg")

                def _ls(sync, tt=tt):
                    S.wait(sync, "la", la_moe)
                    S.wait(sync, "cv", S.cnt("cv"))
                    S.inc(sync.dma_start(
                        out=s_sh[:, :],
                        in_=d_ysh[0:DD, 128 * tt:128 * (tt + 1)],
                        transpose=True), "ls", 16)
                block.sync(_ls)
                ls_sh = S.cnt("ls")

                def _v(vector, tt=tt, lg_cmb=lg_cmb, ls_sh=ls_sh):
                    S.wait(vector, "lg", lg_cmb)
                    S.wait(vector, "ls", ls_sh)
                    S.wait(vector, "cs", S.cnt("cs"))
                    S.inc(vector.tensor_scalar(s_z[:, :], s_gA[:, :],
                                               s_w1c[:, tt:tt + 1], None,
                                               ALU.mult), "cv")
                    S.wait(vector, "cv", S.cnt("cv"))
                    S.inc(vector.scalar_tensor_tensor(
                        s_z[:, :], s_gB[:, :], s_w2c[:, tt:tt + 1], s_z[:, :],
                        ALU.mult, ALU.add), "cv")
                    S.wait(vector, "cv", S.cnt("cv"))
                    S.inc(vector.tensor_add(s_z[:, :], s_z[:, :],
                                            s_ta[:, :]), "cv")
                    S.wait(vector, "cv", S.cnt("cv"))
                    S.inc(vector.tensor_add(s_z[:, :], s_z[:, :],
                                            s_sh[:, :]), "cv")
                    S.wait(vector, "cv", S.cnt("cv"))
                    S.inc(vector.reduce_sum(s_m2[:, :], s_z[:, :],
                                            axis=AX.X), "cv")
                    S.wait(vector, "cv", S.cnt("cv"))
                    S.inc(vector.tensor_scalar(s_m2[:, :], s_m2[:, :],
                                               1.0 / DD, None, ALU.mult), "cv")
                block.vector(_v)
                cv_z = S.cnt("cv")

                def _s(scalar, cv_z=cv_z):
                    S.wait(scalar, "cv", cv_z)
                    S.inc(scalar.activation(s_sq2[:, :], s_z[:, :], AF.Square,
                                            accum_out=s_ss2[:, 0:1]), "cs")
                block.scalar(_s)
                cs_ss = S.cnt("cs")

                def _v(vector, cs_ss=cs_ss):
                    S.wait(vector, "cs", cs_ss)
                    S.inc(vector.tensor_mul(s_msq[:, :], s_m2[:, :],
                                            s_m2[:, :]), "cv")
                    S.wait(vector, "cv", S.cnt("cv"))
                    S.inc(vector.tensor_scalar(s_v2[:, :], s_ss2[:, :],
                                               1.0 / DD, None, ALU.mult), "cv")
                    S.wait(vector, "cv", S.cnt("cv"))
                    S.inc(vector.tensor_sub(s_v2[:, :], s_v2[:, :],
                                            s_msq[:, :]), "cv")
                block.vector(_v)
                cv_v2 = S.cnt("cv")

                def _v3(vector, cv_v2=cv_v2):
                    S.wait(vector, "cv", cv_v2)
                    S.inc(vector.tensor_scalar(s_v2[:, :], s_v2[:, :], 1e-5,
                                               None, ALU.add), "cv")
                    S.wait(vector, "cv", S.cnt("cv"))
                    S.inc(vector.tensor_copy(s_msq[:, :], s_v2[:, :]), "cv")
                block.vector(_v3)
                cv_v2b = S.cnt("cv")

                def _s(scalar, cv_v2b=cv_v2b):
                    S.wait(scalar, "cv", cv_v2b)
                    S.inc(scalar.activation(s_v2[:, :], s_v2[:, :], AF.Ln), "cs")
                    S.wait(scalar, "cs", S.cnt("cs"))
                    S.inc(scalar.mul(s_v2[:, :], s_v2[:, :], -0.5), "cs")
                    S.wait(scalar, "cs", S.cnt("cs"))
                    S.inc(scalar.activation(s_v2[:, :], s_v2[:, :], AF.Exp), "cs")
                block.scalar(_s)
                cs_r2 = S.cnt("cs")

                def _v(vector, cs_r2=cs_r2):
                    S.wait(vector, "cs", cs_r2)
                    S.inc(vector.scalar_tensor_tensor(
                        s_nm2[:, :], s_m2[:, :], -1.0, s_v2[:, :],
                        ALU.mult, ALU.mult), "cv")
                    S.wait(vector, "cv", S.cnt("cv"))
                block.vector(_v)
                cv_nm2 = S.cnt("cv")

                def _v4(vector, cv_nm2=cv_nm2):
                    S.wait(vector, "cv", cv_nm2)
                    S.inc(vector.tensor_scalar(s_orow[:, :], s_z[:, :],
                                               s_v2[:, 0:1], s_nm2[:, 0:1],
                                               ALU.mult, ALU.add), "cv")
                block.vector(_v4)
                cv_out = S.cnt("cv")

                def _s(scalar, tt=tt, cv_out=cv_out):
                    S.wait(scalar, "cv", cv_out)
                    S.inc(scalar.dma_start(
                        out=o_out[128 * tt:128 * (tt + 1), :],
                        in_=s_orow[:, :]), "st", 16)
                block.scalar(_s)

            def _fin(scalar):
                S.wait(scalar, "st", S.cnt("st"))
            block.scalar(_fin)


def _host_inputs(inputs):
    import ml_dtypes
    x = np.ascontiguousarray(inputs["L0_attn_x"], dtype=np.float32)
    alog = np.ascontiguousarray(inputs["L0_attn_A_log"], dtype=np.float32)
    dsk = np.ascontiguousarray(inputs["L0_attn_D"], dtype=np.float32)
    wx = np.ascontiguousarray(inputs["L0_attn_W_x"], dtype=np.float32)
    wdt = np.ascontiguousarray(inputs["L0_attn_W_dt"], dtype=np.float32)
    bdt = np.ascontiguousarray(inputs["L0_attn_b_dt"], dtype=np.float32)
    rout = np.ascontiguousarray(inputs["L0_ffn_router"], dtype=np.float32)
    wug = np.ascontiguousarray(inputs["L0_ffn_w_up_gate"], dtype=np.float32)
    wdn = np.ascontiguousarray(inputs["L0_ffn_w_down"], dtype=np.float32)
    swug = np.ascontiguousarray(inputs["L0_ffn_shared_w_up_gate"],
                                dtype=np.float32)
    swdn = np.ascontiguousarray(inputs["L0_ffn_shared_w_down"],
                                dtype=np.float32)

    wugP = np.ascontiguousarray(
        wug.reshape(EE, 16, 128, 2 * NIT, 128).transpose(0, 3, 2, 1, 4)
           .reshape(EE, 2 * NIT, 128, 16 * 128))
    swugP = np.ascontiguousarray(
        swug.reshape(16, 128, 2 * NJT, 128).transpose(2, 1, 0, 3)
            .reshape(2 * NJT, 128, 16 * 128))
    swdnP = np.ascontiguousarray(
        swdn.reshape(NJT, 128, 16, 128).transpose(2, 1, 0, 3)
            .reshape(16, 128, NJT * 128))

    ident = np.eye(128, dtype=np.float32)
    identb = np.eye(128, dtype=ml_dtypes.bfloat16)
    ones = np.ones((128, 1), dtype=np.float32)
    iota8c = np.arange(8, dtype=np.float32).reshape(8, 1)

    in_maps = []
    for r in range(NCORE):
        dsl = slice(DSL * r, DSL * (r + 1))
        xyrl = np.empty((128, 16), dtype=np.int32)
        for j in range(16):
            xyrl[:, j] = (128 * j + np.arange(128)) * 8 + r
        strl = (np.arange(16, dtype=np.int32) * 8 + r).reshape(16, 1)
        roh = np.zeros((1, 8), dtype=np.float32)
        roh[0, r] = 1.0
        m = {
            "x_dsl": np.ascontiguousarray(x[:, :, dsl]),
            "wx_sl": np.ascontiguousarray(wx[:, dsl]),
            "wdt_sl": np.ascontiguousarray(wdt[dsl, :]),
            "bdt_sl": np.ascontiguousarray(bdt[dsl].reshape(DSL, 1)),
            "alog_sl": np.ascontiguousarray(alog[dsl, :]),
            "d_sl": np.ascontiguousarray(dsk[dsl].reshape(DSL, 1)),
            "rout_sl": np.ascontiguousarray(rout[dsl, :]),
            "rout": rout,
            "wug": wugP, "wdn": wdn, "swug": swugP, "swdn": swdnP,
            "identf": ident, "identb": identb, "onesf": ones,
            "xyrl": xyrl,
            "strl_a": np.ascontiguousarray(strl[0:8]),
            "strl_b": np.ascontiguousarray(strl[8:10]),
            "strl_c": np.ascontiguousarray(np.array([[9 * 8 + r], [9 * 8 + r]],
                                                    dtype=np.int32)),
            "roh": roh, "iota8c": iota8c,
            "onesr": np.ones((1, 128), dtype=np.float32),
            "sel16": np.repeat(np.eye(16, dtype=np.float32), 128,
                               axis=1).reshape(16, 2048),
        }
        in_maps.append(m)
    return in_maps


def kernel(**inputs):
    nc, _ = build()
    in_maps = _host_inputs(inputs)
    res = run_bass_kernel_spmd(nc, in_maps, core_ids=list(range(NCORE)))
    out = np.empty((BB, TT_, DD), dtype=np.float32)
    for r in range(NCORE):
        b = r // 2
        t0 = (TT_ // 2) * (r % 2)
        out[b, t0:t0 + TOK, :] = np.asarray(res.results[r]["out_tok"])
    return out


# version 08
# speedup vs baseline: 1.1356x; 1.1356x over previous
import numpy as np
import concourse.bass as bass
import concourse.mybir as mybir
from concourse.bass_utils import run_bass_kernel_spmd
from contextlib import ExitStack

F32 = mybir.dt.float32
BF16 = mybir.dt.bfloat16
I32 = mybir.dt.int32
U32 = mybir.dt.uint32
AF = mybir.ActivationFunctionType
ALU = mybir.AluOpType
AX = mybir.AxisListType

BB, TT_, DD = 4, 2048, 2048
NST, RK = 16, 64
EE, IH, JH, CAP = 8, 1408, 5632, 2560
NCORE, TOK, DSL, NDT = 8, 1024, 256, 2
TC, NCH = 512, 4
CLOC = 320
NTOKT = TOK // 128
NJT = JH // 128
NIT = IH // 128
RG = [list(range(NCORE))]
EXP23 = 8388608.0
EXPBITS = 0x4B000000
MANT = 0x7FFFFF
DUMP = EE * CLOC


class Sems:
    def __init__(self):
        self.c, self.h = {}, {}

    def make(self, ctx, nc, *names):
        for n in names:
            self.h[n] = ctx.enter_context(nc.semaphore(n))
            self.c[n] = 0

    def inc(self, instr, name, n=1):
        instr.then_inc(self.h[name], n)
        self.c[name] += n
        return instr

    def cnt(self, name):
        return self.c[name]

    def wait(self, eng, name, val=None):
        eng.wait_ge(self.h[name], self.c[name] if val is None else val)


def build(stages=("mamba", "ffn"), taps=()):
    nc = bass.Bass("TRN2", target_bir_lowering=False,
                   detect_race_conditions=False)
    do_ffn = "ffn" in stages

    def ein(name, shape, dt=F32):
        return nc.dram_tensor(name, shape, dt, kind="ExternalInput").ap()

    i_x_dsl = ein("x_dsl", [BB, TT_, DSL])
    i_wx_sl = ein("wx_sl", [96, DSL])
    i_wdt_sl = ein("wdt_sl", [DSL, RK])
    i_bdt_sl = ein("bdt_sl", [DSL, 1])
    i_alog_sl = ein("alog_sl", [DSL, NST])
    i_d_sl = ein("d_sl", [DSL, 1])
    i_rout_sl = ein("rout_sl", [DSL, EE])
    i_rout = ein("rout", [DD, EE])
    i_wug = ein("wug", [EE, 2 * NIT, 128, 16 * 128])
    i_wdn = ein("wdn", [EE, IH, DD])
    i_swug = ein("swug", [2 * NJT, 128, 16 * 128])
    i_swdn = ein("swdn", [16, 128, NJT * 128])
    i_ident = ein("identf", [128, 128])
    i_identb = ein("identb", [128, 128], BF16)
    i_ones = ein("onesf", [128, 1])
    i_xyrl = ein("xyrl", [128, 16], I32)
    i_strl_a = ein("strl_a", [8, 1], I32)
    i_strl_b = ein("strl_b", [2, 1], I32)
    i_strl_c = ein("strl_c", [2, 1], I32)
    i_roh = ein("roh", [1, 8])
    i_iota8c = ein("iota8c", [8, 1])
    i_onesr = ein("onesr", [1, 128])
    i_sel16 = ein("sel16", [16, 2048])

    o_out = nc.dram_tensor("out_tok", [TOK, DD], F32, kind="ExternalOutput").ap()
    tap_aps = {}
    if "xy" in taps:
        tap_aps["t_xy"] = nc.dram_tensor("t_xy", [DSL, BB * TT_], F32,
                                         kind="ExternalOutput").ap()
    if "y1" in taps:
        tap_aps["t_y1"] = nc.dram_tensor("t_y1", [DD, TOK], BF16,
                                         kind="ExternalOutput").ap()
    if "route" in taps:
        tap_aps["t_route"] = nc.dram_tensor("t_route", [8, 2 * TOK], F32,
                                            kind="ExternalOutput").ap()
    if "st" in taps:
        tap_aps["t_st"] = nc.dram_tensor("t_st", [80, TOK], F32,
                                         kind="ExternalOutput").ap()
        tap_aps["t_raw8"] = nc.dram_tensor("t_raw8", [8, TOK], F32,
                                           kind="ExternalOutput").ap()
        tap_aps["t_st2"] = nc.dram_tensor("t_st2", [2, TOK], F32,
                                          kind="ExternalOutput").ap()
        tap_aps["t_xyg"] = nc.dram_tensor("t_xyg", [128, TOK], BF16,
                                          kind="ExternalOutput").ap()
        tap_aps["t_mr"] = nc.dram_tensor("t_mr", [2, TOK], F32,
                                         kind="ExternalOutput").ap()
        tap_aps["t_mb"] = nc.dram_tensor("t_mb", [128, TOK], F32,
                                         kind="ExternalOutput").ap()
        tap_aps["t_rb"] = nc.dram_tensor("t_rb", [128, TOK], F32,
                                         kind="ExternalOutput").ap()
    if "ag" in taps:
        tap_aps["t_ag"] = nc.dram_tensor("t_ag", [DD, 1024], BF16,
                                         kind="ExternalOutput").ap()
    if "dbg" in taps:
        tap_aps["t_xpr"] = nc.dram_tensor("t_xpr", [96, BB * TT_], F32,
                                          kind="ExternalOutput").ap()
        tap_aps["t_delta"] = nc.dram_tensor("t_delta", [128, TC], F32,
                                            kind="ExternalOutput").ap()
        tap_aps["t_u"] = nc.dram_tensor("t_u", [128, TC], F32,
                                        kind="ExternalOutput").ap()
        tap_aps["t_bx"] = nc.dram_tensor("t_bx", [128, 8 * TC], F32,
                                         kind="ExternalOutput").ap()
        tap_aps["t_hh"] = nc.dram_tensor("t_hh", [128, 8 * TC], F32,
                                         kind="ExternalOutput").ap()

    d_xp_in = nc.dram_tensor("xp_in", [96, BB * TT_], F32)
    d_xpr = nc.dram_tensor("xpr_out", [96, BB * TT_], F32)
    d_st_in = nc.dram_tensor("st_in", [10, BB * TT_], F32)
    d_st = nc.dram_tensor("st_out", [10 * NCORE, TOK], F32)
    d_xy_in = nc.dram_tensor("xy_in", [DSL, BB * TT_], BF16)
    d_xy = nc.dram_tensor("xy_out", [DD * NCORE, TOK], BF16)
    d_cnt_in = nc.dram_tensor("cnt_in", [1, 8], F32)
    d_cnt = nc.dram_tensor("cnt_out", [8, 8], F32)
    d_sel = nc.dram_tensor("sel_d", [TOK, 2], F32)
    d_rows = nc.dram_tensor("rows_d", [8, 2048], F32)
    d_y1t = nc.dram_tensor("y1t_d", [TOK, DD], BF16)
    d_gath = nc.dram_tensor("gath_d", [DUMP + 128, DD], BF16)
    d_outb = nc.dram_tensor("outb_d", [DUMP + 128, DD], BF16)
    d_ysh = nc.dram_tensor("ysh_d", [DD, TOK], BF16)

    with ExitStack() as ctx:
        S = Sems()
        S.make(ctx, nc, "ls", "lg", "la", "cc", "ct", "cs", "cv", "st")

        def sb(c, name, shape, dt=F32):
            return c.enter_context(nc.sbuf_tensor(name, shape, dt))

        s_ident = sb(ctx, "s_ident", [128, 128])
        s_identb = sb(ctx, "s_identb", [128, 128], BF16)
        s_ones = sb(ctx, "s_ones", [128, 1])
        s_io8c = sb(ctx, "s_io8c", [8, 1])
        s_onesr = sb(ctx, "s_onesr", [1, 128])


        ps1 = ctx.enter_context(nc.psum_tensor("ps1", [128, 512], F32))
        ps2 = ctx.enter_context(nc.psum_tensor("ps2", [128, 512], F32))
        ps3 = ctx.enter_context(nc.psum_tensor("ps3", [128, 512], F32))
        ps4 = ctx.enter_context(nc.psum_tensor("ps4", [128, 512], F32))
        pse = ctx.enter_context(nc.psum_tensor("pse", [128, 1024], F32))
        pst = ctx.enter_context(nc.psum_tensor("pst", [128, 1024], BF16))
        psf = ctx.enter_context(nc.psum_tensor("psf", [128, 512], F32))

        block = ctx.enter_context(nc.Block())

        keep = dict(nc=nc, block=block, S=S, sb=sb,
                    s_ident=s_ident, s_identb=s_identb, s_ones=s_ones,
                    s_onesr=s_onesr,
                    s_io8c=s_io8c, ps1=ps1, ps2=ps2, ps3=ps3, ps4=ps4,
                    pse=pse, pst=pst, psf=psf,
                    d_xy=d_xy, d_st=d_st, d_cnt_in=d_cnt_in, d_cnt=d_cnt,
                    d_sel=d_sel, d_rows=d_rows, d_y1t=d_y1t, d_gath=d_gath,
                    d_outb=d_outb, d_ysh=d_ysh, i_xyrl=i_xyrl,
                    i_strl_a=i_strl_a, i_strl_b=i_strl_b, i_strl_c=i_strl_c,
                    i_rout=i_rout, i_swug=i_swug, i_swdn=i_swdn, i_wug=i_wug,
                    i_wdn=i_wdn, i_roh=i_roh, o_out=o_out, tap_aps=tap_aps,
                    taps=taps, stages=stages)

        with ExitStack() as mctx:
            s_wxt = [sb(mctx, f"s_wxt{i}", [128, 96], BF16) for i in range(NDT)]
            s_wdtt = [sb(mctx, f"s_wdtt{i}", [128, 128], BF16) for i in range(NDT)]
            s_bdt = sb(mctx, "s_bdt", [128, NDT])
            s_negA = sb(mctx, "s_negA", [128, NST * NDT])
            s_onepD = sb(mctx, "s_onepD", [128, NDT])
            s_routsl = sb(mctx, "s_routsl", [128, EE * NDT])
            s_carry = sb(mctx, "s_carry", [128, NDT * BB * 2 * 8])
            s_wxf = sb(mctx, "s_wxf", [96, 256])
            s_wxb = sb(mctx, "s_wxb", [96, 256], BF16)
            s_wdtf = sb(mctx, "s_wdtf", [128, 128])
            s_wdtb = sb(mctx, "s_wdtb", [128, 128], BF16)
            s_xraw = [sb(mctx, f"s_xraw{i}", [128, 1024]) for i in range(2)]
            s_xdm = [sb(mctx, f"s_xdm{i}", [128, TC]) for i in range(NDT)]
            s_xbf = [sb(mctx, f"s_xbf{i}", [128, TC], BF16) for i in range(NDT)]
            s_dpf = sb(mctx, "s_dpf", [128, TC])
            s_dpb = sb(mctx, "s_dpb", [128, TC], BF16)
            s_delta = [sb(mctx, f"s_delta{i}", [128, TC]) for i in range(NDT)]
            s_expz = sb(mctx, "s_expz", [128, TC])
            s_u = [sb(mctx, f"s_u{i}", [128, TC]) for i in range(NDT)]
            s_xpre = [sb(mctx, f"s_xpre{i}", [128, TC]) for i in range(NDT)]
            s_sq = sb(mctx, "s_sq", [128, TC])
            s_Bb = [sb(mctx, f"s_Bb{i}", [128, 8 * TC]) for i in range(2)]
            s_Cb = [sb(mctx, f"s_Cb{i}", [128, 8 * TC]) for i in range(2)]
            s_ab = [sb(mctx, f"s_ab{i}", [128, 8 * TC]) for i in range(2)]
            s_bx1 = sb(mctx, "s_bx1", [128, 8 * TC])
            s_hh1 = sb(mctx, "s_hh1", [128, 8 * TC])
            s_corr = sb(mctx, "s_corr", [128, 8])
            s_xy = [sb(mctx, f"s_xy{i}", [128, TC]) for i in range(NDT)]
            s_stev = sb(mctx, "s_stev", [128, TC])
            s_bcr = sb(mctx, "s_bcr", [16, TC], BF16)
            s_sel16b = sb(mctx, "s_sel16b", [16, 2048], BF16)
            s_sel16 = sb(mctx, "s_sel16", [16, 2048])


            def _setup_sync(sync):
                S.inc(sync.dma_start(out=s_ident[:, :], in_=i_ident), "ls", 16)
                S.inc(sync.dma_start(out=s_identb[:, :], in_=i_identb), "ls", 16)
                S.inc(sync.dma_start(out=s_ones[:, :], in_=i_ones), "ls", 16)
                S.inc(sync.dma_start(out=s_io8c[:, :], in_=i_iota8c), "ls", 16)
                S.inc(sync.dma_start(out=s_onesr[:, :], in_=i_onesr), "ls", 16)
                S.inc(sync.dma_start(out=s_sel16[:, :], in_=i_sel16), "ls", 16)
                S.inc(sync.dma_start(out=s_wxf[:, :], in_=i_wx_sl), "ls", 16)
                for i in range(NDT):
                    S.inc(sync.dma_start(out=s_wdtf[:, 64 * i:64 * (i + 1)],
                                         in_=i_wdt_sl[128 * i:128 * (i + 1), :]),
                          "ls", 16)
                    S.inc(sync.dma_start(out=s_bdt[:, i:i + 1],
                                         in_=i_bdt_sl[128 * i:128 * (i + 1), :]),
                          "ls", 16)
                    S.inc(sync.dma_start(out=s_onepD[:, i:i + 1],
                                         in_=i_d_sl[128 * i:128 * (i + 1), :]),
                          "ls", 16)
                    S.inc(sync.dma_start(out=s_negA[:, NST * i:NST * (i + 1)],
                                         in_=i_alog_sl[128 * i:128 * (i + 1), :]),
                          "ls", 16)
                    S.inc(sync.dma_start(out=s_routsl[:, EE * i:EE * (i + 1)],
                                         in_=i_rout_sl[128 * i:128 * (i + 1), :]),
                          "ls", 16)
            block.sync(_setup_sync)
            ls0 = S.cnt("ls")

            def _setup_vec(vector):
                S.wait(vector, "ls", ls0)
                S.inc(vector.tensor_copy(s_wxb[:, :], s_wxf[:, :]), "cv")
                S.inc(vector.tensor_copy(s_wdtb[:, :], s_wdtf[:, :]), "cv")
                S.inc(vector.tensor_copy(s_sel16b[:, :], s_sel16[:, :]), "cv")
                S.inc(vector.memset(s_carry[:, :], 0.0), "cv")
                S.inc(vector.tensor_scalar(s_onepD[:, :], s_onepD[:, :], 1.0,
                                           None, ALU.add), "cv")
            block.vector(_setup_vec)
            cv0 = S.cnt("cv")

            def _setup_sca(scalar):
                S.wait(scalar, "ls", ls0)
                S.inc(scalar.activation(s_negA[:, :], s_negA[:, :], AF.Exp), "cs")
                S.wait(scalar, "cs", S.cnt("cs"))
                S.inc(scalar.mul(s_negA[:, :], s_negA[:, :], -1.0), "cs")
            block.scalar(_setup_sca)

            for i in range(NDT):
                def _t(tensor, i=i):
                    S.wait(tensor, "cv", cv0)
                    S.wait(tensor, "cs", S.cnt("cs"))
                    S.inc(tensor.transpose(pst[:, 0:96],
                                           s_wxb[0:96, 128 * i:128 * (i + 1)],
                                           s_identb[0:96, 0:96]), "ct")
                block.tensor(_t)

                def _s(scalar, i=i):
                    S.wait(scalar, "ct", S.cnt("ct"))
                    S.inc(scalar.copy(s_wxt[i][:, 0:96], pst[:, 0:96]), "cs")
                block.scalar(_s)
            for i in range(NDT):
                def _t(tensor, i=i):
                    S.wait(tensor, "cs", S.cnt("cs"))
                    S.inc(tensor.transpose(pst[0:64, 0:128],
                                           s_wdtb[:, 64 * i:64 * (i + 1)],
                                           s_identb[:, :]), "ct")
                block.tensor(_t)

                def _s(scalar, i=i):
                    S.wait(scalar, "ct", S.cnt("ct"))
                    S.inc(scalar.copy(s_wdtt[i][0:64, 0:128], pst[0:64, 0:128]),
                          "cs")
                block.scalar(_s)

            lastcs_xraw = {0: 0, 1: 0}
            lastct_bcr = {0: 0}
            lastcv_ab = {0: 0, 1: 0}
            lastcv_bb = {0: 0, 1: 0}
            lg_after_xy = [0]

            def load_x_chunk(b, t0, par):
                def _f(sync):
                    S.wait(sync, "cs", lastcs_xraw[par])
                    for g in range(4):
                        S.inc(sync.dma_start(
                            out=s_xraw[par][:, 256 * g:256 * (g + 1)],
                            in_=i_x_dsl[b, t0 + 128 * g:t0 + 128 * (g + 1), :]),
                            "ls", 16)
                block.sync(_f)
                return S.cnt("ls")

            def transpose_x_chunk(ls_x, par):
                def _t(tensor):
                    S.wait(tensor, "ls", ls_x)
                    S.wait(tensor, "cs", S.cnt("cs"))
                    for i in range(NDT):
                        ps = ps1 if i == 0 else ps2
                        for g in range(4):
                            S.inc(tensor.transpose(
                                ps[:, 128 * g:128 * (g + 1)],
                                s_xraw[par][:, 256 * g + 128 * i:256 * g + 128 * (i + 1)],
                                s_ident[:, :]), "ct")
                block.tensor(_t)
                ct_tp = S.cnt("ct")

                def _s(scalar):
                    S.wait(scalar, "ct", ct_tp)
                    for i in range(NDT):
                        ps = ps1 if i == 0 else ps2
                        S.inc(scalar.copy(s_xdm[i][:, :], ps[:, :]), "cs")
                block.scalar(_s)
                lastcs_xraw[par] = S.cnt("cs")
                return S.cnt("cs")

            for b in range(BB):
                for c in range(NCH):
                    t0 = c * TC
                    col0 = b * TT_ + t0
                    par = (b * NCH + c) % 2
                    ls_x = load_x_chunk(b, t0, par)
                    cs_xdm = transpose_x_chunk(ls_x, par)

                    def _v(vector):
                        S.wait(vector, "cs", cs_xdm)
                        S.wait(vector, "ct", S.cnt("ct"))
                        for i in range(NDT):
                            S.inc(vector.tensor_copy(s_xbf[i][:, :],
                                                     s_xdm[i][:, :]), "cv")
                    block.vector(_v)
                    cv_xbf = S.cnt("cv")

                    def _t(tensor):
                        S.wait(tensor, "cv", cv_xbf)
                        S.wait(tensor, "cs", S.cnt("cs"))
                        for i in range(NDT):
                            S.inc(tensor.matmul(ps3[0:96, :], s_wxt[i][:, 0:96],
                                                s_xbf[i][:, :], start=(i == 0),
                                                stop=(i == NDT - 1)), "ct")
                    block.tensor(_t)
                    ct_xp = S.cnt("ct")

                    def _s(scalar):
                        S.wait(scalar, "ct", ct_xp)
                        S.wait(scalar, "la", S.cnt("la"))
                        S.inc(scalar.copy(s_stev[0:96, :], ps3[0:96, :]), "cs")
                        S.wait(scalar, "cs", S.cnt("cs"))
                        S.inc(scalar.dma_start(out=d_xp_in[0:96, col0:col0 + TC],
                                               in_=s_stev[0:96, :]), "la", 16)
                    block.scalar(_s)

            def _cc(gpsimd):
                S.wait(gpsimd, "la", S.cnt("la"))
                S.inc(gpsimd.collective_compute(
                    "AllReduce", ALU.add, replica_groups=RG,
                    ins=[d_xp_in.ap().opt()], outs=[d_xpr.ap().opt()]), "cc", 1)
            block.gpsimd(_cc)
            cc_xp = S.cnt("cc")

            for b in range(BB):
                for c in range(NCH):
                    t0 = c * TC
                    col0 = b * TT_ + t0
                    par = (b * NCH + c) % 2
                    ls_x = load_x_chunk(b, t0, par)
                    cs_xdm = transpose_x_chunk(ls_x, par)

                    def _s(scalar):
                        S.wait(scalar, "cv", S.cnt("cv"))
                        for i in range(NDT):
                            S.inc(scalar.activation(s_xpre[i][:, :], s_xdm[i][:, :],
                                                    AF.Copy,
                                                    scale=s_onepD[:, i:i + 1]),
                                  "cs")
                    block.scalar(_s)
                    cs_xpre = S.cnt("cs")

                    def _g(gpsimd, b=b, c=c):
                        S.wait(gpsimd, "cc", cc_xp)
                        S.wait(gpsimd, "cv", S.cnt("cv"))
                        if "dbg" in taps and b == 0 and c == 0:
                            S.inc(gpsimd.dma_start(out=tap_aps["t_xpr"],
                                                   in_=d_xpr[:, :]), "lg", 16)
                        S.inc(gpsimd.dma_start(out=s_dpf[0:64, :],
                                               in_=d_xpr[0:64, col0:col0 + TC]),
                              "lg", 16)
                    block.gpsimd(_g)
                    lg_dp = S.cnt("lg")

                    def _v(vector):
                        S.wait(vector, "lg", lg_dp)
                        S.inc(vector.tensor_copy(s_dpb[0:64, :], s_dpf[0:64, :]),
                              "cv")
                    block.vector(_v)
                    cv_dpb = S.cnt("cv")

                    def _t(tensor):
                        S.wait(tensor, "cv", cv_dpb)
                        S.wait(tensor, "cs", S.cnt("cs"))
                        for i in range(NDT):
                            ps = ps3 if i == 0 else ps4
                            S.inc(tensor.matmul(ps[:, :], s_wdtt[i][0:64, :],
                                                s_dpb[0:64, :], start=True,
                                                stop=True), "ct")
                    block.tensor(_t)
                    ct_delta = S.cnt("ct")

                    def _s(scalar):
                        S.wait(scalar, "ct", ct_delta)
                        S.wait(scalar, "cv", S.cnt("cv"))
                        for i in range(NDT):
                            ps = ps3 if i == 0 else ps4
                            S.inc(scalar.activation(s_expz[:, :], ps[:, :], AF.Exp,
                                                    bias=s_bdt[:, i:i + 1]), "cs")
                            S.wait(scalar, "cs", S.cnt("cs"))
                            S.inc(scalar.activation(s_delta[i][:, :], s_expz[:, :],
                                                    AF.Ln, bias=1.0), "cs")
                    block.scalar(_s)
                    cs_delta = S.cnt("cs")

                    def _v(vector):
                        S.wait(vector, "cs", max(cs_delta, cs_xdm))
                        for i in range(NDT):
                            S.inc(vector.tensor_mul(s_u[i][:, :], s_delta[i][:, :],
                                                    s_xdm[i][:, :]), "cv")
                    block.vector(_v)
                    cv_u = S.cnt("cv")
                    if "dbg" in taps and b == 0 and c == 0:
                        def _gd(gpsimd):
                            S.wait(gpsimd, "cv", cv_u)
                            S.inc(gpsimd.dma_start(out=tap_aps["t_delta"],
                                                   in_=s_delta[0][:, :]), "lg", 16)
                            S.inc(gpsimd.dma_start(out=tap_aps["t_u"],
                                                   in_=s_u[0][:, :]), "lg", 16)
                        block.gpsimd(_gd)

                    for h in range(2):
                        hpar = (b * NCH + c) * 2 + h
                        def _g(gpsimd, h=h, hpar=hpar):
                            S.wait(gpsimd, "cs", S.cnt("cs"))
                            S.inc(gpsimd.dma_start(
                                out=s_bcr[0:8, :],
                                in_=d_xpr[64 + 8 * h:72 + 8 * h,
                                          col0:col0 + TC]), "lg", 16)
                            S.inc(gpsimd.dma_start(
                                out=s_bcr[8:16, :],
                                in_=d_xpr[80 + 8 * h:88 + 8 * h,
                                          col0:col0 + TC]), "lg", 16)
                        block.gpsimd(_g)
                        lg_bcr = S.cnt("lg")

                        cs_bc0 = S.cnt("cs")

                        def _tb(tensor, lg_bcr=lg_bcr, cs_bc0=cs_bc0,
                                hpar=hpar):
                            S.wait(tensor, "lg", lg_bcr)
                            S.wait(tensor, "cv", S.cnt("cv"))
                            S.wait(tensor, "cs", cs_bc0)
                            for w in range(16):
                                if w >= 2:
                                    S.wait(tensor, "cs", cs_bc0 + w - 1)
                                ps = (ps1, ps2)[w % 2]
                                S.inc(tensor.matmul(
                                    ps[:, :], s_sel16b[:, 128 * w:128 * (w + 1)],
                                    s_bcr[:, :], start=True, stop=True),
                                    "ct")
                        block.tensor(_tb)
                        ct_bc0 = S.cnt("ct") - 16
                        lastct_bcr[0] = S.cnt("ct")

                        def _sb2(scalar, ct_bc0=ct_bc0, hpar=hpar):
                            S.wait(scalar, "cv", lastcv_bb[hpar % 2])
                            for w in range(16):
                                S.wait(scalar, "ct", ct_bc0 + w + 1)
                                dst = s_Bb[hpar % 2] if w < 8 else s_Cb[hpar % 2]
                                n = w % 8
                                ps = (ps1, ps2)[w % 2]
                                S.inc(scalar.copy(
                                    dst[:, n * TC:(n + 1) * TC], ps[:, :]),
                                    "cs")
                        block.scalar(_sb2)
                        lg_bc = S.cnt("cs")

                        for i in range(NDT):
                            un = 2 * h + i
                            ab = s_ab[un % 2]
                            bx = s_bx1
                            hh = s_hh1
                            ccol = ((i * BB + b) * 2 + h) * 8

                            def _s(scalar, i=i, h=h, ab=ab, un=un):
                                S.wait(scalar, "cs", cs_delta)
                                S.wait(scalar, "cv", lastcv_ab[un % 2])
                                for n in range(8):
                                    S.inc(scalar.activation(
                                        ab[:, n * TC:(n + 1) * TC],
                                        s_delta[i][:, :], AF.Exp,
                                        scale=s_negA[:, NST * i + 8 * h + n:
                                                     NST * i + 8 * h + n + 1]),
                                        "cs")
                            block.scalar(_s)
                            cs_ab = S.cnt("cs")

                            def _v(vector, i=i, h=h, ab=ab, bx=bx, hh=hh,
                                   ccol=ccol, cs_ab=cs_ab, lg_bc=lg_bc, b=b,
                                   c=c, hpar=hpar):
                                S.wait(vector, "cs", max(cs_ab, lg_bc))
                                if h == 0:
                                    S.wait(vector, "lg", lg_after_xy[0])
                                for n in range(8):
                                    S.inc(vector.tensor_mul(
                                        bx[:, n * TC:(n + 1) * TC], s_u[i][:, :],
                                        s_Bb[hpar % 2][:, n * TC:(n + 1) * TC]),
                                        "cv")
                                abst = ab[:, :].rearrange("p (n t) -> p n t",
                                                          t=TC)[:, :, 0]
                                bxst = bx[:, :].rearrange("p (n t) -> p n t",
                                                          t=TC)[:, :, 0]
                                S.inc(vector.tensor_mul(
                                    s_corr[:, :], abst,
                                    s_carry[:, ccol:ccol + 8]), "cv")
                                S.wait(vector, "cv", S.cnt("cv"))
                                S.inc(vector.tensor_add(bxst, bxst,
                                                        s_corr[:, :]), "cv")
                                S.wait(vector, "cv", S.cnt("cv"))
                                S.inc(vector.memset(abst, 0.0), "cv")
                                S.wait(vector, "cv", S.cnt("cv"))
                                S.inc(vector.tensor_tensor_scan(
                                    hh[:, :], ab[:, :], bx[:, :], 0.0,
                                    ALU.mult, ALU.add), "cv")
                                hhe = hh[:, :].rearrange("p (n t) -> p n t",
                                                         t=TC)[:, :, TC - 1]
                                S.wait(vector, "cv", S.cnt("cv"))
                                S.inc(vector.tensor_copy(
                                    s_carry[:, ccol:ccol + 8], hhe), "cv")
                                S.wait(vector, "cv", S.cnt("cv"))
                                S.inc(vector.tensor_mul(hh[:, :], hh[:, :],
                                                        s_Cb[hpar % 2][:, :]),
                                      "cv")
                                S.inc(vector.tensor_add(
                                    hh[:, 0:4 * TC], hh[:, 0:4 * TC],
                                    hh[:, 4 * TC:8 * TC]), "cv")
                                S.inc(vector.tensor_add(
                                    hh[:, 0:2 * TC], hh[:, 0:2 * TC],
                                    hh[:, 2 * TC:4 * TC]), "cv")
                                S.inc(vector.tensor_add(
                                    hh[:, 0:TC], hh[:, 0:TC],
                                    hh[:, TC:2 * TC]), "cv")
                                S.wait(vector, "cs", cs_xpre)
                                if h == 0:
                                    S.inc(vector.tensor_add(
                                        s_xy[i][:, :], s_xpre[i][:, :],
                                        hh[:, 0:TC]), "cv")
                                else:
                                    S.inc(vector.tensor_add(
                                        s_xy[i][:, :], s_xy[i][:, :],
                                        hh[:, 0:TC]), "cv")
                            block.vector(_v)
                            if ("dbg" in taps and b == 0 and c == 0
                                    and h == 0 and i == 0):
                                cv_u0 = S.cnt("cv")

                                def _gd2(gpsimd):
                                    S.wait(gpsimd, "cv", cv_u0)
                                    S.inc(gpsimd.dma_start(
                                        out=tap_aps["t_bx"],
                                        in_=bx[:, :]), "lg", 16)
                                    S.inc(gpsimd.dma_start(
                                        out=tap_aps["t_hh"],
                                        in_=hh[:, :]), "lg", 16)
                                block.gpsimd(_gd2)
                                lg_d0 = S.cnt("lg")

                                def _vw(vector):
                                    S.wait(vector, "lg", lg_d0)
                                block.vector(_vw)

                                def _sw(scalar):
                                    S.wait(scalar, "lg", lg_d0)
                                block.scalar(_sw)
                    cv_xy = S.cnt("cv")

                    def _g(gpsimd, b=b, c=c):
                        S.wait(gpsimd, "cv", cv_xy)
                        for i in range(NDT):
                            S.inc(gpsimd.dma_start(
                                out=d_xy_in[128 * i:128 * (i + 1),
                                            col0:col0 + TC],
                                in_=s_xy[i][:, :]), "lg", 16)
                            if "xy" in taps:
                                S.inc(gpsimd.dma_start(
                                    out=tap_aps["t_xy"][128 * i:128 * (i + 1),
                                                        col0:col0 + TC],
                                    in_=s_xy[i][:, :]), "lg", 16)
                    block.gpsimd(_g)
                    lg_after_xy[0] = S.cnt("lg")

                    def _s(scalar):
                        S.wait(scalar, "cv", cv_xy)
                        S.inc(scalar.activation(s_sq[:, :], s_xy[0][:, :],
                                                AF.Square), "cs")
                    block.scalar(_s)
                    cs_q0 = S.cnt("cs")

                    def _t(tensor):
                        S.wait(tensor, "cv", cv_xy)
                        S.wait(tensor, "la", S.cnt("la"))
                        for i in range(NDT):
                            S.inc(tensor.matmul(psf[0:8, :],
                                                s_routsl[:, EE * i:EE * (i + 1)],
                                                s_xy[i][:, :], start=(i == 0),
                                                stop=(i == 1)), "ct")
                        for i in range(NDT):
                            S.inc(tensor.matmul(psf[32:33, :], s_ones[:, 0:1],
                                                s_xy[i][:, :], start=(i == 0),
                                                stop=(i == 1)), "ct")
                        S.wait(tensor, "cs", cs_q0)
                        S.inc(tensor.matmul(psf[64:65, :], s_ones[:, 0:1],
                                            s_sq[:, :], start=True, stop=False),
                              "ct")
                    block.tensor(_t)
                    ct_q0 = S.cnt("ct")

                    def _s(scalar):
                        S.wait(scalar, "ct", ct_q0)
                        S.inc(scalar.activation(s_sq[:, :], s_xy[1][:, :],
                                                AF.Square), "cs")
                    block.scalar(_s)
                    cs_q1 = S.cnt("cs")

                    def _t(tensor):
                        S.wait(tensor, "cs", cs_q1)
                        S.inc(tensor.matmul(psf[64:65, :], s_ones[:, 0:1],
                                            s_sq[:, :], start=False, stop=True),
                              "ct")
                    block.tensor(_t)
                    ct_stats = S.cnt("ct")

                    def _s(scalar):
                        S.wait(scalar, "ct", ct_stats)
                        S.wait(scalar, "la", S.cnt("la"))
                        S.inc(scalar.copy(s_stev[0:8, :], psf[0:8, :]), "cs")
                        S.inc(scalar.copy(s_stev[32:33, :], psf[32:33, :]), "cs")
                        S.inc(scalar.copy(s_stev[64:65, :], psf[64:65, :]), "cs")
                        S.wait(scalar, "cs", S.cnt("cs"))
                        S.inc(scalar.dma_start(out=d_st_in[0:8, col0:col0 + TC],
                                               in_=s_stev[0:8, :]), "la", 16)
                        S.inc(scalar.dma_start(out=d_st_in[8:9, col0:col0 + TC],
                                               in_=s_stev[32:33, :]), "la", 16)
                        S.inc(scalar.dma_start(out=d_st_in[9:10, col0:col0 + TC],
                                               in_=s_stev[64:65, :]), "la", 16)
                    block.scalar(_s)

            def _cc2(gpsimd):
                S.wait(gpsimd, "lg", S.cnt("lg"))
                S.wait(gpsimd, "la", S.cnt("la"))
                S.inc(gpsimd.collective_compute(
                    "AllGather", ALU.bypass, replica_groups=RG,
                    ins=[d_xy_in.ap().opt()], outs=[d_xy.ap().opt()]), "cc", 1)
                S.inc(gpsimd.collective_compute(
                    "AllReduce", ALU.add, replica_groups=RG,
                    ins=[d_st_in.ap().opt()], outs=[d_st.ap().opt()]), "cc", 1)
            block.gpsimd(_cc2)

            bar = {k: S.cnt(k) for k in ("ls", "lg", "la", "cc", "ct", "cs", "cv")}
            for sec in (block.sync, block.vector, block.scalar, block.tensor,
                        block.gpsimd):
                def _barr(eng):
                    for k, v in bar.items():
                        eng.wait_ge(S.h[k], v)
                sec(_barr)

        if not do_ffn:
            def _out(gpsimd):
                for tt in range(NTOKT):
                    S.inc(gpsimd.dma_start(
                        out=o_out[128 * tt:128 * (tt + 1), :],
                        in_=d_xy.ap().rearrange("(d r) t -> d (r t)",
                                                r=8)[0:128, 0:DD]), "st", 16)
                S.wait(gpsimd, "st", S.cnt("st"))
            block.gpsimd(_out)
            return nc, tap_aps

        _ffn(keep)
    return nc, tap_aps


def _ffn(E):
    nc, block, S, sb = E["nc"], E["block"], E["S"], E["sb"]
    s_ident, s_identb, s_ones, s_io8c, s_onesr = (E[k] for k in
        ("s_ident", "s_identb", "s_ones", "s_io8c", "s_onesr"))
    ps1, ps2, ps3, ps4, pse, pst, psf = (E[k] for k in
        ("ps1", "ps2", "ps3", "ps4", "pse", "pst", "psf"))
    d_xy, d_st, d_cnt_in, d_cnt, d_sel, d_rows = (E[k] for k in
        ("d_xy", "d_st", "d_cnt_in", "d_cnt", "d_sel", "d_rows"))
    d_y1t, d_gath, d_outb, d_ysh = (E[k] for k in
        ("d_y1t", "d_gath", "d_outb", "d_ysh"))
    i_xyrl, i_strl_a, i_strl_b, i_strl_c = (E[k] for k in
        ("i_xyrl", "i_strl_a", "i_strl_b", "i_strl_c"))
    i_rout, i_swug, i_swdn, i_wug, i_wdn, i_roh = (E[k] for k in
        ("i_rout", "i_swug", "i_swdn", "i_wug", "i_wdn", "i_roh"))
    o_out = E["o_out"]
    tap_aps, taps = E["tap_aps"], E["taps"]
    stages = E.get("stages", ())

    with ExitStack() as fctx:
        s_y1tst = sb(fctx, "s_y1tst", [128, DD], BF16)
        s_zero = sb(fctx, "s_zero", [128, 512], BF16)
        s_xyrl = sb(fctx, "s_xyrl", [128, 16], I32)
        s_sia = sb(fctx, "s_sia", [8, 1], I32)
        s_sib = sb(fctx, "s_sib", [2, 1], I32)
        s_sic = sb(fctx, "s_sic", [2, 1], I32)
        s_soffc = sb(fctx, "s_soffc", [128, 16], I32)
        s_goffc = sb(fctx, "s_goffc", [128, 16], I32)
        s_w1c = sb(fctx, "s_w1c", [128, 8])
        s_w2c = sb(fctx, "s_w2c", [128, 8])

        def _idx(sync):
            S.inc(sync.dma_start(out=s_xyrl[:, :], in_=i_xyrl), "ls", 16)
            S.inc(sync.dma_start(out=s_sia[:, :], in_=i_strl_a), "ls", 16)
            S.inc(sync.dma_start(out=s_sib[:, :], in_=i_strl_b), "ls", 16)
            S.inc(sync.dma_start(out=s_sic[:, :], in_=i_strl_c), "ls", 16)
        block.sync(_idx)
        ls_idx = S.cnt("ls")

        def _z(vector):
            S.inc(vector.memset(s_zero[:, :], 0.0), "cv")
        block.vector(_z)
        cv_zero = S.cnt("cv")

        def _zg(gpsimd):
            S.wait(gpsimd, "cv", cv_zero)
            nrow = DUMP + 128
            for r0 in range(0, nrow, 128):
                nr = min(128, nrow - r0)
                for c0 in range(0, DD, 512):
                    S.inc(gpsimd.dma_start(
                        out=d_gath[r0:r0 + nr, c0:c0 + 512],
                        in_=s_zero[0:nr, :]), "lg", 16)
                    S.inc(gpsimd.dma_start(
                        out=d_outb[r0:r0 + nr, c0:c0 + 512],
                        in_=s_zero[0:nr, :]), "lg", 16)
        block.gpsimd(_zg)

        with ExitStack() as actx:
            s_y1 = [sb(fctx, f"s_y1_{j}", [128, TOK], BF16) for j in range(16)]

            def _gxy(gpsimd):
                S.wait(gpsimd, "ls", ls_idx)
                if "ag" in taps:
                    S.inc(gpsimd.dma_start(
                        out=tap_aps["t_ag"],
                        in_=d_xy.ap().rearrange("(d r) t -> d (r t)",
                                                r=8)[:, 0:1024]), "lg", 16)
                xyv = d_xy.ap()
                for j in range(16):
                    S.inc(gpsimd.indirect_dma_start(
                        s_y1[j][:, :], None, xyv,
                        bass.IndirectOffsetOnAxis(ap=s_xyrl[:, j:j + 1], axis=0)),
                        "lg", 16)
            block.gpsimd(_gxy)
            lg_xy = S.cnt("lg")

            with ExitStack() as bctx:
                s_t1 = sb(bctx, "s_t1", [128, TOK])
                s_meanb = sb(bctx, "s_meanb", [128, TOK])
                s_rstdb = sb(bctx, "s_rstdb", [128, TOK])
                s_raw8 = sb(bctx, "s_raw8", [8, TOK])
                s_st2 = sb(bctx, "s_st2", [2, TOK])
                s_ssq2 = sb(bctx, "s_ssq2", [2, TOK])
                s_mrow = sb(bctx, "s_mrow", [1, TOK])
                s_rrow = sb(bctx, "s_rrow", [1, TOK])
                s_rsr = sb(bctx, "s_rsr", [1, 8])
                s_cntr = sb(bctx, "s_cntr", [1, 8])
                s_rt = sb(bctx, "s_rt", [128, 128])
                s_rsumb = sb(bctx, "s_rsumb", [128, 8])
                s_nmc = sb(bctx, "s_nmc", [128, 8])
                s_rsc = sb(bctx, "s_rsc", [128, 8])
                s_lg = sb(bctx, "s_lg", [128, 8])
                s_srt = sb(bctx, "s_srt", [128, 8])
                s_idxs = sb(bctx, "s_idxs", [128, 8], U32)
                s_tmpu = sb(bctx, "s_tmpu", [128, 2], U32)
                s_ekl = sb(bctx, "s_ekl", [128, 2])
                s_c1 = sb(bctx, "s_c1", [128, 1])
                s_c2 = sb(bctx, "s_c2", [128, 1])
                s_c3 = sb(bctx, "s_c3", [128, 1])
                s_eb8 = sb(bctx, "s_eb8", [8, 2 * TOK])
                s_ind8 = sb(bctx, "s_ind8", [8, 2 * TOK])
                s_cum8 = sb(bctx, "s_cum8", [8, 2 * TOK])
                s_z8 = sb(bctx, "s_z8", [8, 2 * TOK])
                s_lpos = sb(bctx, "s_lpos", [1, 2 * TOK])
                s_gbrow = sb(bctx, "s_gbrow", [1, 2 * TOK])
                s_ml = sb(bctx, "s_ml", [1, 2 * TOK])
                s_mg = sb(bctx, "s_mg", [1, 2 * TOK])
                s_offr = sb(bctx, "s_offr", [1, 2 * TOK])
                s_dmpr = sb(bctx, "s_dmpr", [1, 2 * TOK])
                s_sofr = sb(bctx, "s_sofr", [1, 2 * TOK])
                s_gofr = sb(bctx, "s_gofr", [1, 2 * TOK])
                s_ef = sb(bctx, "s_ef", [1, 2 * TOK])
                s_mycnt = sb(bctx, "s_mycnt", [8, 1])
                s_c88 = sb(bctx, "s_c88", [8, 8])
                s_cT = sb(bctx, "s_cT", [8, 8])
                s_cumr = sb(bctx, "s_cumr", [8, 8])
                s_rohb = sb(bctx, "s_rohb", [8, 8])
                s_rohr = sb(bctx, "s_rohr", [1, 8])
                s_incl = sb(bctx, "s_incl", [8, 1])
                s_gb8 = sb(bctx, "s_gb8", [8, 1])
                s_sm = sb(bctx, "s_sm", [128, 16])

                def _gst(gpsimd):
                    S.wait(gpsimd, "ls", ls_idx)
                    stv = d_st.ap()
                    S.inc(gpsimd.indirect_dma_start(
                        s_raw8[:, :], None, stv,
                        bass.IndirectOffsetOnAxis(ap=s_sia[:, 0:1], axis=0)),
                        "lg", 16)
                    S.inc(gpsimd.indirect_dma_start(
                        s_st2[:, :], None, stv,
                        bass.IndirectOffsetOnAxis(ap=s_sib[:, 0:1], axis=0)),
                        "lg", 16)
                    S.inc(gpsimd.indirect_dma_start(
                        s_ssq2[:, :], None, stv,
                        bass.IndirectOffsetOnAxis(ap=s_sic[:, 0:1], axis=0)),
                        "lg", 16)
                block.gpsimd(_gst)
                lg_st = S.cnt("lg")

                if "st" in taps:
                    def _gtap(gpsimd):
                        S.wait(gpsimd, "lg", lg_st)
                        S.wait(gpsimd, "cc", S.cnt("cc"))
                        S.inc(gpsimd.dma_start(out=tap_aps["t_st"],
                                               in_=d_st[:, :]), "lg", 16)
                        S.inc(gpsimd.dma_start(out=tap_aps["t_raw8"],
                                               in_=s_raw8[:, :]), "lg", 16)
                        S.inc(gpsimd.dma_start(out=tap_aps["t_st2"],
                                               in_=s_st2[:, :]), "lg", 16)
                        S.inc(gpsimd.dma_start(out=tap_aps["t_xyg"],
                                               in_=s_y1[0][:, :]), "lg", 16)
                    block.gpsimd(_gtap)
                    lg_tap = S.cnt("lg")

                    def _vtap(vector):
                        S.wait(vector, "lg", lg_tap)
                    block.vector(_vtap)

                def _rt(sync):
                    S.inc(sync.dma_start(
                        out=s_rt[:, :].rearrange("p (j e) -> p j e", e=8),
                        in_=i_rout.rearrange("(j p) e -> p j e", p=128)),
                        "ls", 16)
                block.sync(_rt)
                ls_rt = S.cnt("ls")

                def _t(tensor):
                    S.wait(tensor, "ls", ls_rt)
                    for j in range(16):
                        S.inc(tensor.matmul(ps3[0:8, 0:1],
                                            s_rt[:, 8 * j:8 * (j + 1)],
                                            s_ones[:, 0:1], start=(j == 0),
                                            stop=(j == 15)), "ct")
                block.tensor(_t)
                ct_rs = S.cnt("ct")

                def _s(scalar):
                    S.wait(scalar, "ct", ct_rs)
                    S.inc(scalar.copy(s_gb8[:, :], ps3[0:8, 0:1]), "cs")
                block.scalar(_s)
                cs_rs = S.cnt("cs")

                def _t(tensor):
                    S.wait(tensor, "cs", cs_rs)
                    S.inc(tensor.transpose(ps4[0:1, 0:8], s_gb8[:, 0:1],
                                           s_ident[0:8, 0:8]), "ct")
                block.tensor(_t)
                ct_rs2 = S.cnt("ct")

                def _s(scalar):
                    S.wait(scalar, "ct", ct_rs2)
                    S.inc(scalar.copy(s_rsr[0:1, 0:8], ps4[0:1, 0:8]), "cs")
                block.scalar(_s)

                def _t2(tensor):
                    S.wait(tensor, "cs", S.cnt("cs"))
                    S.inc(tensor.matmul(ps1[:, 0:8], s_onesr[0:1, :],
                                        s_rsr[0:1, 0:8], start=True,
                                        stop=True), "ct")
                block.tensor(_t2)
                ct_rsb = S.cnt("ct")

                def _s2(scalar):
                    S.wait(scalar, "ct", ct_rsb)
                    S.inc(scalar.copy(s_rsumb[:, :], ps1[:, 0:8]), "cs")
                block.scalar(_s2)
                lg_rsb = S.cnt("cs")

                def _v(vector):
                    S.wait(vector, "lg", lg_st)
                    S.inc(vector.tensor_scalar(s_mrow[:, :], s_st2[0:1, :],
                                               1.0 / DD, None, ALU.mult), "cv")
                    S.wait(vector, "cv", S.cnt("cv"))
                    S.inc(vector.tensor_scalar(s_rrow[:, :], s_ssq2[0:1, :],
                                               1.0 / DD, None, ALU.mult), "cv")
                    S.wait(vector, "cv", S.cnt("cv"))
                    S.inc(vector.tensor_mul(s_t1[0:1, :], s_mrow[:, :],
                                            s_mrow[:, :]), "cv")
                    S.wait(vector, "cv", S.cnt("cv"))
                    S.inc(vector.tensor_sub(s_rrow[:, :], s_rrow[:, :],
                                            s_t1[0:1, :]), "cv")
                block.vector(_v)
                cv_var = S.cnt("cv")

                def _v2(vector):
                    S.wait(vector, "cv", cv_var)
                    S.inc(vector.tensor_scalar(s_rrow[:, :], s_rrow[:, :],
                                               1e-5, None, ALU.add), "cv")
                    S.wait(vector, "cv", S.cnt("cv"))
                    S.inc(vector.tensor_copy(s_t1[0:1, :], s_rrow[:, :]), "cv")
                block.vector(_v2)
                cv_eps = S.cnt("cv")

                def _s(scalar):
                    S.wait(scalar, "cv", cv_eps)
                    S.inc(scalar.activation(s_rrow[:, :], s_rrow[:, :],
                                            AF.Ln), "cs")
                    S.wait(scalar, "cs", S.cnt("cs"))
                    S.inc(scalar.mul(s_rrow[:, :], s_rrow[:, :], -0.5), "cs")
                    S.wait(scalar, "cs", S.cnt("cs"))
                    S.inc(scalar.activation(s_rrow[:, :], s_rrow[:, :],
                                            AF.Exp), "cs")
                block.scalar(_s)
                la_mr = S.cnt("cs")

                cs_mb0 = S.cnt("cs")

                def _t3(tensor):
                    S.wait(tensor, "cs", cs_mb0)
                    for w in range(4):
                        if w >= 2:
                            S.wait(tensor, "cs", cs_mb0 + w - 1)
                        src = s_mrow if w < 2 else s_rrow
                        q = w % 2
                        ps = ps1 if w % 2 == 0 else ps2
                        S.inc(tensor.matmul(
                            ps[:, :], s_onesr[0:1, :],
                            src[0:1, 512 * q:512 * (q + 1)], start=True,
                            stop=True), "ct")
                block.tensor(_t3)
                ct_mb0 = S.cnt("ct") - 4

                def _s3(scalar):
                    for w in range(4):
                        S.wait(scalar, "ct", ct_mb0 + w + 1)
                        dst = s_meanb if w < 2 else s_rstdb
                        q = w % 2
                        ps = ps1 if w % 2 == 0 else ps2
                        S.inc(scalar.copy(dst[:, 512 * q:512 * (q + 1)],
                                          ps[:, :]), "cs")
                block.scalar(_s3)
                lg_mrb = S.cnt("cs")

                if "st" in taps:
                    def _gtap2(gpsimd):
                        S.wait(gpsimd, "cs", lg_mrb)
                        S.inc(gpsimd.dma_start(out=tap_aps["t_mr"][0:1, :],
                                               in_=s_mrow[:, :]), "lg", 16)
                        S.inc(gpsimd.dma_start(out=tap_aps["t_mr"][1:2, :],
                                               in_=s_rrow[:, :]), "lg", 16)
                        S.inc(gpsimd.dma_start(out=tap_aps["t_mb"],
                                               in_=s_meanb[:, :]), "lg", 16)
                        S.inc(gpsimd.dma_start(out=tap_aps["t_rb"],
                                               in_=s_rstdb[:, :]), "lg", 16)
                    block.gpsimd(_gtap2)
                    lg_tap2 = S.cnt("lg")

                    def _vtap2(vector):
                        S.wait(vector, "lg", lg_tap2)
                    block.vector(_vtap2)

                    def _stap2(scalar):
                        S.wait(scalar, "lg", lg_tap2)
                    block.scalar(_stap2)

                def _t(tensor):
                    S.wait(tensor, "cs", S.cnt("cs"))
                    for tt in range(8):
                        S.inc(tensor.transpose(
                            ps3[:, tt:tt + 1],
                            s_mrow[0:1, 128 * tt:128 * (tt + 1)],
                            s_ident[0:1, 0:1]), "ct")
                    for tt in range(8):
                        S.inc(tensor.transpose(
                            ps4[:, tt:tt + 1],
                            s_rrow[0:1, 128 * tt:128 * (tt + 1)],
                            s_ident[0:1, 0:1]), "ct")
                block.tensor(_t)
                ct_cols = S.cnt("ct")

                def _s(scalar):
                    S.wait(scalar, "ct", ct_cols)
                    S.inc(scalar.copy(s_nmc[:, :], ps3[:, 0:8]), "cs")
                    S.inc(scalar.copy(s_rsc[:, :], ps4[:, 0:8]), "cs")
                block.scalar(_s)
                cs_cols = S.cnt("cs")

                def _v(vector):
                    S.wait(vector, "cs", cs_cols)
                    S.inc(vector.tensor_scalar(s_nmc[:, :], s_nmc[:, :], -1.0,
                                               None, ALU.mult), "cv")
                block.vector(_v)
                cv_nmc = S.cnt("cv")

                def _v(vector):
                    S.wait(vector, "lg", lg_xy)
                    S.wait(vector, "cs", lg_mrb)
                    for j in range(16):
                        S.inc(vector.tensor_sub(s_t1[:, :], s_y1[j][:, :],
                                                s_meanb[:, :]), "cv")
                        S.wait(vector, "cv", S.cnt("cv"))
                        S.inc(vector.tensor_mul(s_y1[j][:, :], s_t1[:, :],
                                                s_rstdb[:, :]), "cv")
                block.vector(_v)
                cv_ln1 = S.cnt("cv")

                if "y1" in taps:
                    def _g(gpsimd):
                        S.wait(gpsimd, "cv", cv_ln1)
                        for j in range(16):
                            S.inc(gpsimd.dma_start(
                                out=tap_aps["t_y1"][128 * j:128 * (j + 1), :],
                                in_=s_y1[j][:, :]), "lg", 16)
                    block.gpsimd(_g)

                for tt in range(8):
                    def _t(tensor, tt=tt):
                        S.wait(tensor, "lg", lg_st)
                        S.wait(tensor, "cs", S.cnt("cs"))
                        S.inc(tensor.transpose(
                            ps1[:, 0:8], s_raw8[:, 128 * tt:128 * (tt + 1)],
                            s_ident[0:8, 0:8]), "ct")
                    block.tensor(_t)
                    ct_l = S.cnt("ct")

                    def _s(scalar, ct_l=ct_l):
                        S.wait(scalar, "ct", ct_l)
                        S.wait(scalar, "cv", S.cnt("cv"))
                        S.inc(scalar.copy(s_lg[:, :], ps1[:, 0:8]), "cs")
                    block.scalar(_s)
                    cs_l = S.cnt("cs")

                    def _v(vector, tt=tt, cs_l=cs_l):
                        S.wait(vector, "cs", cs_l)
                        S.wait(vector, "cs", lg_rsb)
                        S.wait(vector, "cv", cv_nmc)
                        S.inc(vector.scalar_tensor_tensor(
                            s_lg[:, :], s_rsumb[:, :], s_nmc[:, tt:tt + 1],
                            s_lg[:, :], ALU.mult, ALU.add), "cv")
                        S.wait(vector, "cv", S.cnt("cv"))
                        S.inc(vector.tensor_scalar(s_lg[:, :], s_lg[:, :],
                                                   s_rsc[:, tt:tt + 1], None,
                                                   ALU.mult), "cv")
                        S.wait(vector, "cv", S.cnt("cv"))
                        S.inc(vector.max(s_srt[:, :], s_lg[:, :]), "cv")
                        S.wait(vector, "cv", S.cnt("cv"))
                        S.inc(vector.max_index(s_idxs[:, :], s_srt[:, :],
                                               s_lg[:, :]), "cv")
                        S.wait(vector, "cv", S.cnt("cv"))
                        S.inc(vector.tensor_scalar(s_tmpu[:, 0:2],
                                                   s_idxs[:, 0:2], EXPBITS,
                                                   None, ALU.bitwise_or), "cv")
                        S.wait(vector, "cv", S.cnt("cv"))
                        S.inc(vector.tensor_scalar(s_ekl[:, :],
                                                   s_tmpu[:, 0:2].bitcast(F32),
                                                   EXP23, None, ALU.subtract),
                              "cv")
                        S.inc(vector.tensor_sub(s_c1[:, :], s_srt[:, 1:2],
                                                s_srt[:, 0:1]), "cv")
                        S.wait(vector, "cv", S.cnt("cv"))
                        S.inc(vector.tensor_copy(s_c3[:, :], s_c1[:, :]), "cv")
                    block.vector(_v)
                    cv_g = S.cnt("cv")

                    def _s(scalar, cv_g=cv_g):
                        S.wait(scalar, "cv", cv_g)
                        S.inc(scalar.activation(s_c2[:, :], s_c1[:, :], AF.Exp),
                              "cs")
                    block.scalar(_s)
                    cs_eg = S.cnt("cs")

                    def _v(vector, tt=tt, cs_eg=cs_eg):
                        S.wait(vector, "cs", cs_eg)
                        S.inc(vector.tensor_scalar(s_c3[:, :], s_c2[:, :], 1.0,
                                                   None, ALU.add), "cv")
                        S.wait(vector, "cv", S.cnt("cv"))
                        S.inc(vector.reciprocal(s_w1c[:, tt:tt + 1],
                                                s_c3[:, :]), "cv")
                        S.wait(vector, "cv", S.cnt("cv"))
                        S.inc(vector.tensor_scalar(s_w2c[:, tt:tt + 1],
                                                   s_w1c[:, tt:tt + 1], -1.0,
                                                   1.0, ALU.mult, ALU.add), "cv")
                        S.wait(vector, "cv", S.cnt("cv"))
                    block.vector(_v)
                    cv_w = S.cnt("cv")

                    def _vd(vector, cv_w=cv_w):
                        S.wait(vector, "cv", cv_w)
                        S.inc(vector.tensor_copy(s_c3[:, :], s_ekl[:, 0:1]),
                              "cv")
                        S.inc(vector.tensor_copy(s_c3[:, :], s_ekl[:, 1:2]),
                              "cv")
                    block.vector(_vd)
                    cv_wd = S.cnt("cv")

                    def _s(scalar, tt=tt, cv_wd=cv_wd):
                        S.wait(scalar, "cv", cv_wd)
                        S.inc(scalar.dma_start(
                            out=d_sel[128 * tt:128 * (tt + 1), :],
                            in_=s_ekl[:, :]), "la", 16)
                    block.scalar(_s)
                la_sel = S.cnt("la")

                def _g(gpsimd):
                    S.wait(gpsimd, "la", la_sel)
                    flat = d_sel.ap().rearrange("t k -> (t k)").unsqueeze(0)
                    S.inc(gpsimd.dma_start(out=s_ef[:, :], in_=flat), "lg", 16)
                block.gpsimd(_g)
                lg_ef = S.cnt("lg")

                cs_eb0 = S.cnt("cs")

                def _t4(tensor):
                    S.wait(tensor, "lg", lg_ef)
                    S.wait(tensor, "cs", cs_eb0)
                    for q in range(4):
                        if q >= 2:
                            S.wait(tensor, "cs", cs_eb0 + q - 1)
                        ps = ps1 if q % 2 == 0 else ps2
                        S.inc(tensor.matmul(
                            ps[0:8, :], s_onesr[0:1, 0:8],
                            s_ef[0:1, 512 * q:512 * (q + 1)], start=True,
                            stop=True), "ct")
                block.tensor(_t4)
                ct_eb0 = S.cnt("ct") - 4

                def _s4(scalar):
                    for q in range(4):
                        S.wait(scalar, "ct", ct_eb0 + q + 1)
                        ps = ps1 if q % 2 == 0 else ps2
                        S.inc(scalar.copy(s_eb8[:, 512 * q:512 * (q + 1)],
                                          ps[0:8, :]), "cs")
                block.scalar(_s4)
                cs_eb = S.cnt("cs")

                def _v(vector):
                    S.wait(vector, "lg", lg_ef)
                    S.wait(vector, "cs", cs_eb)
                    S.inc(vector.memset(s_z8[:, :], 0.0), "cv")
                    S.wait(vector, "cv", S.cnt("cv"))
                    S.inc(vector.tensor_scalar(s_ind8[:, :], s_eb8[:, :],
                                               s_io8c[:, 0:1], None,
                                               ALU.is_equal), "cv")
                    S.wait(vector, "cv", S.cnt("cv"))
                    S.inc(vector.tensor_tensor_scan(
                        s_cum8[:, :], s_ind8[:, :], s_z8[:, :], 0.0,
                        ALU.add, ALU.add), "cv")
                    S.wait(vector, "cv", S.cnt("cv"))
                    S.inc(vector.tensor_copy(
                        s_mycnt[:, :], s_cum8[:, 2 * TOK - 1:2 * TOK]), "cv")
                    S.wait(vector, "cv", S.cnt("cv"))
                    S.inc(vector.tensor_scalar(s_cum8[:, :], s_cum8[:, :], 1.0,
                                               None, ALU.subtract), "cv")
                    S.wait(vector, "cv", S.cnt("cv"))
                    S.inc(vector.tensor_mul(s_cum8[:, :], s_cum8[:, :],
                                            s_ind8[:, :]), "cv")
                    S.wait(vector, "cv", S.cnt("cv"))
                    S.inc(vector.tensor_copy(s_gb8[:, :], s_mycnt[:, :]), "cv")
                block.vector(_v)
                cv_cum = S.cnt("cv")

                cs_base = S.cnt("cs")

                def _t(tensor):
                    S.wait(tensor, "cv", cv_cum)
                    for q in range(4):
                        if q > 0:
                            S.wait(tensor, "cs", cs_base + q)
                        S.inc(tensor.matmul(ps1[0:1, 0:512], s_ones[0:8, 0:1],
                                            s_cum8[:, 512 * q:512 * (q + 1)],
                                            start=True, stop=True), "ct")
                block.tensor(_t)
                ct_lp0 = S.cnt("ct") - 4

                def _s(scalar):
                    for q in range(4):
                        S.wait(scalar, "ct", ct_lp0 + q + 1)
                        S.inc(scalar.copy(s_lpos[0:1, 512 * q:512 * (q + 1)],
                                          ps1[0:1, 0:512]), "cs")
                block.scalar(_s)
                cs_lp = S.cnt("cs")

                def _t(tensor):
                    S.wait(tensor, "cv", cv_cum)
                    S.wait(tensor, "cs", S.cnt("cs"))
                    S.inc(tensor.transpose(ps2[0:1, 0:8], s_mycnt[:, 0:1],
                                           s_ident[0:8, 0:8]), "ct")
                block.tensor(_t)
                ct_cnt = S.cnt("ct")

                def _s(scalar):
                    S.wait(scalar, "ct", ct_cnt)
                    S.inc(scalar.copy(s_cntr[0:1, 0:8], ps2[0:1, 0:8]), "cs")
                    S.wait(scalar, "cs", S.cnt("cs"))
                    S.inc(scalar.dma_start(out=d_cnt_in[0:1, :],
                                           in_=s_cntr[0:1, 0:8]), "la", 16)
                block.scalar(_s)
                la_cnt = S.cnt("la")

                def _g(gpsimd):
                    S.wait(gpsimd, "la", la_cnt)
                    S.inc(gpsimd.collective_compute(
                        "AllGather", ALU.bypass, replica_groups=RG,
                        ins=[d_cnt_in.ap().opt()], outs=[d_cnt.ap().opt()]),
                        "cc", 1)
                    S.wait(gpsimd, "cc", S.cnt("cc"))
                    S.inc(gpsimd.dma_start(out=s_c88[:, :], in_=d_cnt[:, :]),
                          "lg", 16)
                    S.inc(gpsimd.dma_start(out=s_rohr[:, :], in_=i_roh),
                          "lg", 16)
                block.gpsimd(_g)
                lg_c88 = S.cnt("lg")

                def _t5(tensor):
                    S.wait(tensor, "lg", lg_c88)
                    S.wait(tensor, "cs", S.cnt("cs"))
                    S.inc(tensor.matmul(ps3[0:8, 0:8], s_onesr[0:1, 0:8],
                                        s_rohr[0:1, :], start=True,
                                        stop=True), "ct")
                block.tensor(_t5)
                ct_rohb = S.cnt("ct")

                def _s5(scalar):
                    S.wait(scalar, "ct", ct_rohb)
                    S.inc(scalar.copy(s_rohb[:, :], ps3[0:8, 0:8]), "cs")
                block.scalar(_s5)
                cs_rohb = S.cnt("cs")

                def _t(tensor):
                    S.wait(tensor, "lg", lg_c88)
                    S.wait(tensor, "cs", S.cnt("cs"))
                    S.inc(tensor.transpose(ps2[0:8, 0:8], s_c88[:, :],
                                           s_ident[0:8, 0:8]), "ct")
                block.tensor(_t)
                ct_c88 = S.cnt("ct")

                def _s(scalar):
                    S.wait(scalar, "ct", ct_c88)
                    S.inc(scalar.copy(s_cT[:, :], ps2[0:8, 0:8]), "cs")
                block.scalar(_s)
                cs_c88 = S.cnt("cs")

                def _v(vector):
                    S.wait(vector, "cs", max(cs_c88, cs_rohb))
                    S.inc(vector.tensor_tensor_scan(
                        s_cumr[:, :], s_cT[:, :], s_z8[:, 0:8], 0.0,
                        ALU.add, ALU.add), "cv")
                    S.wait(vector, "cv", S.cnt("cv"))
                    S.inc(vector.tensor_mul(s_cumr[:, :], s_cumr[:, :],
                                            s_rohb[:, :]), "cv")
                    S.wait(vector, "cv", S.cnt("cv"))
                    S.inc(vector.reduce_sum(s_incl[:, :], s_cumr[:, :],
                                            axis=AX.X), "cv")
                    S.wait(vector, "cv", S.cnt("cv"))
                    S.inc(vector.tensor_sub(s_gb8[:, :], s_incl[:, :],
                                            s_mycnt[:, :]), "cv")
                    S.wait(vector, "cv", S.cnt("cv"))
                    S.inc(vector.tensor_scalar(s_ind8[:, :], s_ind8[:, :],
                                               s_gb8[:, 0:1], None, ALU.mult),
                          "cv")
                    S.wait(vector, "cv", S.cnt("cv"))
                    S.inc(vector.tensor_copy(s_incl[:, :], s_gb8[:, :]), "cv")
                block.vector(_v)
                cv_gb = S.cnt("cv")

                cs_base2 = S.cnt("cs")

                def _t(tensor):
                    S.wait(tensor, "cv", cv_gb)
                    for q in range(4):
                        if q > 0:
                            S.wait(tensor, "cs", cs_base2 + q)
                        S.inc(tensor.matmul(ps1[0:1, 0:512], s_ones[0:8, 0:1],
                                            s_ind8[:, 512 * q:512 * (q + 1)],
                                            start=True, stop=True), "ct")
                block.tensor(_t)
                ct_gb0 = S.cnt("ct") - 4

                def _s(scalar):
                    for q in range(4):
                        S.wait(scalar, "ct", ct_gb0 + q + 1)
                        S.inc(scalar.copy(s_gbrow[0:1, 512 * q:512 * (q + 1)],
                                          ps1[0:1, 0:512]), "cs")
                block.scalar(_s)
                cs_gb = S.cnt("cs")

                def _v(vector):
                    S.wait(vector, "cs", max(cs_lp, cs_gb))
                    S.inc(vector.tensor_add(s_gbrow[:, :], s_gbrow[:, :],
                                            s_lpos[:, :]), "cv")
                    S.wait(vector, "cv", S.cnt("cv"))
                    S.inc(vector.tensor_scalar(s_ml[:, :], s_lpos[:, :],
                                               float(CLOC), None, ALU.is_lt),
                          "cv")
                    S.wait(vector, "cv", S.cnt("cv"))
                    S.inc(vector.tensor_scalar(s_mg[:, :], s_gbrow[:, :],
                                               float(CAP), None, ALU.is_lt),
                          "cv")
                    S.wait(vector, "cv", S.cnt("cv"))
                    S.inc(vector.tensor_mul(s_mg[:, :], s_mg[:, :],
                                            s_ml[:, :]), "cv")
                    S.wait(vector, "cv", S.cnt("cv"))
                    S.inc(vector.tensor_scalar(s_offr[:, :], s_ef[:, :],
                                               float(CLOC), None, ALU.mult),
                          "cv")
                    S.wait(vector, "cv", S.cnt("cv"))
                    S.inc(vector.tensor_add(s_offr[:, :], s_offr[:, :],
                                            s_lpos[:, :]), "cv")
                    S.wait(vector, "cv", S.cnt("cv"))
                    S.inc(vector.tensor_scalar(s_dmpr[:, :], s_offr[:, :],
                                               float(DUMP), None,
                                               ALU.subtract), "cv")
                    S.wait(vector, "cv", S.cnt("cv"))
                    S.inc(vector.tensor_mul(s_sofr[:, :], s_ml[:, :],
                                            s_dmpr[:, :]), "cv")
                    S.inc(vector.tensor_mul(s_gofr[:, :], s_mg[:, :],
                                            s_dmpr[:, :]), "cv")
                    S.wait(vector, "cv", S.cnt("cv"))
                    S.inc(vector.tensor_scalar(s_sofr[:, :], s_sofr[:, :],
                                               float(DUMP), None, ALU.add),
                          "cv")
                    S.inc(vector.tensor_scalar(s_gofr[:, :], s_gofr[:, :],
                                               float(DUMP), None, ALU.add),
                          "cv")
                    S.wait(vector, "cv", S.cnt("cv"))
                    S.inc(vector.tensor_copy(s_dmpr[:, :], s_sofr[:, :]), "cv")
                    S.inc(vector.tensor_copy(s_dmpr[:, :], s_gofr[:, :]), "cv")
                block.vector(_v)
                cv_offs = S.cnt("cv")

                if "route" in taps:
                    def _s(scalar):
                        S.wait(scalar, "cv", cv_offs)
                        for ri, row in enumerate((s_sofr, s_gofr, s_ef, s_lpos,
                                                  s_gbrow, s_ml, s_mg)):
                            S.inc(scalar.dma_start(
                                out=tap_aps["t_route"][ri:ri + 1, :],
                                in_=row[:, :]), "la", 16)
                    block.scalar(_s)

                for src_row, dst in ((s_sofr, s_soffc), (s_gofr, s_goffc)):
                    def _t(tensor, src_row=src_row):
                        S.wait(tensor, "cv", cv_offs)
                        S.wait(tensor, "cs", S.cnt("cs"))
                        v = src_row[0:1, :].rearrange("p (t k) -> p k t", k=2)
                        for col in range(16):
                            tt, k = col // 2, col % 2
                            S.inc(tensor.transpose(
                                ps2[:, col:col + 1],
                                v[:, k, 128 * tt:128 * (tt + 1)],
                                s_ident[0:1, 0:1]), "ct")
                    block.tensor(_t)
                    ct_o = S.cnt("ct")

                    def _s(scalar, ct_o=ct_o):
                        S.wait(scalar, "ct", ct_o)
                        S.inc(scalar.copy(s_sm[:, :], ps2[:, 0:16]), "cs")
                    block.scalar(_s)
                    cs_o = S.cnt("cs")

                    def _v(vector, dst=dst, cs_o=cs_o):
                        S.wait(vector, "cs", cs_o)
                        S.inc(vector.tensor_scalar(dst[:, :].bitcast(F32),
                                                   s_sm[:, :], EXP23, None,
                                                   ALU.add), "cv")
                        S.wait(vector, "cv", S.cnt("cv"))
                        S.inc(vector.tensor_scalar(dst[:, :], dst[:, :], MANT,
                                                   None, ALU.bitwise_and), "cv")
                        S.wait(vector, "cv", S.cnt("cv"))
                        S.inc(vector.tensor_copy(s_sm[:, :],
                                                 dst[:, :].bitcast(F32)), "cv")
                    block.vector(_v)
                cv_offc = S.cnt("cv")

                if "route_only" in stages:
                    def _out(gpsimd):
                        S.wait(gpsimd, "cv", cv_offc)
                        for tt in range(NTOKT):
                            S.inc(gpsimd.dma_start(
                                out=o_out[128 * tt:128 * (tt + 1), :],
                                in_=d_xy.ap().rearrange(
                                    "(d r) t -> d (r t)", r=8)[0:128, 0:DD]),
                                  "st", 16)
                        S.wait(gpsimd, "st", S.cnt("st"))
                    block.gpsimd(_out)
                    return

            lg_zero_done = S.cnt("lg")
            for tt in range(NTOKT):
                for half in range(2):
                    def _t(tensor, tt=tt, half=half):
                        S.wait(tensor, "cv", cv_offc)
                        S.wait(tensor, "cs", S.cnt("cs"))
                        for j in range(8):
                            dt = 8 * half + j
                            S.inc(tensor.transpose(
                                pst[:, 128 * j:128 * (j + 1)],
                                s_y1[dt][:, 128 * tt:128 * (tt + 1)],
                                s_identb[:, :]), "ct")
                    block.tensor(_t)
                    ct_y1t = S.cnt("ct")

                    def _s(scalar, half=half, ct_y1t=ct_y1t):
                        S.wait(scalar, "ct", ct_y1t)
                        S.wait(scalar, "lg", S.cnt("lg"))
                        S.inc(scalar.copy(
                            s_y1tst[:, 1024 * half:1024 * (half + 1)],
                            pst[:, :]), "cs")
                    block.scalar(_s)
                cs_y1t = S.cnt("cs")

                def _g(gpsimd, tt=tt, cs_y1t=cs_y1t):
                    S.wait(gpsimd, "cs", cs_y1t)
                    S.inc(gpsimd.dma_start(
                        out=d_y1t[128 * tt:128 * (tt + 1), :],
                        in_=s_y1tst[:, :]), "lg", 16)
                    for k in range(2):
                        S.inc(gpsimd.indirect_dma_start(
                            d_gath.ap(),
                            bass.IndirectOffsetOnAxis(
                                ap=s_soffc[:, 2 * tt + k:2 * tt + k + 1],
                                axis=0),
                            s_y1tst[:, :], None), "lg", 16)
                block.gpsimd(_g)
            lg_scat = S.cnt("lg")

            s_h = [sb(fctx, f"s_h{i}", [128, TOK], BF16) for i in range(NJT)]
            with ExitStack() as uctx:
                s_wg = [sb(uctx, f"s_wg{p}", [128, DD], BF16) for p in range(2)]
                s_wu = [sb(uctx, f"s_wu{p}", [128, DD], BF16) for p in range(2)]
                s_g2 = [sb(uctx, f"s_g2{i}", [128, 512]) for i in range(2)]
                lastcv_ps, lastcs_ps = {}, {}
                lastcs_wu = {0: 0, 1: 0}

                for p in range(NJT):
                    pb = p % 2

                    def _g(gpsimd, p=p, pb=pb):
                        S.wait(gpsimd, "cs", lastcs_wu[pb])
                        S.inc(gpsimd.dma_start(
                            out=s_wg[pb][:, :], in_=i_swug[p]), "lg", 16)
                        S.inc(gpsimd.dma_start(
                            out=s_wu[pb][:, :], in_=i_swug[NJT + p]),
                              "lg", 16)
                    block.gpsimd(_g)
                    lg_w = S.cnt("lg")

                    for c2 in range(2):
                        ub = (2 * p + c2) % 2
                        pg, pu = ((ps1, ps2), (ps3, ps4))[ub]
                        gk = "sup%d" % ub

                        def _t(tensor, pb=pb, lg_w=lg_w, c2=c2, pg=pg, pu=pu,
                               gk=gk):
                            S.wait(tensor, "lg", lg_w)
                            S.wait(tensor, "cv", lastcv_ps.get(gk, 0))
                            S.wait(tensor, "cs", lastcs_ps.get(gk, 0))
                            for j in range(16):
                                S.inc(tensor.matmul(
                                    pg[:, :], s_wg[pb][:, 128 * j:128 * (j + 1)],
                                    s_y1[j][:, 512 * c2:512 * (c2 + 1)],
                                    start=(j == 0), stop=(j == 15)), "ct")
                            for j in range(16):
                                S.inc(tensor.matmul(
                                    pu[:, :], s_wu[pb][:, 128 * j:128 * (j + 1)],
                                    s_y1[j][:, 512 * c2:512 * (c2 + 1)],
                                    start=(j == 0), stop=(j == 15)), "ct")
                        block.tensor(_t)
                        ct_up = S.cnt("ct")

                        def _s(scalar, ct_up=ct_up, pg=pg):
                            S.wait(scalar, "ct", ct_up)
                            S.wait(scalar, "cv", S.cnt("cv"))
                            S.inc(scalar.activation(s_g2[(2 * p + c2) % 2][:, :],
                                                    pg[:, :], AF.Silu), "cs")
                        block.scalar(_s)
                        cs_si = S.cnt("cs")
                        lastcs_ps[gk] = cs_si

                        def _v(vector, c2=c2, p=p, cs_si=cs_si, pu=pu):
                            S.wait(vector, "cs", cs_si)
                            S.inc(vector.tensor_mul(
                                s_h[p][:, 512 * c2:512 * (c2 + 1)],
                                s_g2[(2 * p + c2) % 2][:, :], pu[:, :]), "cv")
                        block.vector(_v)
                        lastcv_ps[gk] = S.cnt("cv")
                    lastcs_wu[pb] = S.cnt("cs")
                cv_h = S.cnt("cv")

                pass

        with ExitStack() as dctx:
            s_wdh = [sb(dctx, f"s_wdh{p}", [128, JH // 2], BF16)
                     for p in range(2)]
            s_yst = sb(dctx, "s_yst", [128, TOK], BF16)
            s_gT = [sb(dctx, f"s_gT{j}", [128, CLOC], BF16)
                    for j in range(16)]
            s_he = [[sb(dctx, f"s_he{par}_{i}", [128, CLOC], BF16)
                     for i in range(NIT)] for par in range(2)]
            s_wg2 = [sb(dctx, f"s_wg2{p}", [128, DD], BF16) for p in range(2)]
            s_wu2 = [sb(dctx, f"s_wu2{p}", [128, DD], BF16) for p in range(2)]
            s_wd2q = [[sb(dctx, f"s_wd2q{par}_{p}", [128, 512], BF16)
                       for p in range(NIT)] for par in range(2)]
            s_ge2 = [sb(dctx, "s_ge2", [128, CLOC])]
            s_ost512 = [sb(dctx, f"s_ost512{i}", [128, 512], BF16)
                        for i in range(4)]


            cs_ph0 = S.cnt("cs")
            lastcs_sdn = [cs_ph0]
            lastcs_w2 = {0: cs_ph0, 1: cs_ph0}
            lastcs_gt = [cs_ph0]
            lastcs_ps = [cs_ph0]
            lastcv_ps = [0]
            lastcs_dnev = [cs_ph0]
            cs_dn_done = {0: cs_ph0, 1: cs_ph0}
            lg_wdset = {}
            la_ostp = {i: S.cnt("la") for i in range(4)}
            ostp = [0]
            moe_state = {}
            NSET = 4 * EE

            def _wdset_load(gpsimd, si):
                e2, q2 = si // 4, si % 4
                for p in range(NIT):
                    S.inc(gpsimd.dma_start(
                        out=s_wd2q[si % 2][p][:, :],
                        in_=i_wdn[e2][128 * p:128 * (p + 1),
                                      512 * q2:512 * (q2 + 1)]), "lg", 16)
                lg_wdset[si] = S.cnt("lg")

            def _g0(gpsimd):
                for si in (0, 1):
                    _wdset_load(gpsimd, si)
            block.gpsimd(_g0)



            NJH = NJT // 2

            def emit_sdown(dt):
                for half in range(2):
                    def _l(gpsimd, dt=dt, half=half):
                        S.wait(gpsimd, "ct",
                               moe_state.get(("ctwd", half), 0))
                        S.inc(gpsimd.dma_start(
                            out=s_wdh[half][:, :],
                            in_=i_swdn[dt][:, (JH // 2) * half:
                                           (JH // 2) * (half + 1)]),
                              "lg", 16)
                    block.gpsimd(_l)
                    moe_state[("lgwd", half)] = S.cnt("lg")

                for half in range(2):
                    lg_wd = moe_state[("lgwd", half)]

                    def _t(tensor, half=half, lg_wd=lg_wd):
                        S.wait(tensor, "lg", lg_wd)
                        S.wait(tensor, "cv", cv_h)
                        if half == 0:
                            S.wait(tensor, "cs", lastcs_sdn[0])
                        for i in range(NJH):
                            ig = NJH * half + i
                            for c2 in range(2):
                                S.inc(tensor.matmul(
                                    pse[:, 512 * c2:512 * (c2 + 1)],
                                    s_wdh[half][:, 128 * i:128 * (i + 1)],
                                    s_h[ig][:, 512 * c2:512 * (c2 + 1)],
                                    start=(ig == 0),
                                    stop=(ig == NJT - 1)), "ct")
                    block.tensor(_t)
                    ct_dn = S.cnt("ct")
                    moe_state[("ctwd", half)] = ct_dn

                def _s(scalar, dt=dt, ct_dn=ct_dn):
                    S.wait(scalar, "ct", ct_dn)
                    S.wait(scalar, "la", S.cnt("la"))
                    S.inc(scalar.copy(s_yst[:, 0:512], pse[:, 0:512]), "cs")
                    S.inc(scalar.copy(s_yst[:, 512:1024],
                                      pse[:, 512:1024]), "cs")
                    S.wait(scalar, "cs", S.cnt("cs"))
                    S.inc(scalar.dma_start(
                        out=d_ysh[128 * dt:128 * (dt + 1), :],
                        in_=s_yst[:, :]), "la", 16)
                block.scalar(_s)
                lastcs_sdn[0] = S.cnt("cs")

            def emit_moe_gt(e):
                def _ls(sync, e=e):
                    S.wait(sync, "lg", lg_scat)
                    S.wait(sync, "cs", lastcs_gt[0])
                    for j in range(16):
                        S.inc(sync.dma_start(
                            out=s_gT[j][:, :],
                            in_=d_gath[CLOC * e:CLOC * (e + 1),
                                       128 * j:128 * (j + 1)],
                            transpose=True), "ls", 16)
                block.sync(_ls)
                moe_state[("lsg", e)] = S.cnt("ls")

            def emit_moe_up(e, p):
                pb = p % 2
                par = e % 2

                def _g(gpsimd, e=e, p=p, pb=pb):
                    S.wait(gpsimd, "cs", lastcs_w2[pb])
                    S.inc(gpsimd.dma_start(
                        out=s_wg2[pb][:, :], in_=i_wug[e][p]), "lg", 16)
                    S.inc(gpsimd.dma_start(
                        out=s_wu2[pb][:, :], in_=i_wug[e][NIT + p]),
                          "lg", 16)
                block.gpsimd(_g)
                lg_w2 = S.cnt("lg")
                ls_g = moe_state[("lsg", e)]

                def _t(tensor, pb=pb, lg_w2=lg_w2, ls_g=ls_g):
                    S.wait(tensor, "lg", lg_w2)
                    S.wait(tensor, "ls", ls_g)
                    S.wait(tensor, "cv", lastcv_ps[0])
                    S.wait(tensor, "cs", lastcs_ps[0])
                    for j in range(16):
                        S.inc(tensor.matmul(
                            ps1[:, 0:CLOC],
                            s_wg2[pb][:, 128 * j:128 * (j + 1)],
                            s_gT[j][:, :], start=(j == 0),
                            stop=(j == 15)), "ct")
                    for j in range(16):
                        S.inc(tensor.matmul(
                            ps2[:, 0:CLOC],
                            s_wu2[pb][:, 128 * j:128 * (j + 1)],
                            s_gT[j][:, :], start=(j == 0),
                            stop=(j == 15)), "ct")
                block.tensor(_t)
                ct_up2 = S.cnt("ct")

                def _s(scalar, ct_up2=ct_up2, pb=pb):
                    S.wait(scalar, "ct", ct_up2)
                    S.wait(scalar, "cv", S.cnt("cv"))
                    S.inc(scalar.activation(s_ge2[0][:, 0:CLOC],
                                            ps1[:, 0:CLOC], AF.Silu), "cs")
                block.scalar(_s)
                cs_si2 = S.cnt("cs")
                lastcs_ps[0] = cs_si2
                lastcs_w2[pb] = cs_si2

                def _v(vector, p=p, cs_si2=cs_si2, pb=pb, par=par):
                    S.wait(vector, "cs", cs_si2)
                    S.wait(vector, "cs", cs_dn_done[par])
                    S.inc(vector.tensor_mul(s_he[par][p][:, :],
                                            s_ge2[0][:, 0:CLOC],
                                            ps2[:, 0:CLOC]), "cv")
                block.vector(_v)
                lastcv_ps[0] = S.cnt("cv")
                if p == NIT - 1:
                    moe_state[("cvhe", e)] = S.cnt("cv")
                    lastcs_gt[0] = S.cnt("cs")

            STB = ((0, 128, "ps3"), (128, 128, "ps4"), (256, 64, "psf"))

            def emit_moe_dn(e, q):
                si = 4 * e + q
                par = si % 2
                hpar = e % 2
                lg_set = lg_wdset[si]
                cv_he = moe_state[("cvhe", e)]
                psb = {"ps3": ps3, "ps4": ps4, "psf": psf}

                def _t(tensor, e=e, par=par, hpar=hpar,
                       lg_set=lg_set, cv_he=cv_he):
                    S.wait(tensor, "lg", lg_set)
                    S.wait(tensor, "cv", cv_he)
                    S.wait(tensor, "cs", lastcs_dnev[0])
                    for p in range(NIT):
                        for (r0, nr, pn) in STB:
                            S.inc(tensor.matmul(
                                psb[pn][0:nr, :],
                                s_he[hpar][p][:, r0:r0 + nr],
                                s_wd2q[par][p][:, :],
                                start=(p == 0), stop=(p == NIT - 1)), "ct")
                block.tensor(_t)
                ct_g = S.cnt("ct")

                def _s(scalar, e=e, q=q, si=si, ct_g=ct_g):
                    S.wait(scalar, "ct", ct_g)
                    for (r0, nr, pn) in STB:
                        op = ostp[0]
                        ostp[0] = (ostp[0] + 1) % 4
                        S.wait(scalar, "la", la_ostp[op])
                        S.inc(scalar.copy(s_ost512[op][0:nr, :],
                                          psb[pn][0:nr, :]), "cs")
                        S.wait(scalar, "cs", S.cnt("cs"))
                        S.inc(scalar.dma_start(
                            out=d_outb[CLOC * e + r0:CLOC * e + r0 + nr,
                                       512 * q:512 * (q + 1)],
                            in_=s_ost512[op][0:nr, :]), "la", 16)
                        la_ostp[op] = S.cnt("la")
                block.scalar(_s)
                lastcs_dnev[0] = S.cnt("cs")
                if q == 3:
                    cs_dn_done[e % 2] = S.cnt("cs")
                if si + 2 < NSET:
                    def _gw(gpsimd, si=si, ct_g=ct_g):
                        S.wait(gpsimd, "ct", ct_g)
                        _wdset_load(gpsimd, si + 2)
                    block.gpsimd(_gw)

            moe_steps = []
            for e in range(EE):
                moe_steps.append(("gt", e))
                ups = [("up", e, p) for p in range(NIT)]
                dns = ([("dn", e - 1, q) for q in range(4)]
                       if e > 0 else [])
                mix = []
                ui = di = 0
                for slot_i in range(len(ups) + len(dns)):
                    if di < len(dns) and slot_i % 4 == 2:
                        mix.append(dns[di]); di += 1
                    elif ui < len(ups):
                        mix.append(ups[ui]); ui += 1
                    else:
                        mix.append(dns[di]); di += 1
                moe_steps.extend(mix)
            moe_steps.extend([("dn", EE - 1, q) for q in range(4)])

            k = 0

            def emit_one():
                nonlocal k
                st_ = moe_steps[k]
                if st_[0] == "gt":
                    emit_moe_gt(st_[1])
                elif st_[0] == "up":
                    emit_moe_up(st_[1], st_[2])
                else:
                    emit_moe_dn(st_[1], st_[2])
                k += 1

            per_dt = (len(moe_steps) + 15) // 16
            for dt in range(16):
                emit_sdown(dt)
                for _ in range(per_dt):
                    if k < len(moe_steps):
                        emit_one()
            while k < len(moe_steps):
                emit_one()
            la_moe = S.cnt("la")
            la_ysh = la_moe

        with ExitStack() as octx:
            s_ta = sb(octx, "s_ta", [128, DD], BF16)
            s_gA = sb(octx, "s_gA", [128, DD], BF16)
            s_gB = sb(octx, "s_gB", [128, DD], BF16)
            s_sh = sb(octx, "s_sh", [128, DD], BF16)
            s_z = sb(octx, "s_z", [128, DD])
            s_sq2 = sb(octx, "s_sq2", [128, DD])
            s_m2 = sb(octx, "s_m2", [128, 1])
            s_ss2 = sb(octx, "s_ss2", [128, 1])
            s_msq = sb(octx, "s_msq", [128, 1])
            s_v2 = sb(octx, "s_v2", [128, 1])
            s_nm2 = sb(octx, "s_nm2", [128, 1])
            s_orow = sb(octx, "s_orow", [128, DD])

            for tt in range(NTOKT):
                def _g(gpsimd, tt=tt):
                    S.wait(gpsimd, "la", la_moe)
                    S.wait(gpsimd, "cv", S.cnt("cv"))
                    S.inc(gpsimd.dma_start(
                        out=s_ta[:, :],
                        in_=d_y1t[128 * tt:128 * (tt + 1), :]), "lg", 16)
                    S.inc(gpsimd.indirect_dma_start(
                        s_gA[:, :], None, d_outb.ap(),
                        bass.IndirectOffsetOnAxis(
                            ap=s_goffc[:, 2 * tt:2 * tt + 1], axis=0)),
                        "lg", 16)
                    S.inc(gpsimd.indirect_dma_start(
                        s_gB[:, :], None, d_outb.ap(),
                        bass.IndirectOffsetOnAxis(
                            ap=s_goffc[:, 2 * tt + 1:2 * tt + 2], axis=0)),
                        "lg", 16)
                block.gpsimd(_g)
                lg_cmb = S.cnt("lg")

                def _ls(sync, tt=tt):
                    S.wait(sync, "la", la_moe)
                    S.wait(sync, "cv", S.cnt("cv"))
                    S.inc(sync.dma_start(
                        out=s_sh[:, :],
                        in_=d_ysh[0:DD, 128 * tt:128 * (tt + 1)],
                        transpose=True), "ls", 16)
                block.sync(_ls)
                ls_sh = S.cnt("ls")

                def _v(vector, tt=tt, lg_cmb=lg_cmb, ls_sh=ls_sh):
                    S.wait(vector, "lg", lg_cmb)
                    S.wait(vector, "ls", ls_sh)
                    S.wait(vector, "cs", S.cnt("cs"))
                    S.inc(vector.tensor_scalar(s_z[:, :], s_gA[:, :],
                                               s_w1c[:, tt:tt + 1], None,
                                               ALU.mult), "cv")
                    S.wait(vector, "cv", S.cnt("cv"))
                    S.inc(vector.scalar_tensor_tensor(
                        s_z[:, :], s_gB[:, :], s_w2c[:, tt:tt + 1], s_z[:, :],
                        ALU.mult, ALU.add), "cv")
                    S.wait(vector, "cv", S.cnt("cv"))
                    S.inc(vector.tensor_add(s_z[:, :], s_z[:, :],
                                            s_ta[:, :]), "cv")
                    S.wait(vector, "cv", S.cnt("cv"))
                    S.inc(vector.tensor_add(s_z[:, :], s_z[:, :],
                                            s_sh[:, :]), "cv")
                    S.wait(vector, "cv", S.cnt("cv"))
                    S.inc(vector.reduce_sum(s_m2[:, :], s_z[:, :],
                                            axis=AX.X), "cv")
                    S.wait(vector, "cv", S.cnt("cv"))
                    S.inc(vector.tensor_scalar(s_m2[:, :], s_m2[:, :],
                                               1.0 / DD, None, ALU.mult), "cv")
                block.vector(_v)
                cv_z = S.cnt("cv")

                def _s(scalar, cv_z=cv_z):
                    S.wait(scalar, "cv", cv_z)
                    S.inc(scalar.activation(s_sq2[:, :], s_z[:, :], AF.Square,
                                            accum_out=s_ss2[:, 0:1]), "cs")
                block.scalar(_s)
                cs_ss = S.cnt("cs")

                def _v(vector, cs_ss=cs_ss):
                    S.wait(vector, "cs", cs_ss)
                    S.inc(vector.tensor_mul(s_msq[:, :], s_m2[:, :],
                                            s_m2[:, :]), "cv")
                    S.wait(vector, "cv", S.cnt("cv"))
                    S.inc(vector.tensor_scalar(s_v2[:, :], s_ss2[:, :],
                                               1.0 / DD, None, ALU.mult), "cv")
                    S.wait(vector, "cv", S.cnt("cv"))
                    S.inc(vector.tensor_sub(s_v2[:, :], s_v2[:, :],
                                            s_msq[:, :]), "cv")
                block.vector(_v)
                cv_v2 = S.cnt("cv")

                def _v3(vector, cv_v2=cv_v2):
                    S.wait(vector, "cv", cv_v2)
                    S.inc(vector.tensor_scalar(s_v2[:, :], s_v2[:, :], 1e-5,
                                               None, ALU.add), "cv")
                    S.wait(vector, "cv", S.cnt("cv"))
                    S.inc(vector.tensor_copy(s_msq[:, :], s_v2[:, :]), "cv")
                block.vector(_v3)
                cv_v2b = S.cnt("cv")

                def _s(scalar, cv_v2b=cv_v2b):
                    S.wait(scalar, "cv", cv_v2b)
                    S.inc(scalar.activation(s_v2[:, :], s_v2[:, :], AF.Ln), "cs")
                    S.wait(scalar, "cs", S.cnt("cs"))
                    S.inc(scalar.mul(s_v2[:, :], s_v2[:, :], -0.5), "cs")
                    S.wait(scalar, "cs", S.cnt("cs"))
                    S.inc(scalar.activation(s_v2[:, :], s_v2[:, :], AF.Exp), "cs")
                block.scalar(_s)
                cs_r2 = S.cnt("cs")

                def _v(vector, cs_r2=cs_r2):
                    S.wait(vector, "cs", cs_r2)
                    S.inc(vector.scalar_tensor_tensor(
                        s_nm2[:, :], s_m2[:, :], -1.0, s_v2[:, :],
                        ALU.mult, ALU.mult), "cv")
                    S.wait(vector, "cv", S.cnt("cv"))
                block.vector(_v)
                cv_nm2 = S.cnt("cv")

                def _v4(vector, cv_nm2=cv_nm2):
                    S.wait(vector, "cv", cv_nm2)
                    S.inc(vector.tensor_scalar(s_orow[:, :], s_z[:, :],
                                               s_v2[:, 0:1], s_nm2[:, 0:1],
                                               ALU.mult, ALU.add), "cv")
                block.vector(_v4)
                cv_out = S.cnt("cv")

                def _s(scalar, tt=tt, cv_out=cv_out):
                    S.wait(scalar, "cv", cv_out)
                    S.inc(scalar.dma_start(
                        out=o_out[128 * tt:128 * (tt + 1), :],
                        in_=s_orow[:, :]), "st", 16)
                block.scalar(_s)

            def _fin(scalar):
                S.wait(scalar, "st", S.cnt("st"))
            block.scalar(_fin)


def _host_inputs(inputs):
    import ml_dtypes
    x = np.ascontiguousarray(inputs["L0_attn_x"], dtype=np.float32)
    alog = np.ascontiguousarray(inputs["L0_attn_A_log"], dtype=np.float32)
    dsk = np.ascontiguousarray(inputs["L0_attn_D"], dtype=np.float32)
    wx = np.ascontiguousarray(inputs["L0_attn_W_x"], dtype=np.float32)
    wdt = np.ascontiguousarray(inputs["L0_attn_W_dt"], dtype=np.float32)
    bdt = np.ascontiguousarray(inputs["L0_attn_b_dt"], dtype=np.float32)
    rout = np.ascontiguousarray(inputs["L0_ffn_router"], dtype=np.float32)
    wug = np.ascontiguousarray(inputs["L0_ffn_w_up_gate"], dtype=np.float32)
    wdn = np.ascontiguousarray(inputs["L0_ffn_w_down"], dtype=np.float32)
    swug = np.ascontiguousarray(inputs["L0_ffn_shared_w_up_gate"],
                                dtype=np.float32)
    swdn = np.ascontiguousarray(inputs["L0_ffn_shared_w_down"],
                                dtype=np.float32)

    wugP = np.ascontiguousarray(
        wug.reshape(EE, 16, 128, 2 * NIT, 128).transpose(0, 3, 2, 1, 4)
           .reshape(EE, 2 * NIT, 128, 16 * 128))
    swugP = np.ascontiguousarray(
        swug.reshape(16, 128, 2 * NJT, 128).transpose(2, 1, 0, 3)
            .reshape(2 * NJT, 128, 16 * 128))
    swdnP = np.ascontiguousarray(
        swdn.reshape(NJT, 128, 16, 128).transpose(2, 1, 0, 3)
            .reshape(16, 128, NJT * 128))

    ident = np.eye(128, dtype=np.float32)
    identb = np.eye(128, dtype=ml_dtypes.bfloat16)
    ones = np.ones((128, 1), dtype=np.float32)
    iota8c = np.arange(8, dtype=np.float32).reshape(8, 1)

    in_maps = []
    for r in range(NCORE):
        dsl = slice(DSL * r, DSL * (r + 1))
        xyrl = np.empty((128, 16), dtype=np.int32)
        for j in range(16):
            xyrl[:, j] = (128 * j + np.arange(128)) * 8 + r
        strl = (np.arange(16, dtype=np.int32) * 8 + r).reshape(16, 1)
        roh = np.zeros((1, 8), dtype=np.float32)
        roh[0, r] = 1.0
        m = {
            "x_dsl": np.ascontiguousarray(x[:, :, dsl]),
            "wx_sl": np.ascontiguousarray(wx[:, dsl]),
            "wdt_sl": np.ascontiguousarray(wdt[dsl, :]),
            "bdt_sl": np.ascontiguousarray(bdt[dsl].reshape(DSL, 1)),
            "alog_sl": np.ascontiguousarray(alog[dsl, :]),
            "d_sl": np.ascontiguousarray(dsk[dsl].reshape(DSL, 1)),
            "rout_sl": np.ascontiguousarray(rout[dsl, :]),
            "rout": rout,
            "wug": wugP, "wdn": wdn, "swug": swugP, "swdn": swdnP,
            "identf": ident, "identb": identb, "onesf": ones,
            "xyrl": xyrl,
            "strl_a": np.ascontiguousarray(strl[0:8]),
            "strl_b": np.ascontiguousarray(strl[8:10]),
            "strl_c": np.ascontiguousarray(np.array([[9 * 8 + r], [9 * 8 + r]],
                                                    dtype=np.int32)),
            "roh": roh, "iota8c": iota8c,
            "onesr": np.ones((1, 128), dtype=np.float32),
            "sel16": np.repeat(np.eye(16, dtype=np.float32), 128,
                               axis=1).reshape(16, 2048),
        }
        in_maps.append(m)
    return in_maps


def kernel(**inputs):
    nc, _ = build()
    in_maps = _host_inputs(inputs)
    res = run_bass_kernel_spmd(nc, in_maps, core_ids=list(range(NCORE)))
    out = np.empty((BB, TT_, DD), dtype=np.float32)
    for r in range(NCORE):
        b = r // 2
        t0 = (TT_ // 2) * (r % 2)
        out[b, t0:t0 + TOK, :] = np.asarray(res.results[r]["out_tok"])
    return out


# version 09 (final)
# speedup vs baseline: 1.1899x; 1.0478x over previous
import numpy as np
import concourse.bass as bass
import concourse.mybir as mybir
from concourse.bass_utils import run_bass_kernel_spmd
from contextlib import ExitStack

F32 = mybir.dt.float32
BF16 = mybir.dt.bfloat16
I32 = mybir.dt.int32
U32 = mybir.dt.uint32
AF = mybir.ActivationFunctionType
ALU = mybir.AluOpType
AX = mybir.AxisListType

BB, TT_, DD = 4, 2048, 2048
NST, RK = 16, 64
EE, IH, JH, CAP = 8, 1408, 5632, 2560
NCORE, TOK, DSL, NDT = 8, 1024, 256, 2
TC, NCH = 512, 4
CLOC = 320
NTOKT = TOK // 128
NJT = JH // 128
NIT = IH // 128
RG = [list(range(NCORE))]
EXP23 = 8388608.0
EXPBITS = 0x4B000000
MANT = 0x7FFFFF
DUMP = EE * CLOC


class Sems:
    def __init__(self):
        self.c, self.h = {}, {}

    def make(self, ctx, nc, *names):
        for n in names:
            self.h[n] = ctx.enter_context(nc.semaphore(n))
            self.c[n] = 0

    def inc(self, instr, name, n=1):
        instr.then_inc(self.h[name], n)
        self.c[name] += n
        return instr

    def cnt(self, name):
        return self.c[name]

    def wait(self, eng, name, val=None):
        eng.wait_ge(self.h[name], self.c[name] if val is None else val)


def build(stages=("mamba", "ffn"), taps=()):
    nc = bass.Bass("TRN2", target_bir_lowering=False,
                   detect_race_conditions=False)
    do_ffn = "ffn" in stages

    def ein(name, shape, dt=F32):
        return nc.dram_tensor(name, shape, dt, kind="ExternalInput").ap()

    i_x_dsl = ein("x_dsl", [BB, TT_, DSL])
    i_wx_sl = ein("wx_sl", [96, DSL])
    i_wdt_sl = ein("wdt_sl", [DSL, RK])
    i_bdt_sl = ein("bdt_sl", [DSL, 1])
    i_alog_sl = ein("alog_sl", [DSL, NST])
    i_d_sl = ein("d_sl", [DSL, 1])
    i_rout_sl = ein("rout_sl", [DSL, EE])
    i_rout = ein("rout", [DD, EE])
    i_wug = ein("wug", [EE, 2 * NIT, 128, 16 * 128])
    i_wdn = ein("wdn", [EE, IH, DD])
    i_swug = ein("swug", [2 * NJT, 128, 16 * 128])
    i_swdn = ein("swdn", [16, 128, NJT * 128])
    i_ident = ein("identf", [128, 128])
    i_identb = ein("identb", [128, 128], BF16)
    i_ones = ein("onesf", [128, 1])
    i_xyrl = ein("xyrl", [128, 16], I32)
    i_strl_a = ein("strl_a", [8, 1], I32)
    i_strl_b = ein("strl_b", [2, 1], I32)
    i_strl_c = ein("strl_c", [2, 1], I32)
    i_roh = ein("roh", [1, 8])
    i_iota8c = ein("iota8c", [8, 1])
    i_onesr = ein("onesr", [1, 128])
    i_sel16 = ein("sel16", [16, 2048])

    o_out = nc.dram_tensor("out_tok", [TOK, DD], F32, kind="ExternalOutput").ap()
    tap_aps = {}
    if "xy" in taps:
        tap_aps["t_xy"] = nc.dram_tensor("t_xy", [DSL, BB * TT_], F32,
                                         kind="ExternalOutput").ap()
    if "y1" in taps:
        tap_aps["t_y1"] = nc.dram_tensor("t_y1", [DD, TOK], BF16,
                                         kind="ExternalOutput").ap()
    if "route" in taps:
        tap_aps["t_route"] = nc.dram_tensor("t_route", [8, 2 * TOK], F32,
                                            kind="ExternalOutput").ap()
    if "st" in taps:
        tap_aps["t_st"] = nc.dram_tensor("t_st", [80, TOK], F32,
                                         kind="ExternalOutput").ap()
        tap_aps["t_raw8"] = nc.dram_tensor("t_raw8", [8, TOK], F32,
                                           kind="ExternalOutput").ap()
        tap_aps["t_st2"] = nc.dram_tensor("t_st2", [2, TOK], F32,
                                          kind="ExternalOutput").ap()
        tap_aps["t_xyg"] = nc.dram_tensor("t_xyg", [128, TOK], BF16,
                                          kind="ExternalOutput").ap()
        tap_aps["t_mr"] = nc.dram_tensor("t_mr", [2, TOK], F32,
                                         kind="ExternalOutput").ap()
        tap_aps["t_mb"] = nc.dram_tensor("t_mb", [128, TOK], F32,
                                         kind="ExternalOutput").ap()
        tap_aps["t_rb"] = nc.dram_tensor("t_rb", [128, TOK], F32,
                                         kind="ExternalOutput").ap()
    if "ag" in taps:
        tap_aps["t_ag"] = nc.dram_tensor("t_ag", [DD, 1024], BF16,
                                         kind="ExternalOutput").ap()
    if "dbg" in taps:
        tap_aps["t_xpr"] = nc.dram_tensor("t_xpr", [96, BB * TT_], F32,
                                          kind="ExternalOutput").ap()
        tap_aps["t_delta"] = nc.dram_tensor("t_delta", [128, TC], F32,
                                            kind="ExternalOutput").ap()
        tap_aps["t_u"] = nc.dram_tensor("t_u", [128, TC], F32,
                                        kind="ExternalOutput").ap()
        tap_aps["t_bx"] = nc.dram_tensor("t_bx", [128, 8 * TC], F32,
                                         kind="ExternalOutput").ap()
        tap_aps["t_hh"] = nc.dram_tensor("t_hh", [128, 8 * TC], F32,
                                         kind="ExternalOutput").ap()

    d_xp_in = nc.dram_tensor("xp_in", [96, BB * TT_], F32)
    d_xpr = nc.dram_tensor("xpr_out", [96, BB * TT_], F32)
    d_st_in = nc.dram_tensor("st_in", [10, BB * TT_], F32)
    d_st = nc.dram_tensor("st_out", [10 * NCORE, TOK], F32)
    d_xy_in = nc.dram_tensor("xy_in", [DSL, BB * TT_], BF16)
    d_xy = nc.dram_tensor("xy_out", [DD * NCORE, TOK], BF16)
    d_cnt_in = nc.dram_tensor("cnt_in", [1, 8], F32)
    d_cnt = nc.dram_tensor("cnt_out", [8, 8], F32)
    d_sel = nc.dram_tensor("sel_d", [TOK, 2], F32)
    d_rows = nc.dram_tensor("rows_d", [8, 2048], F32)
    d_y1t = nc.dram_tensor("y1t_d", [TOK, DD], BF16)
    d_gath = nc.dram_tensor("gath_d", [DUMP + 128, DD], BF16)
    d_outb = nc.dram_tensor("outb_d", [DUMP + 128, DD], BF16)
    d_ysh = nc.dram_tensor("ysh_d", [DD, TOK], BF16)

    with ExitStack() as ctx:
        S = Sems()
        S.make(ctx, nc, "ls", "lg", "la", "cc", "ct", "cs", "cv", "st")

        def sb(c, name, shape, dt=F32):
            return c.enter_context(nc.sbuf_tensor(name, shape, dt))

        s_ident = sb(ctx, "s_ident", [128, 128])
        s_identb = sb(ctx, "s_identb", [128, 128], BF16)
        s_ones = sb(ctx, "s_ones", [128, 1])
        s_io8c = sb(ctx, "s_io8c", [8, 1])
        s_onesr = sb(ctx, "s_onesr", [1, 128])


        ps1 = ctx.enter_context(nc.psum_tensor("ps1", [128, 512], F32))
        ps2 = ctx.enter_context(nc.psum_tensor("ps2", [128, 512], F32))
        ps3 = ctx.enter_context(nc.psum_tensor("ps3", [128, 512], F32))
        ps4 = ctx.enter_context(nc.psum_tensor("ps4", [128, 512], F32))
        pse = ctx.enter_context(nc.psum_tensor("pse", [128, 1024], F32))
        pst = ctx.enter_context(nc.psum_tensor("pst", [128, 1024], BF16))
        psf = ctx.enter_context(nc.psum_tensor("psf", [128, 512], F32))

        block = ctx.enter_context(nc.Block())

        keep = dict(nc=nc, block=block, S=S, sb=sb,
                    s_ident=s_ident, s_identb=s_identb, s_ones=s_ones,
                    s_onesr=s_onesr,
                    s_io8c=s_io8c, ps1=ps1, ps2=ps2, ps3=ps3, ps4=ps4,
                    pse=pse, pst=pst, psf=psf,
                    d_xy=d_xy, d_st=d_st, d_cnt_in=d_cnt_in, d_cnt=d_cnt,
                    d_sel=d_sel, d_rows=d_rows, d_y1t=d_y1t, d_gath=d_gath,
                    d_outb=d_outb, d_ysh=d_ysh, i_xyrl=i_xyrl,
                    i_strl_a=i_strl_a, i_strl_b=i_strl_b, i_strl_c=i_strl_c,
                    i_rout=i_rout, i_swug=i_swug, i_swdn=i_swdn, i_wug=i_wug,
                    i_wdn=i_wdn, i_roh=i_roh, o_out=o_out, tap_aps=tap_aps,
                    taps=taps, stages=stages)

        with ExitStack() as mctx:
            s_wxt = [sb(mctx, f"s_wxt{i}", [128, 96], BF16) for i in range(NDT)]
            s_wdtt = [sb(mctx, f"s_wdtt{i}", [128, 128], BF16) for i in range(NDT)]
            s_bdt = sb(mctx, "s_bdt", [128, NDT])
            s_negA = sb(mctx, "s_negA", [128, NST * NDT])
            s_onepD = sb(mctx, "s_onepD", [128, NDT])
            s_routsl = sb(mctx, "s_routsl", [128, EE * NDT])
            s_carry = sb(mctx, "s_carry", [128, NDT * BB * 2 * 8])
            s_wxf = sb(mctx, "s_wxf", [96, 256])
            s_wxb = sb(mctx, "s_wxb", [96, 256], BF16)
            s_wdtf = sb(mctx, "s_wdtf", [128, 128])
            s_wdtb = sb(mctx, "s_wdtb", [128, 128], BF16)
            s_xraw = [sb(mctx, f"s_xraw{i}", [128, 1024]) for i in range(2)]
            s_xdm = [sb(mctx, f"s_xdm{i}", [128, TC]) for i in range(NDT)]
            s_xbf = [sb(mctx, f"s_xbf{i}", [128, TC], BF16) for i in range(NDT)]
            s_dpf = sb(mctx, "s_dpf", [128, TC])
            s_dpb = sb(mctx, "s_dpb", [128, TC], BF16)
            s_delta = [sb(mctx, f"s_delta{i}", [128, TC]) for i in range(NDT)]
            s_expz = sb(mctx, "s_expz", [128, TC])
            s_u = [sb(mctx, f"s_u{i}", [128, TC]) for i in range(NDT)]
            s_xpre = [sb(mctx, f"s_xpre{i}", [128, TC]) for i in range(NDT)]
            s_sq = sb(mctx, "s_sq", [128, TC])
            s_Bb = [sb(mctx, f"s_Bb{i}", [128, 8 * TC]) for i in range(2)]
            s_Cb = [sb(mctx, f"s_Cb{i}", [128, 8 * TC]) for i in range(2)]
            s_ab = [sb(mctx, f"s_ab{i}", [128, 8 * TC]) for i in range(2)]
            s_bx1 = sb(mctx, "s_bx1", [128, 8 * TC])
            s_hh1 = sb(mctx, "s_hh1", [128, 8 * TC])
            s_corr = sb(mctx, "s_corr", [128, 8])
            s_xy = [sb(mctx, f"s_xy{i}", [128, TC]) for i in range(NDT)]
            s_stev = sb(mctx, "s_stev", [128, TC])
            s_bcr = sb(mctx, "s_bcr", [16, TC], BF16)
            s_sel16b = sb(mctx, "s_sel16b", [16, 2048], BF16)
            s_sel16 = sb(mctx, "s_sel16", [16, 2048])


            def _setup_sync(sync):
                S.inc(sync.dma_start(out=s_ident[:, :], in_=i_ident), "ls", 16)
                S.inc(sync.dma_start(out=s_identb[:, :], in_=i_identb), "ls", 16)
                S.inc(sync.dma_start(out=s_ones[:, :], in_=i_ones), "ls", 16)
                S.inc(sync.dma_start(out=s_io8c[:, :], in_=i_iota8c), "ls", 16)
                S.inc(sync.dma_start(out=s_onesr[:, :], in_=i_onesr), "ls", 16)
                S.inc(sync.dma_start(out=s_sel16[:, :], in_=i_sel16), "ls", 16)
                S.inc(sync.dma_start(out=s_wxf[:, :], in_=i_wx_sl), "ls", 16)
                for i in range(NDT):
                    S.inc(sync.dma_start(out=s_wdtf[:, 64 * i:64 * (i + 1)],
                                         in_=i_wdt_sl[128 * i:128 * (i + 1), :]),
                          "ls", 16)
                    S.inc(sync.dma_start(out=s_bdt[:, i:i + 1],
                                         in_=i_bdt_sl[128 * i:128 * (i + 1), :]),
                          "ls", 16)
                    S.inc(sync.dma_start(out=s_onepD[:, i:i + 1],
                                         in_=i_d_sl[128 * i:128 * (i + 1), :]),
                          "ls", 16)
                    S.inc(sync.dma_start(out=s_negA[:, NST * i:NST * (i + 1)],
                                         in_=i_alog_sl[128 * i:128 * (i + 1), :]),
                          "ls", 16)
                    S.inc(sync.dma_start(out=s_routsl[:, EE * i:EE * (i + 1)],
                                         in_=i_rout_sl[128 * i:128 * (i + 1), :]),
                          "ls", 16)
            block.sync(_setup_sync)
            ls0 = S.cnt("ls")

            def _setup_vec(vector):
                S.wait(vector, "ls", ls0)
                S.inc(vector.tensor_copy(s_wxb[:, :], s_wxf[:, :]), "cv")
                S.inc(vector.tensor_copy(s_wdtb[:, :], s_wdtf[:, :]), "cv")
                S.inc(vector.tensor_copy(s_sel16b[:, :], s_sel16[:, :]), "cv")
                S.inc(vector.memset(s_carry[:, :], 0.0), "cv")
                S.inc(vector.tensor_scalar(s_onepD[:, :], s_onepD[:, :], 1.0,
                                           None, ALU.add), "cv")
            block.vector(_setup_vec)
            cv0 = S.cnt("cv")

            def _setup_sca(scalar):
                S.wait(scalar, "ls", ls0)
                S.inc(scalar.activation(s_negA[:, :], s_negA[:, :], AF.Exp), "cs")
                S.wait(scalar, "cs", S.cnt("cs"))
                S.inc(scalar.mul(s_negA[:, :], s_negA[:, :], -1.0), "cs")
            block.scalar(_setup_sca)

            for i in range(NDT):
                def _t(tensor, i=i):
                    S.wait(tensor, "cv", cv0)
                    S.wait(tensor, "cs", S.cnt("cs"))
                    S.inc(tensor.transpose(pst[:, 0:96],
                                           s_wxb[0:96, 128 * i:128 * (i + 1)],
                                           s_identb[0:96, 0:96]), "ct")
                block.tensor(_t)

                def _s(scalar, i=i):
                    S.wait(scalar, "ct", S.cnt("ct"))
                    S.inc(scalar.copy(s_wxt[i][:, 0:96], pst[:, 0:96]), "cs")
                block.scalar(_s)
            for i in range(NDT):
                def _t(tensor, i=i):
                    S.wait(tensor, "cs", S.cnt("cs"))
                    S.inc(tensor.transpose(pst[0:64, 0:128],
                                           s_wdtb[:, 64 * i:64 * (i + 1)],
                                           s_identb[:, :]), "ct")
                block.tensor(_t)

                def _s(scalar, i=i):
                    S.wait(scalar, "ct", S.cnt("ct"))
                    S.inc(scalar.copy(s_wdtt[i][0:64, 0:128], pst[0:64, 0:128]),
                          "cs")
                block.scalar(_s)

            lastcs_xraw = {0: 0, 1: 0}
            lastct_bcr = {0: 0}
            lastcv_ab = {0: 0, 1: 0}
            lastcv_bb = {0: 0, 1: 0}
            lg_after_xy = [0]

            def load_x_chunk(b, t0, par):
                def _f(sync):
                    S.wait(sync, "cs", lastcs_xraw[par])
                    for g in range(4):
                        S.inc(sync.dma_start(
                            out=s_xraw[par][:, 256 * g:256 * (g + 1)],
                            in_=i_x_dsl[b, t0 + 128 * g:t0 + 128 * (g + 1), :]),
                            "ls", 16)
                block.sync(_f)
                return S.cnt("ls")

            def transpose_x_chunk(ls_x, par):
                def _t(tensor):
                    S.wait(tensor, "ls", ls_x)
                    S.wait(tensor, "cs", S.cnt("cs"))
                    for i in range(NDT):
                        ps = ps1 if i == 0 else ps2
                        for g in range(4):
                            S.inc(tensor.transpose(
                                ps[:, 128 * g:128 * (g + 1)],
                                s_xraw[par][:, 256 * g + 128 * i:256 * g + 128 * (i + 1)],
                                s_ident[:, :]), "ct")
                block.tensor(_t)
                ct_tp = S.cnt("ct")

                def _s(scalar):
                    S.wait(scalar, "ct", ct_tp)
                    for i in range(NDT):
                        ps = ps1 if i == 0 else ps2
                        S.inc(scalar.copy(s_xdm[i][:, :], ps[:, :]), "cs")
                block.scalar(_s)
                lastcs_xraw[par] = S.cnt("cs")
                return S.cnt("cs")

            for b in range(BB):
                for c in range(NCH):
                    t0 = c * TC
                    col0 = b * TT_ + t0
                    par = (b * NCH + c) % 2
                    ls_x = load_x_chunk(b, t0, par)
                    cs_xdm = transpose_x_chunk(ls_x, par)

                    def _v(vector):
                        S.wait(vector, "cs", cs_xdm)
                        S.wait(vector, "ct", S.cnt("ct"))
                        for i in range(NDT):
                            S.inc(vector.tensor_copy(s_xbf[i][:, :],
                                                     s_xdm[i][:, :]), "cv")
                    block.vector(_v)
                    cv_xbf = S.cnt("cv")

                    def _t(tensor):
                        S.wait(tensor, "cv", cv_xbf)
                        S.wait(tensor, "cs", S.cnt("cs"))
                        for i in range(NDT):
                            S.inc(tensor.matmul(ps3[0:96, :], s_wxt[i][:, 0:96],
                                                s_xbf[i][:, :], start=(i == 0),
                                                stop=(i == NDT - 1)), "ct")
                    block.tensor(_t)
                    ct_xp = S.cnt("ct")

                    def _s(scalar):
                        S.wait(scalar, "ct", ct_xp)
                        S.wait(scalar, "la", S.cnt("la"))
                        S.inc(scalar.copy(s_stev[0:96, :], ps3[0:96, :]), "cs")
                        S.wait(scalar, "cs", S.cnt("cs"))
                        S.inc(scalar.dma_start(out=d_xp_in[0:96, col0:col0 + TC],
                                               in_=s_stev[0:96, :]), "la", 16)
                    block.scalar(_s)

            def _cc(gpsimd):
                S.wait(gpsimd, "la", S.cnt("la"))
                S.inc(gpsimd.collective_compute(
                    "AllReduce", ALU.add, replica_groups=RG,
                    ins=[d_xp_in.ap().opt()], outs=[d_xpr.ap().opt()]), "cc", 1)
            block.gpsimd(_cc)
            cc_xp = S.cnt("cc")

            for b in range(BB):
                for c in range(NCH):
                    t0 = c * TC
                    col0 = b * TT_ + t0
                    par = (b * NCH + c) % 2
                    ls_x = load_x_chunk(b, t0, par)
                    cs_xdm = transpose_x_chunk(ls_x, par)

                    def _s(scalar):
                        S.wait(scalar, "cv", S.cnt("cv"))
                        for i in range(NDT):
                            S.inc(scalar.activation(s_xpre[i][:, :], s_xdm[i][:, :],
                                                    AF.Copy,
                                                    scale=s_onepD[:, i:i + 1]),
                                  "cs")
                    block.scalar(_s)
                    cs_xpre = S.cnt("cs")

                    def _g(gpsimd, b=b, c=c):
                        S.wait(gpsimd, "cc", cc_xp)
                        S.wait(gpsimd, "cv", S.cnt("cv"))
                        if "dbg" in taps and b == 0 and c == 0:
                            S.inc(gpsimd.dma_start(out=tap_aps["t_xpr"],
                                                   in_=d_xpr[:, :]), "lg", 16)
                        S.inc(gpsimd.dma_start(out=s_dpf[0:64, :],
                                               in_=d_xpr[0:64, col0:col0 + TC]),
                              "lg", 16)
                    block.gpsimd(_g)
                    lg_dp = S.cnt("lg")

                    def _v(vector):
                        S.wait(vector, "lg", lg_dp)
                        S.inc(vector.tensor_copy(s_dpb[0:64, :], s_dpf[0:64, :]),
                              "cv")
                    block.vector(_v)
                    cv_dpb = S.cnt("cv")

                    def _t(tensor):
                        S.wait(tensor, "cv", cv_dpb)
                        S.wait(tensor, "cs", S.cnt("cs"))
                        for i in range(NDT):
                            ps = ps3 if i == 0 else ps4
                            S.inc(tensor.matmul(ps[:, :], s_wdtt[i][0:64, :],
                                                s_dpb[0:64, :], start=True,
                                                stop=True), "ct")
                    block.tensor(_t)
                    ct_delta = S.cnt("ct")

                    def _s(scalar):
                        S.wait(scalar, "ct", ct_delta)
                        S.wait(scalar, "cv", S.cnt("cv"))
                        for i in range(NDT):
                            ps = ps3 if i == 0 else ps4
                            S.inc(scalar.activation(s_expz[:, :], ps[:, :], AF.Exp,
                                                    bias=s_bdt[:, i:i + 1]), "cs")
                            S.wait(scalar, "cs", S.cnt("cs"))
                            S.inc(scalar.activation(s_delta[i][:, :], s_expz[:, :],
                                                    AF.Ln, bias=1.0), "cs")
                    block.scalar(_s)
                    cs_delta = S.cnt("cs")

                    def _v(vector):
                        S.wait(vector, "cs", max(cs_delta, cs_xdm))
                        for i in range(NDT):
                            S.inc(vector.tensor_mul(s_u[i][:, :], s_delta[i][:, :],
                                                    s_xdm[i][:, :]), "cv")
                    block.vector(_v)
                    cv_u = S.cnt("cv")
                    if "dbg" in taps and b == 0 and c == 0:
                        def _gd(gpsimd):
                            S.wait(gpsimd, "cv", cv_u)
                            S.inc(gpsimd.dma_start(out=tap_aps["t_delta"],
                                                   in_=s_delta[0][:, :]), "lg", 16)
                            S.inc(gpsimd.dma_start(out=tap_aps["t_u"],
                                                   in_=s_u[0][:, :]), "lg", 16)
                        block.gpsimd(_gd)

                    for h in range(2):
                        hpar = (b * NCH + c) * 2 + h
                        def _g(gpsimd, h=h, hpar=hpar):
                            S.wait(gpsimd, "cs", S.cnt("cs"))
                            S.inc(gpsimd.dma_start(
                                out=s_bcr[0:8, :],
                                in_=d_xpr[64 + 8 * h:72 + 8 * h,
                                          col0:col0 + TC]), "lg", 16)
                            S.inc(gpsimd.dma_start(
                                out=s_bcr[8:16, :],
                                in_=d_xpr[80 + 8 * h:88 + 8 * h,
                                          col0:col0 + TC]), "lg", 16)
                        block.gpsimd(_g)
                        lg_bcr = S.cnt("lg")

                        cs_bc0 = S.cnt("cs")

                        def _tb(tensor, lg_bcr=lg_bcr, cs_bc0=cs_bc0,
                                hpar=hpar):
                            S.wait(tensor, "lg", lg_bcr)
                            S.wait(tensor, "cv", S.cnt("cv"))
                            S.wait(tensor, "cs", cs_bc0)
                            for w in range(16):
                                if w >= 2:
                                    S.wait(tensor, "cs", cs_bc0 + w - 1)
                                ps = (ps1, ps2)[w % 2]
                                S.inc(tensor.matmul(
                                    ps[:, :], s_sel16b[:, 128 * w:128 * (w + 1)],
                                    s_bcr[:, :], start=True, stop=True),
                                    "ct")
                        block.tensor(_tb)
                        ct_bc0 = S.cnt("ct") - 16
                        lastct_bcr[0] = S.cnt("ct")

                        def _sb2(scalar, ct_bc0=ct_bc0, hpar=hpar):
                            S.wait(scalar, "cv", lastcv_bb[hpar % 2])
                            for w in range(16):
                                S.wait(scalar, "ct", ct_bc0 + w + 1)
                                dst = s_Bb[hpar % 2] if w < 8 else s_Cb[hpar % 2]
                                n = w % 8
                                ps = (ps1, ps2)[w % 2]
                                S.inc(scalar.copy(
                                    dst[:, n * TC:(n + 1) * TC], ps[:, :]),
                                    "cs")
                        block.scalar(_sb2)
                        lg_bc = S.cnt("cs")

                        for i in range(NDT):
                            un = 2 * h + i
                            ab = s_ab[un % 2]
                            bx = s_bx1
                            hh = s_hh1
                            ccol = ((i * BB + b) * 2 + h) * 8

                            def _s(scalar, i=i, h=h, ab=ab, un=un):
                                S.wait(scalar, "cs", cs_delta)
                                S.wait(scalar, "cv", lastcv_ab[un % 2])
                                for n in range(8):
                                    S.inc(scalar.activation(
                                        ab[:, n * TC:(n + 1) * TC],
                                        s_delta[i][:, :], AF.Exp,
                                        scale=s_negA[:, NST * i + 8 * h + n:
                                                     NST * i + 8 * h + n + 1]),
                                        "cs")
                            block.scalar(_s)
                            cs_ab = S.cnt("cs")

                            def _v(vector, i=i, h=h, ab=ab, bx=bx, hh=hh,
                                   ccol=ccol, cs_ab=cs_ab, lg_bc=lg_bc, b=b,
                                   c=c, hpar=hpar):
                                S.wait(vector, "cs", max(cs_ab, lg_bc))
                                if h == 0:
                                    S.wait(vector, "lg", lg_after_xy[0])
                                for n in range(8):
                                    S.inc(vector.tensor_mul(
                                        bx[:, n * TC:(n + 1) * TC], s_u[i][:, :],
                                        s_Bb[hpar % 2][:, n * TC:(n + 1) * TC]),
                                        "cv")
                                abst = ab[:, :].rearrange("p (n t) -> p n t",
                                                          t=TC)[:, :, 0]
                                bxst = bx[:, :].rearrange("p (n t) -> p n t",
                                                          t=TC)[:, :, 0]
                                S.inc(vector.tensor_mul(
                                    s_corr[:, :], abst,
                                    s_carry[:, ccol:ccol + 8]), "cv")
                                S.wait(vector, "cv", S.cnt("cv"))
                                S.inc(vector.tensor_add(bxst, bxst,
                                                        s_corr[:, :]), "cv")
                                S.wait(vector, "cv", S.cnt("cv"))
                                S.inc(vector.memset(abst, 0.0), "cv")
                                S.wait(vector, "cv", S.cnt("cv"))
                                S.inc(vector.tensor_tensor_scan(
                                    hh[:, :], ab[:, :], bx[:, :], 0.0,
                                    ALU.mult, ALU.add), "cv")
                                hhe = hh[:, :].rearrange("p (n t) -> p n t",
                                                         t=TC)[:, :, TC - 1]
                                S.wait(vector, "cv", S.cnt("cv"))
                                S.inc(vector.tensor_copy(
                                    s_carry[:, ccol:ccol + 8], hhe), "cv")
                                S.wait(vector, "cv", S.cnt("cv"))
                                S.inc(vector.tensor_mul(hh[:, :], hh[:, :],
                                                        s_Cb[hpar % 2][:, :]),
                                      "cv")
                                S.inc(vector.tensor_add(
                                    hh[:, 0:4 * TC], hh[:, 0:4 * TC],
                                    hh[:, 4 * TC:8 * TC]), "cv")
                                S.inc(vector.tensor_add(
                                    hh[:, 0:2 * TC], hh[:, 0:2 * TC],
                                    hh[:, 2 * TC:4 * TC]), "cv")
                                S.inc(vector.tensor_add(
                                    hh[:, 0:TC], hh[:, 0:TC],
                                    hh[:, TC:2 * TC]), "cv")
                                S.wait(vector, "cs", cs_xpre)
                                if h == 0:
                                    S.inc(vector.tensor_add(
                                        s_xy[i][:, :], s_xpre[i][:, :],
                                        hh[:, 0:TC]), "cv")
                                else:
                                    S.inc(vector.tensor_add(
                                        s_xy[i][:, :], s_xy[i][:, :],
                                        hh[:, 0:TC]), "cv")
                            block.vector(_v)
                            if ("dbg" in taps and b == 0 and c == 0
                                    and h == 0 and i == 0):
                                cv_u0 = S.cnt("cv")

                                def _gd2(gpsimd):
                                    S.wait(gpsimd, "cv", cv_u0)
                                    S.inc(gpsimd.dma_start(
                                        out=tap_aps["t_bx"],
                                        in_=bx[:, :]), "lg", 16)
                                    S.inc(gpsimd.dma_start(
                                        out=tap_aps["t_hh"],
                                        in_=hh[:, :]), "lg", 16)
                                block.gpsimd(_gd2)
                                lg_d0 = S.cnt("lg")

                                def _vw(vector):
                                    S.wait(vector, "lg", lg_d0)
                                block.vector(_vw)

                                def _sw(scalar):
                                    S.wait(scalar, "lg", lg_d0)
                                block.scalar(_sw)
                    cv_xy = S.cnt("cv")

                    def _g(gpsimd, b=b, c=c):
                        S.wait(gpsimd, "cv", cv_xy)
                        for i in range(NDT):
                            S.inc(gpsimd.dma_start(
                                out=d_xy_in[128 * i:128 * (i + 1),
                                            col0:col0 + TC],
                                in_=s_xy[i][:, :]), "lg", 16)
                            if "xy" in taps:
                                S.inc(gpsimd.dma_start(
                                    out=tap_aps["t_xy"][128 * i:128 * (i + 1),
                                                        col0:col0 + TC],
                                    in_=s_xy[i][:, :]), "lg", 16)
                    block.gpsimd(_g)
                    lg_after_xy[0] = S.cnt("lg")

                    def _s(scalar):
                        S.wait(scalar, "cv", cv_xy)
                        S.inc(scalar.activation(s_sq[:, :], s_xy[0][:, :],
                                                AF.Square), "cs")
                    block.scalar(_s)
                    cs_q0 = S.cnt("cs")

                    def _t(tensor):
                        S.wait(tensor, "cv", cv_xy)
                        S.wait(tensor, "la", S.cnt("la"))
                        for i in range(NDT):
                            S.inc(tensor.matmul(psf[0:8, :],
                                                s_routsl[:, EE * i:EE * (i + 1)],
                                                s_xy[i][:, :], start=(i == 0),
                                                stop=(i == 1)), "ct")
                        for i in range(NDT):
                            S.inc(tensor.matmul(psf[32:33, :], s_ones[:, 0:1],
                                                s_xy[i][:, :], start=(i == 0),
                                                stop=(i == 1)), "ct")
                        S.wait(tensor, "cs", cs_q0)
                        S.inc(tensor.matmul(psf[64:65, :], s_ones[:, 0:1],
                                            s_sq[:, :], start=True, stop=False),
                              "ct")
                    block.tensor(_t)
                    ct_q0 = S.cnt("ct")

                    def _s(scalar):
                        S.wait(scalar, "ct", ct_q0)
                        S.inc(scalar.activation(s_sq[:, :], s_xy[1][:, :],
                                                AF.Square), "cs")
                    block.scalar(_s)
                    cs_q1 = S.cnt("cs")

                    def _t(tensor):
                        S.wait(tensor, "cs", cs_q1)
                        S.inc(tensor.matmul(psf[64:65, :], s_ones[:, 0:1],
                                            s_sq[:, :], start=False, stop=True),
                              "ct")
                    block.tensor(_t)
                    ct_stats = S.cnt("ct")

                    def _s(scalar):
                        S.wait(scalar, "ct", ct_stats)
                        S.wait(scalar, "la", S.cnt("la"))
                        S.inc(scalar.copy(s_stev[0:8, :], psf[0:8, :]), "cs")
                        S.inc(scalar.copy(s_stev[32:33, :], psf[32:33, :]), "cs")
                        S.inc(scalar.copy(s_stev[64:65, :], psf[64:65, :]), "cs")
                        S.wait(scalar, "cs", S.cnt("cs"))
                        S.inc(scalar.dma_start(out=d_st_in[0:8, col0:col0 + TC],
                                               in_=s_stev[0:8, :]), "la", 16)
                        S.inc(scalar.dma_start(out=d_st_in[8:9, col0:col0 + TC],
                                               in_=s_stev[32:33, :]), "la", 16)
                        S.inc(scalar.dma_start(out=d_st_in[9:10, col0:col0 + TC],
                                               in_=s_stev[64:65, :]), "la", 16)
                    block.scalar(_s)

            def _cc2(gpsimd):
                S.wait(gpsimd, "lg", S.cnt("lg"))
                S.wait(gpsimd, "la", S.cnt("la"))
                S.inc(gpsimd.collective_compute(
                    "AllGather", ALU.bypass, replica_groups=RG,
                    ins=[d_xy_in.ap().opt()], outs=[d_xy.ap().opt()]), "cc", 1)
                S.inc(gpsimd.collective_compute(
                    "AllReduce", ALU.add, replica_groups=RG,
                    ins=[d_st_in.ap().opt()], outs=[d_st.ap().opt()]), "cc", 1)
            block.gpsimd(_cc2)

            bar = {k: S.cnt(k) for k in ("ls", "lg", "la", "cc", "ct", "cs", "cv")}
            for sec in (block.sync, block.vector, block.scalar, block.tensor,
                        block.gpsimd):
                def _barr(eng):
                    for k, v in bar.items():
                        eng.wait_ge(S.h[k], v)
                sec(_barr)

        if not do_ffn:
            def _out(gpsimd):
                for tt in range(NTOKT):
                    S.inc(gpsimd.dma_start(
                        out=o_out[128 * tt:128 * (tt + 1), :],
                        in_=d_xy.ap().rearrange("(d r) t -> d (r t)",
                                                r=8)[0:128, 0:DD]), "st", 16)
                S.wait(gpsimd, "st", S.cnt("st"))
            block.gpsimd(_out)
            return nc, tap_aps

        _ffn(keep)
    return nc, tap_aps


def _ffn(E):
    nc, block, S, sb = E["nc"], E["block"], E["S"], E["sb"]
    s_ident, s_identb, s_ones, s_io8c, s_onesr = (E[k] for k in
        ("s_ident", "s_identb", "s_ones", "s_io8c", "s_onesr"))
    ps1, ps2, ps3, ps4, pse, pst, psf = (E[k] for k in
        ("ps1", "ps2", "ps3", "ps4", "pse", "pst", "psf"))
    d_xy, d_st, d_cnt_in, d_cnt, d_sel, d_rows = (E[k] for k in
        ("d_xy", "d_st", "d_cnt_in", "d_cnt", "d_sel", "d_rows"))
    d_y1t, d_gath, d_outb, d_ysh = (E[k] for k in
        ("d_y1t", "d_gath", "d_outb", "d_ysh"))
    i_xyrl, i_strl_a, i_strl_b, i_strl_c = (E[k] for k in
        ("i_xyrl", "i_strl_a", "i_strl_b", "i_strl_c"))
    i_rout, i_swug, i_swdn, i_wug, i_wdn, i_roh = (E[k] for k in
        ("i_rout", "i_swug", "i_swdn", "i_wug", "i_wdn", "i_roh"))
    o_out = E["o_out"]
    tap_aps, taps = E["tap_aps"], E["taps"]
    stages = E.get("stages", ())

    with ExitStack() as fctx:
        s_y1tst = sb(fctx, "s_y1tst", [128, DD], BF16)
        s_zero = sb(fctx, "s_zero", [128, 512], BF16)
        s_xyrl = sb(fctx, "s_xyrl", [128, 16], I32)
        s_sia = sb(fctx, "s_sia", [8, 1], I32)
        s_sib = sb(fctx, "s_sib", [2, 1], I32)
        s_sic = sb(fctx, "s_sic", [2, 1], I32)
        s_soffc = sb(fctx, "s_soffc", [128, 16], I32)
        s_goffc = sb(fctx, "s_goffc", [128, 16], I32)
        s_w1c = sb(fctx, "s_w1c", [128, 8])
        s_w2c = sb(fctx, "s_w2c", [128, 8])

        def _idx(sync):
            S.inc(sync.dma_start(out=s_xyrl[:, :], in_=i_xyrl), "ls", 16)
            S.inc(sync.dma_start(out=s_sia[:, :], in_=i_strl_a), "ls", 16)
            S.inc(sync.dma_start(out=s_sib[:, :], in_=i_strl_b), "ls", 16)
            S.inc(sync.dma_start(out=s_sic[:, :], in_=i_strl_c), "ls", 16)
        block.sync(_idx)
        ls_idx = S.cnt("ls")

        def _z(vector):
            S.inc(vector.memset(s_zero[:, :], 0.0), "cv")
        block.vector(_z)
        cv_zero = S.cnt("cv")

        def _zg(gpsimd):
            S.wait(gpsimd, "cv", cv_zero)
            nrow = DUMP + 128
            for r0 in range(0, nrow, 128):
                nr = min(128, nrow - r0)
                for c0 in range(0, DD, 512):
                    S.inc(gpsimd.dma_start(
                        out=d_gath[r0:r0 + nr, c0:c0 + 512],
                        in_=s_zero[0:nr, :]), "lg", 16)
                    S.inc(gpsimd.dma_start(
                        out=d_outb[r0:r0 + nr, c0:c0 + 512],
                        in_=s_zero[0:nr, :]), "lg", 16)
        block.gpsimd(_zg)

        with ExitStack() as actx:
            s_y1 = [sb(fctx, f"s_y1_{j}", [128, TOK], BF16) for j in range(16)]

            def _gxy(gpsimd):
                S.wait(gpsimd, "ls", ls_idx)
                if "ag" in taps:
                    S.inc(gpsimd.dma_start(
                        out=tap_aps["t_ag"],
                        in_=d_xy.ap().rearrange("(d r) t -> d (r t)",
                                                r=8)[:, 0:1024]), "lg", 16)
                xyv = d_xy.ap()
                for j in range(16):
                    S.inc(gpsimd.indirect_dma_start(
                        s_y1[j][:, :], None, xyv,
                        bass.IndirectOffsetOnAxis(ap=s_xyrl[:, j:j + 1], axis=0)),
                        "lg", 16)
            block.gpsimd(_gxy)
            lg_xy = S.cnt("lg")

            with ExitStack() as bctx:
                s_t1 = sb(bctx, "s_t1", [128, TOK])
                s_meanb = sb(bctx, "s_meanb", [128, TOK])
                s_rstdb = sb(bctx, "s_rstdb", [128, TOK])
                s_raw8 = sb(bctx, "s_raw8", [8, TOK])
                s_st2 = sb(bctx, "s_st2", [2, TOK])
                s_ssq2 = sb(bctx, "s_ssq2", [2, TOK])
                s_mrow = sb(bctx, "s_mrow", [1, TOK])
                s_rrow = sb(bctx, "s_rrow", [1, TOK])
                s_rsr = sb(bctx, "s_rsr", [1, 8])
                s_cntr = sb(bctx, "s_cntr", [1, 8])
                s_rt = sb(bctx, "s_rt", [128, 128])
                s_rsumb = sb(bctx, "s_rsumb", [128, 8])
                s_nmc = sb(bctx, "s_nmc", [128, 8])
                s_rsc = sb(bctx, "s_rsc", [128, 8])
                s_lg = sb(bctx, "s_lg", [128, 8])
                s_srt = sb(bctx, "s_srt", [128, 8])
                s_idxs = sb(bctx, "s_idxs", [128, 8], U32)
                s_tmpu = sb(bctx, "s_tmpu", [128, 2], U32)
                s_ekl = sb(bctx, "s_ekl", [128, 2])
                s_c1 = sb(bctx, "s_c1", [128, 1])
                s_c2 = sb(bctx, "s_c2", [128, 1])
                s_c3 = sb(bctx, "s_c3", [128, 1])
                s_eb8 = sb(bctx, "s_eb8", [8, 2 * TOK])
                s_ind8 = sb(bctx, "s_ind8", [8, 2 * TOK])
                s_cum8 = sb(bctx, "s_cum8", [8, 2 * TOK])
                s_z8 = sb(bctx, "s_z8", [8, 2 * TOK])
                s_lpos = sb(bctx, "s_lpos", [1, 2 * TOK])
                s_gbrow = sb(bctx, "s_gbrow", [1, 2 * TOK])
                s_ml = sb(bctx, "s_ml", [1, 2 * TOK])
                s_mg = sb(bctx, "s_mg", [1, 2 * TOK])
                s_offr = sb(bctx, "s_offr", [1, 2 * TOK])
                s_dmpr = sb(bctx, "s_dmpr", [1, 2 * TOK])
                s_sofr = sb(bctx, "s_sofr", [1, 2 * TOK])
                s_gofr = sb(bctx, "s_gofr", [1, 2 * TOK])
                s_ef = sb(bctx, "s_ef", [1, 2 * TOK])
                s_mycnt = sb(bctx, "s_mycnt", [8, 1])
                s_c88 = sb(bctx, "s_c88", [8, 8])
                s_cT = sb(bctx, "s_cT", [8, 8])
                s_cumr = sb(bctx, "s_cumr", [8, 8])
                s_rohb = sb(bctx, "s_rohb", [8, 8])
                s_rohr = sb(bctx, "s_rohr", [1, 8])
                s_incl = sb(bctx, "s_incl", [8, 1])
                s_gb8 = sb(bctx, "s_gb8", [8, 1])
                s_sm = sb(bctx, "s_sm", [128, 16])

                def _gst(gpsimd):
                    S.wait(gpsimd, "ls", ls_idx)
                    stv = d_st.ap()
                    S.inc(gpsimd.indirect_dma_start(
                        s_raw8[:, :], None, stv,
                        bass.IndirectOffsetOnAxis(ap=s_sia[:, 0:1], axis=0)),
                        "lg", 16)
                    S.inc(gpsimd.indirect_dma_start(
                        s_st2[:, :], None, stv,
                        bass.IndirectOffsetOnAxis(ap=s_sib[:, 0:1], axis=0)),
                        "lg", 16)
                    S.inc(gpsimd.indirect_dma_start(
                        s_ssq2[:, :], None, stv,
                        bass.IndirectOffsetOnAxis(ap=s_sic[:, 0:1], axis=0)),
                        "lg", 16)
                block.gpsimd(_gst)
                lg_st = S.cnt("lg")

                if "st" in taps:
                    def _gtap(gpsimd):
                        S.wait(gpsimd, "lg", lg_st)
                        S.wait(gpsimd, "cc", S.cnt("cc"))
                        S.inc(gpsimd.dma_start(out=tap_aps["t_st"],
                                               in_=d_st[:, :]), "lg", 16)
                        S.inc(gpsimd.dma_start(out=tap_aps["t_raw8"],
                                               in_=s_raw8[:, :]), "lg", 16)
                        S.inc(gpsimd.dma_start(out=tap_aps["t_st2"],
                                               in_=s_st2[:, :]), "lg", 16)
                        S.inc(gpsimd.dma_start(out=tap_aps["t_xyg"],
                                               in_=s_y1[0][:, :]), "lg", 16)
                    block.gpsimd(_gtap)
                    lg_tap = S.cnt("lg")

                    def _vtap(vector):
                        S.wait(vector, "lg", lg_tap)
                    block.vector(_vtap)

                def _rt(sync):
                    S.inc(sync.dma_start(
                        out=s_rt[:, :].rearrange("p (j e) -> p j e", e=8),
                        in_=i_rout.rearrange("(j p) e -> p j e", p=128)),
                        "ls", 16)
                block.sync(_rt)
                ls_rt = S.cnt("ls")

                def _t(tensor):
                    S.wait(tensor, "ls", ls_rt)
                    for j in range(16):
                        S.inc(tensor.matmul(ps3[0:8, 0:1],
                                            s_rt[:, 8 * j:8 * (j + 1)],
                                            s_ones[:, 0:1], start=(j == 0),
                                            stop=(j == 15)), "ct")
                block.tensor(_t)
                ct_rs = S.cnt("ct")

                def _s(scalar):
                    S.wait(scalar, "ct", ct_rs)
                    S.inc(scalar.copy(s_gb8[:, :], ps3[0:8, 0:1]), "cs")
                block.scalar(_s)
                cs_rs = S.cnt("cs")

                def _t(tensor):
                    S.wait(tensor, "cs", cs_rs)
                    S.inc(tensor.transpose(ps4[0:1, 0:8], s_gb8[:, 0:1],
                                           s_ident[0:8, 0:8]), "ct")
                block.tensor(_t)
                ct_rs2 = S.cnt("ct")

                def _s(scalar):
                    S.wait(scalar, "ct", ct_rs2)
                    S.inc(scalar.copy(s_rsr[0:1, 0:8], ps4[0:1, 0:8]), "cs")
                block.scalar(_s)

                def _t2(tensor):
                    S.wait(tensor, "cs", S.cnt("cs"))
                    S.inc(tensor.matmul(ps1[:, 0:8], s_onesr[0:1, :],
                                        s_rsr[0:1, 0:8], start=True,
                                        stop=True), "ct")
                block.tensor(_t2)
                ct_rsb = S.cnt("ct")

                def _s2(scalar):
                    S.wait(scalar, "ct", ct_rsb)
                    S.inc(scalar.copy(s_rsumb[:, :], ps1[:, 0:8]), "cs")
                block.scalar(_s2)
                lg_rsb = S.cnt("cs")

                def _v(vector):
                    S.wait(vector, "lg", lg_st)
                    S.inc(vector.tensor_scalar(s_mrow[:, :], s_st2[0:1, :],
                                               1.0 / DD, None, ALU.mult), "cv")
                    S.wait(vector, "cv", S.cnt("cv"))
                    S.inc(vector.tensor_scalar(s_rrow[:, :], s_ssq2[0:1, :],
                                               1.0 / DD, None, ALU.mult), "cv")
                    S.wait(vector, "cv", S.cnt("cv"))
                    S.inc(vector.tensor_mul(s_t1[0:1, :], s_mrow[:, :],
                                            s_mrow[:, :]), "cv")
                    S.wait(vector, "cv", S.cnt("cv"))
                    S.inc(vector.tensor_sub(s_rrow[:, :], s_rrow[:, :],
                                            s_t1[0:1, :]), "cv")
                block.vector(_v)
                cv_var = S.cnt("cv")

                def _v2(vector):
                    S.wait(vector, "cv", cv_var)
                    S.inc(vector.tensor_scalar(s_rrow[:, :], s_rrow[:, :],
                                               1e-5, None, ALU.add), "cv")
                    S.wait(vector, "cv", S.cnt("cv"))
                    S.inc(vector.tensor_copy(s_t1[0:1, :], s_rrow[:, :]), "cv")
                block.vector(_v2)
                cv_eps = S.cnt("cv")

                def _s(scalar):
                    S.wait(scalar, "cv", cv_eps)
                    S.inc(scalar.activation(s_rrow[:, :], s_rrow[:, :],
                                            AF.Ln), "cs")
                    S.wait(scalar, "cs", S.cnt("cs"))
                    S.inc(scalar.mul(s_rrow[:, :], s_rrow[:, :], -0.5), "cs")
                    S.wait(scalar, "cs", S.cnt("cs"))
                    S.inc(scalar.activation(s_rrow[:, :], s_rrow[:, :],
                                            AF.Exp), "cs")
                block.scalar(_s)
                la_mr = S.cnt("cs")

                cs_mb0 = S.cnt("cs")

                def _t3(tensor):
                    S.wait(tensor, "cs", cs_mb0)
                    for w in range(4):
                        if w >= 2:
                            S.wait(tensor, "cs", cs_mb0 + w - 1)
                        src = s_mrow if w < 2 else s_rrow
                        q = w % 2
                        ps = ps1 if w % 2 == 0 else ps2
                        S.inc(tensor.matmul(
                            ps[:, :], s_onesr[0:1, :],
                            src[0:1, 512 * q:512 * (q + 1)], start=True,
                            stop=True), "ct")
                block.tensor(_t3)
                ct_mb0 = S.cnt("ct") - 4

                def _s3(scalar):
                    for w in range(4):
                        S.wait(scalar, "ct", ct_mb0 + w + 1)
                        dst = s_meanb if w < 2 else s_rstdb
                        q = w % 2
                        ps = ps1 if w % 2 == 0 else ps2
                        S.inc(scalar.copy(dst[:, 512 * q:512 * (q + 1)],
                                          ps[:, :]), "cs")
                block.scalar(_s3)
                lg_mrb = S.cnt("cs")

                if "st" in taps:
                    def _gtap2(gpsimd):
                        S.wait(gpsimd, "cs", lg_mrb)
                        S.inc(gpsimd.dma_start(out=tap_aps["t_mr"][0:1, :],
                                               in_=s_mrow[:, :]), "lg", 16)
                        S.inc(gpsimd.dma_start(out=tap_aps["t_mr"][1:2, :],
                                               in_=s_rrow[:, :]), "lg", 16)
                        S.inc(gpsimd.dma_start(out=tap_aps["t_mb"],
                                               in_=s_meanb[:, :]), "lg", 16)
                        S.inc(gpsimd.dma_start(out=tap_aps["t_rb"],
                                               in_=s_rstdb[:, :]), "lg", 16)
                    block.gpsimd(_gtap2)
                    lg_tap2 = S.cnt("lg")

                    def _vtap2(vector):
                        S.wait(vector, "lg", lg_tap2)
                    block.vector(_vtap2)

                    def _stap2(scalar):
                        S.wait(scalar, "lg", lg_tap2)
                    block.scalar(_stap2)

                def _t(tensor):
                    S.wait(tensor, "cs", S.cnt("cs"))
                    for tt in range(8):
                        S.inc(tensor.transpose(
                            ps3[:, tt:tt + 1],
                            s_mrow[0:1, 128 * tt:128 * (tt + 1)],
                            s_ident[0:1, 0:1]), "ct")
                    for tt in range(8):
                        S.inc(tensor.transpose(
                            ps4[:, tt:tt + 1],
                            s_rrow[0:1, 128 * tt:128 * (tt + 1)],
                            s_ident[0:1, 0:1]), "ct")
                block.tensor(_t)
                ct_cols = S.cnt("ct")

                def _s(scalar):
                    S.wait(scalar, "ct", ct_cols)
                    S.inc(scalar.copy(s_nmc[:, :], ps3[:, 0:8]), "cs")
                    S.inc(scalar.copy(s_rsc[:, :], ps4[:, 0:8]), "cs")
                block.scalar(_s)
                cs_cols = S.cnt("cs")

                def _v(vector):
                    S.wait(vector, "cs", cs_cols)
                    S.inc(vector.tensor_scalar(s_nmc[:, :], s_nmc[:, :], -1.0,
                                               None, ALU.mult), "cv")
                block.vector(_v)
                cv_nmc = S.cnt("cv")

                def _v(vector):
                    S.wait(vector, "lg", lg_xy)
                    S.wait(vector, "cs", lg_mrb)
                    for j in range(16):
                        S.inc(vector.tensor_sub(s_t1[:, :], s_y1[j][:, :],
                                                s_meanb[:, :]), "cv")
                        S.wait(vector, "cv", S.cnt("cv"))
                        S.inc(vector.tensor_mul(s_y1[j][:, :], s_t1[:, :],
                                                s_rstdb[:, :]), "cv")
                block.vector(_v)
                cv_ln1 = S.cnt("cv")

                if "y1" in taps:
                    def _g(gpsimd):
                        S.wait(gpsimd, "cv", cv_ln1)
                        for j in range(16):
                            S.inc(gpsimd.dma_start(
                                out=tap_aps["t_y1"][128 * j:128 * (j + 1), :],
                                in_=s_y1[j][:, :]), "lg", 16)
                    block.gpsimd(_g)

                for tt in range(8):
                    def _t(tensor, tt=tt):
                        S.wait(tensor, "lg", lg_st)
                        S.wait(tensor, "cs", S.cnt("cs"))
                        S.inc(tensor.transpose(
                            ps1[:, 0:8], s_raw8[:, 128 * tt:128 * (tt + 1)],
                            s_ident[0:8, 0:8]), "ct")
                    block.tensor(_t)
                    ct_l = S.cnt("ct")

                    def _s(scalar, ct_l=ct_l):
                        S.wait(scalar, "ct", ct_l)
                        S.wait(scalar, "cv", S.cnt("cv"))
                        S.inc(scalar.copy(s_lg[:, :], ps1[:, 0:8]), "cs")
                    block.scalar(_s)
                    cs_l = S.cnt("cs")

                    def _v(vector, tt=tt, cs_l=cs_l):
                        S.wait(vector, "cs", cs_l)
                        S.wait(vector, "cs", lg_rsb)
                        S.wait(vector, "cv", cv_nmc)
                        S.inc(vector.scalar_tensor_tensor(
                            s_lg[:, :], s_rsumb[:, :], s_nmc[:, tt:tt + 1],
                            s_lg[:, :], ALU.mult, ALU.add), "cv")
                        S.wait(vector, "cv", S.cnt("cv"))
                        S.inc(vector.tensor_scalar(s_lg[:, :], s_lg[:, :],
                                                   s_rsc[:, tt:tt + 1], None,
                                                   ALU.mult), "cv")
                        S.wait(vector, "cv", S.cnt("cv"))
                        S.inc(vector.max(s_srt[:, :], s_lg[:, :]), "cv")
                        S.wait(vector, "cv", S.cnt("cv"))
                        S.inc(vector.max_index(s_idxs[:, :], s_srt[:, :],
                                               s_lg[:, :]), "cv")
                        S.wait(vector, "cv", S.cnt("cv"))
                        S.inc(vector.tensor_scalar(s_tmpu[:, 0:2],
                                                   s_idxs[:, 0:2], EXPBITS,
                                                   None, ALU.bitwise_or), "cv")
                        S.wait(vector, "cv", S.cnt("cv"))
                        S.inc(vector.tensor_scalar(s_ekl[:, :],
                                                   s_tmpu[:, 0:2].bitcast(F32),
                                                   EXP23, None, ALU.subtract),
                              "cv")
                        S.inc(vector.tensor_sub(s_c1[:, :], s_srt[:, 1:2],
                                                s_srt[:, 0:1]), "cv")
                        S.wait(vector, "cv", S.cnt("cv"))
                        S.inc(vector.tensor_copy(s_c3[:, :], s_c1[:, :]), "cv")
                    block.vector(_v)
                    cv_g = S.cnt("cv")

                    def _s(scalar, cv_g=cv_g):
                        S.wait(scalar, "cv", cv_g)
                        S.inc(scalar.activation(s_c2[:, :], s_c1[:, :], AF.Exp),
                              "cs")
                    block.scalar(_s)
                    cs_eg = S.cnt("cs")

                    def _v(vector, tt=tt, cs_eg=cs_eg):
                        S.wait(vector, "cs", cs_eg)
                        S.inc(vector.tensor_scalar(s_c3[:, :], s_c2[:, :], 1.0,
                                                   None, ALU.add), "cv")
                        S.wait(vector, "cv", S.cnt("cv"))
                        S.inc(vector.reciprocal(s_w1c[:, tt:tt + 1],
                                                s_c3[:, :]), "cv")
                        S.wait(vector, "cv", S.cnt("cv"))
                        S.inc(vector.tensor_scalar(s_w2c[:, tt:tt + 1],
                                                   s_w1c[:, tt:tt + 1], -1.0,
                                                   1.0, ALU.mult, ALU.add), "cv")
                        S.wait(vector, "cv", S.cnt("cv"))
                    block.vector(_v)
                    cv_w = S.cnt("cv")

                    def _vd(vector, cv_w=cv_w):
                        S.wait(vector, "cv", cv_w)
                        S.inc(vector.tensor_copy(s_c3[:, :], s_ekl[:, 0:1]),
                              "cv")
                        S.inc(vector.tensor_copy(s_c3[:, :], s_ekl[:, 1:2]),
                              "cv")
                    block.vector(_vd)
                    cv_wd = S.cnt("cv")

                    def _s(scalar, tt=tt, cv_wd=cv_wd):
                        S.wait(scalar, "cv", cv_wd)
                        S.inc(scalar.dma_start(
                            out=d_sel[128 * tt:128 * (tt + 1), :],
                            in_=s_ekl[:, :]), "la", 16)
                    block.scalar(_s)
                la_sel = S.cnt("la")

                def _g(gpsimd):
                    S.wait(gpsimd, "la", la_sel)
                    flat = d_sel.ap().rearrange("t k -> (t k)").unsqueeze(0)
                    S.inc(gpsimd.dma_start(out=s_ef[:, :], in_=flat), "lg", 16)
                block.gpsimd(_g)
                lg_ef = S.cnt("lg")

                cs_eb0 = S.cnt("cs")

                def _t4(tensor):
                    S.wait(tensor, "lg", lg_ef)
                    S.wait(tensor, "cs", cs_eb0)
                    for q in range(4):
                        if q >= 2:
                            S.wait(tensor, "cs", cs_eb0 + q - 1)
                        ps = ps1 if q % 2 == 0 else ps2
                        S.inc(tensor.matmul(
                            ps[0:8, :], s_onesr[0:1, 0:8],
                            s_ef[0:1, 512 * q:512 * (q + 1)], start=True,
                            stop=True), "ct")
                block.tensor(_t4)
                ct_eb0 = S.cnt("ct") - 4

                def _s4(scalar):
                    for q in range(4):
                        S.wait(scalar, "ct", ct_eb0 + q + 1)
                        ps = ps1 if q % 2 == 0 else ps2
                        S.inc(scalar.copy(s_eb8[:, 512 * q:512 * (q + 1)],
                                          ps[0:8, :]), "cs")
                block.scalar(_s4)
                cs_eb = S.cnt("cs")

                def _v(vector):
                    S.wait(vector, "lg", lg_ef)
                    S.wait(vector, "cs", cs_eb)
                    S.inc(vector.memset(s_z8[:, :], 0.0), "cv")
                    S.wait(vector, "cv", S.cnt("cv"))
                    S.inc(vector.tensor_scalar(s_ind8[:, :], s_eb8[:, :],
                                               s_io8c[:, 0:1], None,
                                               ALU.is_equal), "cv")
                    S.wait(vector, "cv", S.cnt("cv"))
                    S.inc(vector.tensor_tensor_scan(
                        s_cum8[:, :], s_ind8[:, :], s_z8[:, :], 0.0,
                        ALU.add, ALU.add), "cv")
                    S.wait(vector, "cv", S.cnt("cv"))
                    S.inc(vector.tensor_copy(
                        s_mycnt[:, :], s_cum8[:, 2 * TOK - 1:2 * TOK]), "cv")
                    S.wait(vector, "cv", S.cnt("cv"))
                    S.inc(vector.tensor_scalar(s_cum8[:, :], s_cum8[:, :], 1.0,
                                               None, ALU.subtract), "cv")
                    S.wait(vector, "cv", S.cnt("cv"))
                    S.inc(vector.tensor_mul(s_cum8[:, :], s_cum8[:, :],
                                            s_ind8[:, :]), "cv")
                    S.wait(vector, "cv", S.cnt("cv"))
                    S.inc(vector.tensor_copy(s_gb8[:, :], s_mycnt[:, :]), "cv")
                block.vector(_v)
                cv_cum = S.cnt("cv")

                cs_base = S.cnt("cs")

                def _t(tensor):
                    S.wait(tensor, "cv", cv_cum)
                    for q in range(4):
                        if q > 0:
                            S.wait(tensor, "cs", cs_base + q)
                        S.inc(tensor.matmul(ps1[0:1, 0:512], s_ones[0:8, 0:1],
                                            s_cum8[:, 512 * q:512 * (q + 1)],
                                            start=True, stop=True), "ct")
                block.tensor(_t)
                ct_lp0 = S.cnt("ct") - 4

                def _s(scalar):
                    for q in range(4):
                        S.wait(scalar, "ct", ct_lp0 + q + 1)
                        S.inc(scalar.copy(s_lpos[0:1, 512 * q:512 * (q + 1)],
                                          ps1[0:1, 0:512]), "cs")
                block.scalar(_s)
                cs_lp = S.cnt("cs")

                def _t(tensor):
                    S.wait(tensor, "cv", cv_cum)
                    S.wait(tensor, "cs", S.cnt("cs"))
                    S.inc(tensor.transpose(ps2[0:1, 0:8], s_mycnt[:, 0:1],
                                           s_ident[0:8, 0:8]), "ct")
                block.tensor(_t)
                ct_cnt = S.cnt("ct")

                def _s(scalar):
                    S.wait(scalar, "ct", ct_cnt)
                    S.inc(scalar.copy(s_cntr[0:1, 0:8], ps2[0:1, 0:8]), "cs")
                    S.wait(scalar, "cs", S.cnt("cs"))
                    S.inc(scalar.dma_start(out=d_cnt_in[0:1, :],
                                           in_=s_cntr[0:1, 0:8]), "la", 16)
                block.scalar(_s)
                la_cnt = S.cnt("la")

                def _g(gpsimd):
                    S.wait(gpsimd, "la", la_cnt)
                    S.inc(gpsimd.collective_compute(
                        "AllGather", ALU.bypass, replica_groups=RG,
                        ins=[d_cnt_in.ap().opt()], outs=[d_cnt.ap().opt()]),
                        "cc", 1)
                    S.wait(gpsimd, "cc", S.cnt("cc"))
                    S.inc(gpsimd.dma_start(out=s_c88[:, :], in_=d_cnt[:, :]),
                          "lg", 16)
                    S.inc(gpsimd.dma_start(out=s_rohr[:, :], in_=i_roh),
                          "lg", 16)
                block.gpsimd(_g)
                lg_c88 = S.cnt("lg")

                def _t5(tensor):
                    S.wait(tensor, "lg", lg_c88)
                    S.wait(tensor, "cs", S.cnt("cs"))
                    S.inc(tensor.matmul(ps3[0:8, 0:8], s_onesr[0:1, 0:8],
                                        s_rohr[0:1, :], start=True,
                                        stop=True), "ct")
                block.tensor(_t5)
                ct_rohb = S.cnt("ct")

                def _s5(scalar):
                    S.wait(scalar, "ct", ct_rohb)
                    S.inc(scalar.copy(s_rohb[:, :], ps3[0:8, 0:8]), "cs")
                block.scalar(_s5)
                cs_rohb = S.cnt("cs")

                def _t(tensor):
                    S.wait(tensor, "lg", lg_c88)
                    S.wait(tensor, "cs", S.cnt("cs"))
                    S.inc(tensor.transpose(ps2[0:8, 0:8], s_c88[:, :],
                                           s_ident[0:8, 0:8]), "ct")
                block.tensor(_t)
                ct_c88 = S.cnt("ct")

                def _s(scalar):
                    S.wait(scalar, "ct", ct_c88)
                    S.inc(scalar.copy(s_cT[:, :], ps2[0:8, 0:8]), "cs")
                block.scalar(_s)
                cs_c88 = S.cnt("cs")

                def _v(vector):
                    S.wait(vector, "cs", max(cs_c88, cs_rohb))
                    S.inc(vector.tensor_tensor_scan(
                        s_cumr[:, :], s_cT[:, :], s_z8[:, 0:8], 0.0,
                        ALU.add, ALU.add), "cv")
                    S.wait(vector, "cv", S.cnt("cv"))
                    S.inc(vector.tensor_mul(s_cumr[:, :], s_cumr[:, :],
                                            s_rohb[:, :]), "cv")
                    S.wait(vector, "cv", S.cnt("cv"))
                    S.inc(vector.reduce_sum(s_incl[:, :], s_cumr[:, :],
                                            axis=AX.X), "cv")
                    S.wait(vector, "cv", S.cnt("cv"))
                    S.inc(vector.tensor_sub(s_gb8[:, :], s_incl[:, :],
                                            s_mycnt[:, :]), "cv")
                    S.wait(vector, "cv", S.cnt("cv"))
                    S.inc(vector.tensor_scalar(s_ind8[:, :], s_ind8[:, :],
                                               s_gb8[:, 0:1], None, ALU.mult),
                          "cv")
                    S.wait(vector, "cv", S.cnt("cv"))
                    S.inc(vector.tensor_copy(s_incl[:, :], s_gb8[:, :]), "cv")
                block.vector(_v)
                cv_gb = S.cnt("cv")

                cs_base2 = S.cnt("cs")

                def _t(tensor):
                    S.wait(tensor, "cv", cv_gb)
                    for q in range(4):
                        if q > 0:
                            S.wait(tensor, "cs", cs_base2 + q)
                        S.inc(tensor.matmul(ps1[0:1, 0:512], s_ones[0:8, 0:1],
                                            s_ind8[:, 512 * q:512 * (q + 1)],
                                            start=True, stop=True), "ct")
                block.tensor(_t)
                ct_gb0 = S.cnt("ct") - 4

                def _s(scalar):
                    for q in range(4):
                        S.wait(scalar, "ct", ct_gb0 + q + 1)
                        S.inc(scalar.copy(s_gbrow[0:1, 512 * q:512 * (q + 1)],
                                          ps1[0:1, 0:512]), "cs")
                block.scalar(_s)
                cs_gb = S.cnt("cs")

                def _v(vector):
                    S.wait(vector, "cs", max(cs_lp, cs_gb))
                    S.inc(vector.tensor_add(s_gbrow[:, :], s_gbrow[:, :],
                                            s_lpos[:, :]), "cv")
                    S.wait(vector, "cv", S.cnt("cv"))
                    S.inc(vector.tensor_scalar(s_ml[:, :], s_lpos[:, :],
                                               float(CLOC), None, ALU.is_lt),
                          "cv")
                    S.wait(vector, "cv", S.cnt("cv"))
                    S.inc(vector.tensor_scalar(s_mg[:, :], s_gbrow[:, :],
                                               float(CAP), None, ALU.is_lt),
                          "cv")
                    S.wait(vector, "cv", S.cnt("cv"))
                    S.inc(vector.tensor_mul(s_mg[:, :], s_mg[:, :],
                                            s_ml[:, :]), "cv")
                    S.wait(vector, "cv", S.cnt("cv"))
                    S.inc(vector.tensor_scalar(s_offr[:, :], s_ef[:, :],
                                               float(CLOC), None, ALU.mult),
                          "cv")
                    S.wait(vector, "cv", S.cnt("cv"))
                    S.inc(vector.tensor_add(s_offr[:, :], s_offr[:, :],
                                            s_lpos[:, :]), "cv")
                    S.wait(vector, "cv", S.cnt("cv"))
                    S.inc(vector.tensor_scalar(s_dmpr[:, :], s_offr[:, :],
                                               float(DUMP), None,
                                               ALU.subtract), "cv")
                    S.wait(vector, "cv", S.cnt("cv"))
                    S.inc(vector.tensor_mul(s_sofr[:, :], s_ml[:, :],
                                            s_dmpr[:, :]), "cv")
                    S.inc(vector.tensor_mul(s_gofr[:, :], s_mg[:, :],
                                            s_dmpr[:, :]), "cv")
                    S.wait(vector, "cv", S.cnt("cv"))
                    S.inc(vector.tensor_scalar(s_sofr[:, :], s_sofr[:, :],
                                               float(DUMP), None, ALU.add),
                          "cv")
                    S.inc(vector.tensor_scalar(s_gofr[:, :], s_gofr[:, :],
                                               float(DUMP), None, ALU.add),
                          "cv")
                    S.wait(vector, "cv", S.cnt("cv"))
                    S.inc(vector.tensor_copy(s_dmpr[:, :], s_sofr[:, :]), "cv")
                    S.inc(vector.tensor_copy(s_dmpr[:, :], s_gofr[:, :]), "cv")
                block.vector(_v)
                cv_offs = S.cnt("cv")

                if "route" in taps:
                    def _s(scalar):
                        S.wait(scalar, "cv", cv_offs)
                        for ri, row in enumerate((s_sofr, s_gofr, s_ef, s_lpos,
                                                  s_gbrow, s_ml, s_mg)):
                            S.inc(scalar.dma_start(
                                out=tap_aps["t_route"][ri:ri + 1, :],
                                in_=row[:, :]), "la", 16)
                    block.scalar(_s)

                for src_row, dst in ((s_sofr, s_soffc), (s_gofr, s_goffc)):
                    def _t(tensor, src_row=src_row):
                        S.wait(tensor, "cv", cv_offs)
                        S.wait(tensor, "cs", S.cnt("cs"))
                        v = src_row[0:1, :].rearrange("p (t k) -> p k t", k=2)
                        for col in range(16):
                            tt, k = col // 2, col % 2
                            S.inc(tensor.transpose(
                                ps2[:, col:col + 1],
                                v[:, k, 128 * tt:128 * (tt + 1)],
                                s_ident[0:1, 0:1]), "ct")
                    block.tensor(_t)
                    ct_o = S.cnt("ct")

                    def _s(scalar, ct_o=ct_o):
                        S.wait(scalar, "ct", ct_o)
                        S.inc(scalar.copy(s_sm[:, :], ps2[:, 0:16]), "cs")
                    block.scalar(_s)
                    cs_o = S.cnt("cs")

                    def _v(vector, dst=dst, cs_o=cs_o):
                        S.wait(vector, "cs", cs_o)
                        S.inc(vector.tensor_scalar(dst[:, :].bitcast(F32),
                                                   s_sm[:, :], EXP23, None,
                                                   ALU.add), "cv")
                        S.wait(vector, "cv", S.cnt("cv"))
                        S.inc(vector.tensor_scalar(dst[:, :], dst[:, :], MANT,
                                                   None, ALU.bitwise_and), "cv")
                        S.wait(vector, "cv", S.cnt("cv"))
                        S.inc(vector.tensor_copy(s_sm[:, :],
                                                 dst[:, :].bitcast(F32)), "cv")
                    block.vector(_v)
                cv_offc = S.cnt("cv")

                if "route_only" in stages:
                    def _out(gpsimd):
                        S.wait(gpsimd, "cv", cv_offc)
                        for tt in range(NTOKT):
                            S.inc(gpsimd.dma_start(
                                out=o_out[128 * tt:128 * (tt + 1), :],
                                in_=d_xy.ap().rearrange(
                                    "(d r) t -> d (r t)", r=8)[0:128, 0:DD]),
                                  "st", 16)
                        S.wait(gpsimd, "st", S.cnt("st"))
                    block.gpsimd(_out)
                    return

            lg_zero_done = S.cnt("lg")
            for tt in range(NTOKT):
                for half in range(2):
                    def _t(tensor, tt=tt, half=half):
                        S.wait(tensor, "cv", cv_offc)
                        S.wait(tensor, "cs", S.cnt("cs"))
                        for j in range(8):
                            dt = 8 * half + j
                            S.inc(tensor.transpose(
                                pst[:, 128 * j:128 * (j + 1)],
                                s_y1[dt][:, 128 * tt:128 * (tt + 1)],
                                s_identb[:, :]), "ct")
                    block.tensor(_t)
                    ct_y1t = S.cnt("ct")

                    def _s(scalar, half=half, ct_y1t=ct_y1t):
                        S.wait(scalar, "ct", ct_y1t)
                        S.wait(scalar, "lg", S.cnt("lg"))
                        S.inc(scalar.copy(
                            s_y1tst[:, 1024 * half:1024 * (half + 1)],
                            pst[:, :]), "cs")
                    block.scalar(_s)
                cs_y1t = S.cnt("cs")

                def _g(gpsimd, tt=tt, cs_y1t=cs_y1t):
                    S.wait(gpsimd, "cs", cs_y1t)
                    S.inc(gpsimd.dma_start(
                        out=d_y1t[128 * tt:128 * (tt + 1), :],
                        in_=s_y1tst[:, :]), "lg", 16)
                    for k in range(2):
                        S.inc(gpsimd.indirect_dma_start(
                            d_gath.ap(),
                            bass.IndirectOffsetOnAxis(
                                ap=s_soffc[:, 2 * tt + k:2 * tt + k + 1],
                                axis=0),
                            s_y1tst[:, :], None), "lg", 16)
                block.gpsimd(_g)
            lg_scat = S.cnt("lg")

            s_h = [sb(fctx, f"s_h{i}", [128, TOK], BF16) for i in range(NJT)]
            with ExitStack() as uctx:
                s_wg = [sb(uctx, f"s_wg{p}", [128, DD], BF16) for p in range(2)]
                s_wu = [sb(uctx, f"s_wu{p}", [128, DD], BF16) for p in range(2)]
                s_g2 = [sb(uctx, f"s_g2{i}", [128, 512]) for i in range(2)]
                lastcv_ps, lastcs_ps = {}, {}
                lastcs_wu = {0: 0, 1: 0}

                for p in range(NJT):
                    pb = p % 2

                    def _g(gpsimd, p=p, pb=pb):
                        S.wait(gpsimd, "cs", lastcs_wu[pb])
                        S.inc(gpsimd.dma_start(
                            out=s_wg[pb][:, :], in_=i_swug[p]), "lg", 16)
                        S.inc(gpsimd.dma_start(
                            out=s_wu[pb][:, :], in_=i_swug[NJT + p]),
                              "lg", 16)
                    block.gpsimd(_g)
                    lg_w = S.cnt("lg")

                    for c2 in range(2):
                        ub = (2 * p + c2) % 2
                        pg, pu = ((ps1, ps2), (ps3, ps4))[ub]
                        gk = "sup%d" % ub

                        def _t(tensor, pb=pb, lg_w=lg_w, c2=c2, pg=pg, pu=pu,
                               gk=gk):
                            S.wait(tensor, "lg", lg_w)
                            S.wait(tensor, "cv", lastcv_ps.get(gk, 0))
                            S.wait(tensor, "cs", lastcs_ps.get(gk, 0))
                            for j in range(16):
                                S.inc(tensor.matmul(
                                    pg[:, :], s_wg[pb][:, 128 * j:128 * (j + 1)],
                                    s_y1[j][:, 512 * c2:512 * (c2 + 1)],
                                    start=(j == 0), stop=(j == 15)), "ct")
                            for j in range(16):
                                S.inc(tensor.matmul(
                                    pu[:, :], s_wu[pb][:, 128 * j:128 * (j + 1)],
                                    s_y1[j][:, 512 * c2:512 * (c2 + 1)],
                                    start=(j == 0), stop=(j == 15)), "ct")
                        block.tensor(_t)
                        ct_up = S.cnt("ct")

                        def _s(scalar, ct_up=ct_up, pg=pg):
                            S.wait(scalar, "ct", ct_up)
                            S.wait(scalar, "cv", S.cnt("cv"))
                            S.inc(scalar.activation(s_g2[(2 * p + c2) % 2][:, :],
                                                    pg[:, :], AF.Silu), "cs")
                        block.scalar(_s)
                        cs_si = S.cnt("cs")
                        lastcs_ps[gk] = cs_si

                        def _v(vector, c2=c2, p=p, cs_si=cs_si, pu=pu):
                            S.wait(vector, "cs", cs_si)
                            S.inc(vector.tensor_mul(
                                s_h[p][:, 512 * c2:512 * (c2 + 1)],
                                s_g2[(2 * p + c2) % 2][:, :], pu[:, :]), "cv")
                        block.vector(_v)
                        lastcv_ps[gk] = S.cnt("cv")
                    lastcs_wu[pb] = S.cnt("cs")
                cv_h = S.cnt("cv")

                pass

        with ExitStack() as dctx:
            s_wdh = [sb(dctx, f"s_wdh{p}", [128, JH // 2], BF16)
                     for p in range(2)]
            s_yst = sb(dctx, "s_yst", [128, TOK], BF16)
            s_gT = [sb(dctx, f"s_gT{j}", [128, CLOC], BF16)
                    for j in range(16)]
            s_he = [[sb(dctx, f"s_he{par}_{i}", [128, CLOC], BF16)
                     for i in range(NIT)] for par in range(2)]
            s_wg2 = [sb(dctx, f"s_wg2{p}", [128, DD], BF16) for p in range(2)]
            s_wu2 = [sb(dctx, f"s_wu2{p}", [128, DD], BF16) for p in range(2)]
            s_wd2q = [[sb(dctx, f"s_wd2q{par}_{p}", [128, 512], BF16)
                       for p in range(NIT)] for par in range(2)]
            s_ge2 = [sb(dctx, "s_ge2", [128, CLOC])]
            s_ost512 = [sb(dctx, f"s_ost512{i}", [128, 512], BF16)
                        for i in range(4)]


            cs_ph0 = S.cnt("cs")
            lastcs_sdn = [cs_ph0]
            lastcs_w2 = {0: cs_ph0, 1: cs_ph0}
            lastcs_gt = [cs_ph0]
            lastcs_ps = [cs_ph0]
            lastcv_ps = [0]
            lastcs_dnev = [cs_ph0]
            cs_dn_done = {0: cs_ph0, 1: cs_ph0}
            lg_wdset = {}
            la_ostp = {i: S.cnt("la") for i in range(4)}
            ostp = [0]
            moe_state = {}
            NSET = 4 * EE

            def _wdset_load(gpsimd, si):
                e2, q2 = si // 4, si % 4
                for p in range(NIT):
                    S.inc(gpsimd.dma_start(
                        out=s_wd2q[si % 2][p][:, :],
                        in_=i_wdn[e2][128 * p:128 * (p + 1),
                                      512 * q2:512 * (q2 + 1)]), "lg", 16)
                lg_wdset[si] = S.cnt("lg")

            def _g0(gpsimd):
                for si in (0, 1):
                    _wdset_load(gpsimd, si)
            block.gpsimd(_g0)

            pending_gw = []

            def flush_gw():
                while pending_gw:
                    si2, ctg = pending_gw.pop(0)

                    def _gw(gpsimd, si2=si2, ctg=ctg):
                        S.wait(gpsimd, "ct", ctg)
                        _wdset_load(gpsimd, si2)
                    block.gpsimd(_gw)



            NJH = NJT // 2

            def emit_sdown(dt):
                for half in range(2):
                    def _l(gpsimd, dt=dt, half=half):
                        S.wait(gpsimd, "ct",
                               moe_state.get(("ctwd", half), 0))
                        S.inc(gpsimd.dma_start(
                            out=s_wdh[half][:, :],
                            in_=i_swdn[dt][:, (JH // 2) * half:
                                           (JH // 2) * (half + 1)]),
                              "lg", 16)
                    block.gpsimd(_l)
                    moe_state[("lgwd", half)] = S.cnt("lg")
                flush_gw()

                for half in range(2):
                    lg_wd = moe_state[("lgwd", half)]

                    def _t(tensor, half=half, lg_wd=lg_wd):
                        S.wait(tensor, "lg", lg_wd)
                        S.wait(tensor, "cv", cv_h)
                        if half == 0:
                            S.wait(tensor, "cs", lastcs_sdn[0])
                        for i in range(NJH):
                            ig = NJH * half + i
                            for c2 in range(2):
                                S.inc(tensor.matmul(
                                    pse[:, 512 * c2:512 * (c2 + 1)],
                                    s_wdh[half][:, 128 * i:128 * (i + 1)],
                                    s_h[ig][:, 512 * c2:512 * (c2 + 1)],
                                    start=(ig == 0),
                                    stop=(ig == NJT - 1)), "ct")
                    block.tensor(_t)
                    ct_dn = S.cnt("ct")
                    moe_state[("ctwd", half)] = ct_dn

                def _s(scalar, dt=dt, ct_dn=ct_dn):
                    S.wait(scalar, "ct", ct_dn)
                    S.wait(scalar, "la", S.cnt("la"))
                    S.inc(scalar.copy(s_yst[:, 0:512], pse[:, 0:512]), "cs")
                    S.inc(scalar.copy(s_yst[:, 512:1024],
                                      pse[:, 512:1024]), "cs")
                    S.wait(scalar, "cs", S.cnt("cs"))
                    S.inc(scalar.dma_start(
                        out=d_ysh[128 * dt:128 * (dt + 1), :],
                        in_=s_yst[:, :]), "la", 16)
                block.scalar(_s)
                lastcs_sdn[0] = S.cnt("cs")

            def emit_moe_gt(e):
                def _ls(sync, e=e):
                    S.wait(sync, "lg", lg_scat)
                    S.wait(sync, "cs", lastcs_gt[0])
                    for j in range(16):
                        S.inc(sync.dma_start(
                            out=s_gT[j][:, :],
                            in_=d_gath[CLOC * e:CLOC * (e + 1),
                                       128 * j:128 * (j + 1)],
                            transpose=True), "ls", 16)
                block.sync(_ls)
                moe_state[("lsg", e)] = S.cnt("ls")

            def emit_moe_up(e, p):
                pb = p % 2
                par = e % 2

                def _g(gpsimd, e=e, p=p, pb=pb):
                    S.wait(gpsimd, "cs", lastcs_w2[pb])
                    S.inc(gpsimd.dma_start(
                        out=s_wg2[pb][:, :], in_=i_wug[e][p]), "lg", 16)
                    S.inc(gpsimd.dma_start(
                        out=s_wu2[pb][:, :], in_=i_wug[e][NIT + p]),
                          "lg", 16)
                block.gpsimd(_g)
                lg_w2 = S.cnt("lg")
                ls_g = moe_state[("lsg", e)]
                flush_gw()

                def _t(tensor, pb=pb, lg_w2=lg_w2, ls_g=ls_g):
                    S.wait(tensor, "lg", lg_w2)
                    S.wait(tensor, "ls", ls_g)
                    S.wait(tensor, "cv", lastcv_ps[0])
                    S.wait(tensor, "cs", lastcs_ps[0])
                    for j in range(16):
                        S.inc(tensor.matmul(
                            ps1[:, 0:CLOC],
                            s_wg2[pb][:, 128 * j:128 * (j + 1)],
                            s_gT[j][:, :], start=(j == 0),
                            stop=(j == 15)), "ct")
                    for j in range(16):
                        S.inc(tensor.matmul(
                            ps2[:, 0:CLOC],
                            s_wu2[pb][:, 128 * j:128 * (j + 1)],
                            s_gT[j][:, :], start=(j == 0),
                            stop=(j == 15)), "ct")
                block.tensor(_t)
                ct_up2 = S.cnt("ct")

                def _s(scalar, ct_up2=ct_up2, pb=pb):
                    S.wait(scalar, "ct", ct_up2)
                    S.wait(scalar, "cv", S.cnt("cv"))
                    S.inc(scalar.activation(s_ge2[0][:, 0:CLOC],
                                            ps1[:, 0:CLOC], AF.Silu), "cs")
                block.scalar(_s)
                cs_si2 = S.cnt("cs")
                lastcs_ps[0] = cs_si2
                lastcs_w2[pb] = cs_si2

                def _v(vector, p=p, cs_si2=cs_si2, pb=pb, par=par):
                    S.wait(vector, "cs", cs_si2)
                    S.wait(vector, "cs", cs_dn_done[par])
                    S.inc(vector.tensor_mul(s_he[par][p][:, :],
                                            s_ge2[0][:, 0:CLOC],
                                            ps2[:, 0:CLOC]), "cv")
                block.vector(_v)
                lastcv_ps[0] = S.cnt("cv")
                if p == NIT - 1:
                    moe_state[("cvhe", e)] = S.cnt("cv")
                    lastcs_gt[0] = S.cnt("cs")

            STB = ((0, 128, "ps3"), (128, 128, "ps4"), (256, 64, "psf"))

            def emit_moe_dn(e, q):
                flush_gw()
                si = 4 * e + q
                par = si % 2
                hpar = e % 2
                lg_set = lg_wdset[si]
                cv_he = moe_state[("cvhe", e)]
                psb = {"ps3": ps3, "ps4": ps4, "psf": psf}

                def _t(tensor, e=e, par=par, hpar=hpar,
                       lg_set=lg_set, cv_he=cv_he):
                    S.wait(tensor, "lg", lg_set)
                    S.wait(tensor, "cv", cv_he)
                    S.wait(tensor, "cs", lastcs_dnev[0])
                    for p in range(NIT):
                        for (r0, nr, pn) in STB:
                            S.inc(tensor.matmul(
                                psb[pn][0:nr, :],
                                s_he[hpar][p][:, r0:r0 + nr],
                                s_wd2q[par][p][:, :],
                                start=(p == 0), stop=(p == NIT - 1)), "ct")
                block.tensor(_t)
                ct_g = S.cnt("ct")

                def _s(scalar, e=e, q=q, si=si, ct_g=ct_g):
                    S.wait(scalar, "ct", ct_g)
                    for (r0, nr, pn) in STB:
                        op = ostp[0]
                        ostp[0] = (ostp[0] + 1) % 4
                        S.wait(scalar, "la", la_ostp[op])
                        S.inc(scalar.copy(s_ost512[op][0:nr, :],
                                          psb[pn][0:nr, :]), "cs")
                        S.wait(scalar, "cs", S.cnt("cs"))
                        S.inc(scalar.dma_start(
                            out=d_outb[CLOC * e + r0:CLOC * e + r0 + nr,
                                       512 * q:512 * (q + 1)],
                            in_=s_ost512[op][0:nr, :]), "la", 16)
                        la_ostp[op] = S.cnt("la")
                block.scalar(_s)
                lastcs_dnev[0] = S.cnt("cs")
                if q == 3:
                    cs_dn_done[e % 2] = S.cnt("cs")
                if si + 2 < NSET:
                    pending_gw.append((si + 2, ct_g))

            moe_steps = []
            for e in range(EE):
                moe_steps.append(("gt", e))
                ups = [("up", e, p) for p in range(NIT)]
                dns = ([("dn", e - 1, q) for q in range(4)]
                       if e > 0 else [])
                mix = []
                ui = di = 0
                for slot_i in range(len(ups) + len(dns)):
                    if di < len(dns) and slot_i % 4 == 2:
                        mix.append(dns[di]); di += 1
                    elif ui < len(ups):
                        mix.append(ups[ui]); ui += 1
                    else:
                        mix.append(dns[di]); di += 1
                moe_steps.extend(mix)
            moe_steps.extend([("dn", EE - 1, q) for q in range(4)])

            k = 0

            def emit_one():
                nonlocal k
                st_ = moe_steps[k]
                if st_[0] == "gt":
                    emit_moe_gt(st_[1])
                elif st_[0] == "up":
                    emit_moe_up(st_[1], st_[2])
                else:
                    emit_moe_dn(st_[1], st_[2])
                k += 1

            per_dt = (len(moe_steps) + 15) // 16
            for dt in range(16):
                emit_sdown(dt)
                for _ in range(per_dt):
                    if k < len(moe_steps):
                        emit_one()
            while k < len(moe_steps):
                emit_one()
            la_moe = S.cnt("la")
            la_ysh = la_moe

        with ExitStack() as octx:
            s_ta = sb(octx, "s_ta", [128, DD], BF16)
            s_gA = sb(octx, "s_gA", [128, DD], BF16)
            s_gB = sb(octx, "s_gB", [128, DD], BF16)
            s_sh = sb(octx, "s_sh", [128, DD], BF16)
            s_z = sb(octx, "s_z", [128, DD])
            s_sq2 = sb(octx, "s_sq2", [128, DD])
            s_m2 = sb(octx, "s_m2", [128, 1])
            s_ss2 = sb(octx, "s_ss2", [128, 1])
            s_msq = sb(octx, "s_msq", [128, 1])
            s_v2 = sb(octx, "s_v2", [128, 1])
            s_nm2 = sb(octx, "s_nm2", [128, 1])
            s_orow = sb(octx, "s_orow", [128, DD])

            for tt in range(NTOKT):
                def _g(gpsimd, tt=tt):
                    S.wait(gpsimd, "la", la_moe)
                    S.wait(gpsimd, "cv", S.cnt("cv"))
                    S.inc(gpsimd.dma_start(
                        out=s_ta[:, :],
                        in_=d_y1t[128 * tt:128 * (tt + 1), :]), "lg", 16)
                    S.inc(gpsimd.indirect_dma_start(
                        s_gA[:, :], None, d_outb.ap(),
                        bass.IndirectOffsetOnAxis(
                            ap=s_goffc[:, 2 * tt:2 * tt + 1], axis=0)),
                        "lg", 16)
                    S.inc(gpsimd.indirect_dma_start(
                        s_gB[:, :], None, d_outb.ap(),
                        bass.IndirectOffsetOnAxis(
                            ap=s_goffc[:, 2 * tt + 1:2 * tt + 2], axis=0)),
                        "lg", 16)
                block.gpsimd(_g)
                lg_cmb = S.cnt("lg")

                def _ls(sync, tt=tt):
                    S.wait(sync, "la", la_moe)
                    S.wait(sync, "cv", S.cnt("cv"))
                    S.inc(sync.dma_start(
                        out=s_sh[:, :],
                        in_=d_ysh[0:DD, 128 * tt:128 * (tt + 1)],
                        transpose=True), "ls", 16)
                block.sync(_ls)
                ls_sh = S.cnt("ls")

                def _v(vector, tt=tt, lg_cmb=lg_cmb, ls_sh=ls_sh):
                    S.wait(vector, "lg", lg_cmb)
                    S.wait(vector, "ls", ls_sh)
                    S.wait(vector, "cs", S.cnt("cs"))
                    S.inc(vector.tensor_scalar(s_z[:, :], s_gA[:, :],
                                               s_w1c[:, tt:tt + 1], None,
                                               ALU.mult), "cv")
                    S.wait(vector, "cv", S.cnt("cv"))
                    S.inc(vector.scalar_tensor_tensor(
                        s_z[:, :], s_gB[:, :], s_w2c[:, tt:tt + 1], s_z[:, :],
                        ALU.mult, ALU.add), "cv")
                    S.wait(vector, "cv", S.cnt("cv"))
                    S.inc(vector.tensor_add(s_z[:, :], s_z[:, :],
                                            s_ta[:, :]), "cv")
                    S.wait(vector, "cv", S.cnt("cv"))
                    S.inc(vector.tensor_add(s_z[:, :], s_z[:, :],
                                            s_sh[:, :]), "cv")
                    S.wait(vector, "cv", S.cnt("cv"))
                    S.inc(vector.reduce_sum(s_m2[:, :], s_z[:, :],
                                            axis=AX.X), "cv")
                    S.wait(vector, "cv", S.cnt("cv"))
                    S.inc(vector.tensor_scalar(s_m2[:, :], s_m2[:, :],
                                               1.0 / DD, None, ALU.mult), "cv")
                block.vector(_v)
                cv_z = S.cnt("cv")

                def _s(scalar, cv_z=cv_z):
                    S.wait(scalar, "cv", cv_z)
                    S.inc(scalar.activation(s_sq2[:, :], s_z[:, :], AF.Square,
                                            accum_out=s_ss2[:, 0:1]), "cs")
                block.scalar(_s)
                cs_ss = S.cnt("cs")

                def _v(vector, cs_ss=cs_ss):
                    S.wait(vector, "cs", cs_ss)
                    S.inc(vector.tensor_mul(s_msq[:, :], s_m2[:, :],
                                            s_m2[:, :]), "cv")
                    S.wait(vector, "cv", S.cnt("cv"))
                    S.inc(vector.tensor_scalar(s_v2[:, :], s_ss2[:, :],
                                               1.0 / DD, None, ALU.mult), "cv")
                    S.wait(vector, "cv", S.cnt("cv"))
                    S.inc(vector.tensor_sub(s_v2[:, :], s_v2[:, :],
                                            s_msq[:, :]), "cv")
                block.vector(_v)
                cv_v2 = S.cnt("cv")

                def _v3(vector, cv_v2=cv_v2):
                    S.wait(vector, "cv", cv_v2)
                    S.inc(vector.tensor_scalar(s_v2[:, :], s_v2[:, :], 1e-5,
                                               None, ALU.add), "cv")
                    S.wait(vector, "cv", S.cnt("cv"))
                    S.inc(vector.tensor_copy(s_msq[:, :], s_v2[:, :]), "cv")
                block.vector(_v3)
                cv_v2b = S.cnt("cv")

                def _s(scalar, cv_v2b=cv_v2b):
                    S.wait(scalar, "cv", cv_v2b)
                    S.inc(scalar.activation(s_v2[:, :], s_v2[:, :], AF.Ln), "cs")
                    S.wait(scalar, "cs", S.cnt("cs"))
                    S.inc(scalar.mul(s_v2[:, :], s_v2[:, :], -0.5), "cs")
                    S.wait(scalar, "cs", S.cnt("cs"))
                    S.inc(scalar.activation(s_v2[:, :], s_v2[:, :], AF.Exp), "cs")
                block.scalar(_s)
                cs_r2 = S.cnt("cs")

                def _v(vector, cs_r2=cs_r2):
                    S.wait(vector, "cs", cs_r2)
                    S.inc(vector.scalar_tensor_tensor(
                        s_nm2[:, :], s_m2[:, :], -1.0, s_v2[:, :],
                        ALU.mult, ALU.mult), "cv")
                    S.wait(vector, "cv", S.cnt("cv"))
                block.vector(_v)
                cv_nm2 = S.cnt("cv")

                def _v4(vector, cv_nm2=cv_nm2):
                    S.wait(vector, "cv", cv_nm2)
                    S.inc(vector.tensor_scalar(s_orow[:, :], s_z[:, :],
                                               s_v2[:, 0:1], s_nm2[:, 0:1],
                                               ALU.mult, ALU.add), "cv")
                block.vector(_v4)
                cv_out = S.cnt("cv")

                def _s(scalar, tt=tt, cv_out=cv_out):
                    S.wait(scalar, "cv", cv_out)
                    S.inc(scalar.dma_start(
                        out=o_out[128 * tt:128 * (tt + 1), :],
                        in_=s_orow[:, :]), "st", 16)
                block.scalar(_s)

            def _fin(scalar):
                S.wait(scalar, "st", S.cnt("st"))
            block.scalar(_fin)


def _host_inputs(inputs):
    import ml_dtypes
    x = np.ascontiguousarray(inputs["L0_attn_x"], dtype=np.float32)
    alog = np.ascontiguousarray(inputs["L0_attn_A_log"], dtype=np.float32)
    dsk = np.ascontiguousarray(inputs["L0_attn_D"], dtype=np.float32)
    wx = np.ascontiguousarray(inputs["L0_attn_W_x"], dtype=np.float32)
    wdt = np.ascontiguousarray(inputs["L0_attn_W_dt"], dtype=np.float32)
    bdt = np.ascontiguousarray(inputs["L0_attn_b_dt"], dtype=np.float32)
    rout = np.ascontiguousarray(inputs["L0_ffn_router"], dtype=np.float32)
    wug = np.ascontiguousarray(inputs["L0_ffn_w_up_gate"], dtype=np.float32)
    wdn = np.ascontiguousarray(inputs["L0_ffn_w_down"], dtype=np.float32)
    swug = np.ascontiguousarray(inputs["L0_ffn_shared_w_up_gate"],
                                dtype=np.float32)
    swdn = np.ascontiguousarray(inputs["L0_ffn_shared_w_down"],
                                dtype=np.float32)

    wugP = np.ascontiguousarray(
        wug.reshape(EE, 16, 128, 2 * NIT, 128).transpose(0, 3, 2, 1, 4)
           .reshape(EE, 2 * NIT, 128, 16 * 128))
    swugP = np.ascontiguousarray(
        swug.reshape(16, 128, 2 * NJT, 128).transpose(2, 1, 0, 3)
            .reshape(2 * NJT, 128, 16 * 128))
    swdnP = np.ascontiguousarray(
        swdn.reshape(NJT, 128, 16, 128).transpose(2, 1, 0, 3)
            .reshape(16, 128, NJT * 128))

    ident = np.eye(128, dtype=np.float32)
    identb = np.eye(128, dtype=ml_dtypes.bfloat16)
    ones = np.ones((128, 1), dtype=np.float32)
    iota8c = np.arange(8, dtype=np.float32).reshape(8, 1)

    in_maps = []
    for r in range(NCORE):
        dsl = slice(DSL * r, DSL * (r + 1))
        xyrl = np.empty((128, 16), dtype=np.int32)
        for j in range(16):
            xyrl[:, j] = (128 * j + np.arange(128)) * 8 + r
        strl = (np.arange(16, dtype=np.int32) * 8 + r).reshape(16, 1)
        roh = np.zeros((1, 8), dtype=np.float32)
        roh[0, r] = 1.0
        m = {
            "x_dsl": np.ascontiguousarray(x[:, :, dsl]),
            "wx_sl": np.ascontiguousarray(wx[:, dsl]),
            "wdt_sl": np.ascontiguousarray(wdt[dsl, :]),
            "bdt_sl": np.ascontiguousarray(bdt[dsl].reshape(DSL, 1)),
            "alog_sl": np.ascontiguousarray(alog[dsl, :]),
            "d_sl": np.ascontiguousarray(dsk[dsl].reshape(DSL, 1)),
            "rout_sl": np.ascontiguousarray(rout[dsl, :]),
            "rout": rout,
            "wug": wugP, "wdn": wdn, "swug": swugP, "swdn": swdnP,
            "identf": ident, "identb": identb, "onesf": ones,
            "xyrl": xyrl,
            "strl_a": np.ascontiguousarray(strl[0:8]),
            "strl_b": np.ascontiguousarray(strl[8:10]),
            "strl_c": np.ascontiguousarray(np.array([[9 * 8 + r], [9 * 8 + r]],
                                                    dtype=np.int32)),
            "roh": roh, "iota8c": iota8c,
            "onesr": np.ones((1, 128), dtype=np.float32),
            "sel16": np.repeat(np.eye(16, dtype=np.float32), 128,
                               axis=1).reshape(16, 2048),
        }
        in_maps.append(m)
    return in_maps


def kernel(**inputs):
    nc, _ = build()
    in_maps = _host_inputs(inputs)
    res = run_bass_kernel_spmd(nc, in_maps, core_ids=list(range(NCORE)))
    out = np.empty((BB, TT_, DD), dtype=np.float32)
    for r in range(NCORE):
        b = r // 2
        t0 = (TT_ // 2) * (r % 2)
        out[b, t0:t0 + TOK, :] = np.asarray(res.results[r]["out_tok"])
    return out


# version 11
# speedup vs baseline: 1.1983x; 1.0070x over previous
import numpy as np
import concourse.bass as bass
import concourse.mybir as mybir
from concourse.bass_utils import run_bass_kernel_spmd
from contextlib import ExitStack

F32 = mybir.dt.float32
BF16 = mybir.dt.bfloat16
I32 = mybir.dt.int32
U32 = mybir.dt.uint32
AF = mybir.ActivationFunctionType
ALU = mybir.AluOpType
AX = mybir.AxisListType

BB, TT_, DD = 4, 2048, 2048
NST, RK = 16, 64
EE, IH, JH, CAP = 8, 1408, 5632, 2560
NCORE, TOK, DSL, NDT = 8, 1024, 256, 2
TC, NCH = 512, 4
CLOC = 320
NTOKT = TOK // 128
NJT = JH // 128
NIT = IH // 128
RG = [list(range(NCORE))]
EXP23 = 8388608.0
EXPBITS = 0x4B000000
MANT = 0x7FFFFF
DUMP = EE * CLOC


class Sems:
    def __init__(self):
        self.c, self.h = {}, {}

    def make(self, ctx, nc, *names):
        for n in names:
            self.h[n] = ctx.enter_context(nc.semaphore(n))
            self.c[n] = 0

    def inc(self, instr, name, n=1):
        instr.then_inc(self.h[name], n)
        self.c[name] += n
        return instr

    def cnt(self, name):
        return self.c[name]

    def wait(self, eng, name, val=None):
        eng.wait_ge(self.h[name], self.c[name] if val is None else val)


def build(stages=("mamba", "ffn"), taps=()):
    nc = bass.Bass("TRN2", target_bir_lowering=False,
                   detect_race_conditions=False)
    do_ffn = "ffn" in stages

    def ein(name, shape, dt=F32):
        return nc.dram_tensor(name, shape, dt, kind="ExternalInput").ap()

    i_x_dsl = ein("x_dsl", [BB, TT_, DSL])
    i_wx_sl = ein("wx_sl", [96, DSL])
    i_wdt_sl = ein("wdt_sl", [DSL, RK])
    i_bdt_sl = ein("bdt_sl", [DSL, 1])
    i_alog_sl = ein("alog_sl", [DSL, NST])
    i_d_sl = ein("d_sl", [DSL, 1])
    i_rout_sl = ein("rout_sl", [DSL, EE])
    i_rout = ein("rout", [DD, EE])
    i_wug = ein("wug", [EE, 2 * NIT, 128, 16 * 128])
    i_wdn = ein("wdn", [EE, IH, DD])
    i_swug = ein("swug", [2 * NJT, 128, 16 * 128])
    i_swdn = ein("swdn", [16, 128, NJT * 128])
    i_ident = ein("identf", [128, 128])
    i_identb = ein("identb", [128, 128], BF16)
    i_ones = ein("onesf", [128, 1])
    i_xyrl = ein("xyrl", [128, 16], I32)
    i_strl_a = ein("strl_a", [8, 1], I32)
    i_strl_b = ein("strl_b", [2, 1], I32)
    i_strl_c = ein("strl_c", [2, 1], I32)
    i_roh = ein("roh", [1, 8])
    i_iota8c = ein("iota8c", [8, 1])
    i_onesr = ein("onesr", [1, 128])
    i_sel16 = ein("sel16", [16, 2048])

    o_out = nc.dram_tensor("out_tok", [TOK, DD], F32, kind="ExternalOutput").ap()
    tap_aps = {}
    if "xy" in taps:
        tap_aps["t_xy"] = nc.dram_tensor("t_xy", [DSL, BB * TT_], F32,
                                         kind="ExternalOutput").ap()
    if "y1" in taps:
        tap_aps["t_y1"] = nc.dram_tensor("t_y1", [DD, TOK], BF16,
                                         kind="ExternalOutput").ap()
    if "route" in taps:
        tap_aps["t_route"] = nc.dram_tensor("t_route", [8, 2 * TOK], F32,
                                            kind="ExternalOutput").ap()
    if "st" in taps:
        tap_aps["t_st"] = nc.dram_tensor("t_st", [80, TOK], F32,
                                         kind="ExternalOutput").ap()
        tap_aps["t_raw8"] = nc.dram_tensor("t_raw8", [8, TOK], F32,
                                           kind="ExternalOutput").ap()
        tap_aps["t_st2"] = nc.dram_tensor("t_st2", [2, TOK], F32,
                                          kind="ExternalOutput").ap()
        tap_aps["t_xyg"] = nc.dram_tensor("t_xyg", [128, TOK], BF16,
                                          kind="ExternalOutput").ap()
        tap_aps["t_mr"] = nc.dram_tensor("t_mr", [2, TOK], F32,
                                         kind="ExternalOutput").ap()
        tap_aps["t_mb"] = nc.dram_tensor("t_mb", [128, TOK], F32,
                                         kind="ExternalOutput").ap()
        tap_aps["t_rb"] = nc.dram_tensor("t_rb", [128, TOK], F32,
                                         kind="ExternalOutput").ap()
    if "ag" in taps:
        tap_aps["t_ag"] = nc.dram_tensor("t_ag", [DD, 1024], BF16,
                                         kind="ExternalOutput").ap()
    if "dbg" in taps:
        tap_aps["t_xpr"] = nc.dram_tensor("t_xpr", [96, BB * TT_], F32,
                                          kind="ExternalOutput").ap()
        tap_aps["t_delta"] = nc.dram_tensor("t_delta", [128, TC], F32,
                                            kind="ExternalOutput").ap()
        tap_aps["t_u"] = nc.dram_tensor("t_u", [128, TC], F32,
                                        kind="ExternalOutput").ap()
        tap_aps["t_bx"] = nc.dram_tensor("t_bx", [128, 8 * TC], F32,
                                         kind="ExternalOutput").ap()
        tap_aps["t_hh"] = nc.dram_tensor("t_hh", [128, 8 * TC], F32,
                                         kind="ExternalOutput").ap()

    d_xp_in = nc.dram_tensor("xp_in", [96, BB * TT_], F32)
    d_xpr = nc.dram_tensor("xpr_out", [96, BB * TT_], F32)
    d_st_in = nc.dram_tensor("st_in", [10, BB * TT_], F32)
    d_st = nc.dram_tensor("st_out", [10 * NCORE, TOK], F32)
    d_xy_in = nc.dram_tensor("xy_in", [DSL, BB * TT_], BF16)
    d_xy = nc.dram_tensor("xy_out", [DD * NCORE, TOK], BF16)
    d_cnt_in = nc.dram_tensor("cnt_in", [1, 8], F32)
    d_cnt = nc.dram_tensor("cnt_out", [8, 8], F32)
    d_sel = nc.dram_tensor("sel_d", [TOK, 2], F32)
    d_rows = nc.dram_tensor("rows_d", [8, 2048], F32)
    d_y1t = nc.dram_tensor("y1t_d", [TOK, DD], BF16)
    d_gath = nc.dram_tensor("gath_d", [DUMP + 128, DD], BF16)
    d_outb = nc.dram_tensor("outb_d", [DUMP + 128, DD], BF16)
    d_ysh = nc.dram_tensor("ysh_d", [DD, TOK], BF16)

    with ExitStack() as ctx:
        S = Sems()
        S.make(ctx, nc, "ls", "lg", "la", "cc", "ct", "cs", "cv", "st")

        def sb(c, name, shape, dt=F32):
            return c.enter_context(nc.sbuf_tensor(name, shape, dt))

        s_ident = sb(ctx, "s_ident", [128, 128])
        s_identb = sb(ctx, "s_identb", [128, 128], BF16)
        s_ones = sb(ctx, "s_ones", [128, 1])
        s_io8c = sb(ctx, "s_io8c", [8, 1])
        s_onesr = sb(ctx, "s_onesr", [1, 128])


        ps1 = ctx.enter_context(nc.psum_tensor("ps1", [128, 512], F32))
        ps2 = ctx.enter_context(nc.psum_tensor("ps2", [128, 512], F32))
        ps3 = ctx.enter_context(nc.psum_tensor("ps3", [128, 512], F32))
        ps4 = ctx.enter_context(nc.psum_tensor("ps4", [128, 512], F32))
        pse = ctx.enter_context(nc.psum_tensor("pse", [128, 1024], F32))
        pst = ctx.enter_context(nc.psum_tensor("pst", [128, 1024], BF16))
        psf = ctx.enter_context(nc.psum_tensor("psf", [128, 512], F32))

        block = ctx.enter_context(nc.Block())

        keep = dict(nc=nc, block=block, S=S, sb=sb,
                    s_ident=s_ident, s_identb=s_identb, s_ones=s_ones,
                    s_onesr=s_onesr,
                    s_io8c=s_io8c, ps1=ps1, ps2=ps2, ps3=ps3, ps4=ps4,
                    pse=pse, pst=pst, psf=psf,
                    d_xy=d_xy, d_st=d_st, d_cnt_in=d_cnt_in, d_cnt=d_cnt,
                    d_sel=d_sel, d_rows=d_rows, d_y1t=d_y1t, d_gath=d_gath,
                    d_outb=d_outb, d_ysh=d_ysh, i_xyrl=i_xyrl,
                    i_strl_a=i_strl_a, i_strl_b=i_strl_b, i_strl_c=i_strl_c,
                    i_rout=i_rout, i_swug=i_swug, i_swdn=i_swdn, i_wug=i_wug,
                    i_wdn=i_wdn, i_roh=i_roh, o_out=o_out, tap_aps=tap_aps,
                    taps=taps, stages=stages)

        with ExitStack() as mctx:
            s_wxt = [sb(mctx, f"s_wxt{i}", [128, 96], BF16) for i in range(NDT)]
            s_wdtt = [sb(mctx, f"s_wdtt{i}", [128, 128], BF16) for i in range(NDT)]
            s_bdt = sb(mctx, "s_bdt", [128, NDT])
            s_negA = sb(mctx, "s_negA", [128, NST * NDT])
            s_onepD = sb(mctx, "s_onepD", [128, NDT])
            s_routsl = sb(mctx, "s_routsl", [128, EE * NDT])
            s_carry = sb(mctx, "s_carry", [128, NDT * BB * 2 * 8])
            s_wxf = sb(mctx, "s_wxf", [96, 256])
            s_wxb = sb(mctx, "s_wxb", [96, 256], BF16)
            s_wdtf = sb(mctx, "s_wdtf", [128, 128])
            s_wdtb = sb(mctx, "s_wdtb", [128, 128], BF16)
            s_xraw = [sb(mctx, f"s_xraw{i}", [128, 1024]) for i in range(2)]
            s_xdm = [sb(mctx, f"s_xdm{i}", [128, TC]) for i in range(NDT)]
            s_xbf = [sb(mctx, f"s_xbf{i}", [128, TC], BF16) for i in range(NDT)]
            s_dpf = sb(mctx, "s_dpf", [128, TC])
            s_dpb = sb(mctx, "s_dpb", [128, TC], BF16)
            s_delta = [sb(mctx, f"s_delta{i}", [128, TC]) for i in range(NDT)]
            s_expz = sb(mctx, "s_expz", [128, TC])
            s_u = [sb(mctx, f"s_u{i}", [128, TC]) for i in range(NDT)]
            s_xpre = [sb(mctx, f"s_xpre{i}", [128, TC]) for i in range(NDT)]
            s_sq = sb(mctx, "s_sq", [128, TC])
            s_Bb = [sb(mctx, f"s_Bb{i}", [128, 8 * TC]) for i in range(2)]
            s_Cb = [sb(mctx, f"s_Cb{i}", [128, 8 * TC]) for i in range(2)]
            s_ab = [sb(mctx, f"s_ab{i}", [128, 8 * TC]) for i in range(2)]
            s_bx1 = sb(mctx, "s_bx1", [128, 8 * TC])
            s_hh1 = sb(mctx, "s_hh1", [128, 8 * TC])
            s_corr = sb(mctx, "s_corr", [128, 8])
            s_xy = [sb(mctx, f"s_xy{i}", [128, TC]) for i in range(NDT)]
            s_stev = sb(mctx, "s_stev", [128, TC])
            s_bcr = sb(mctx, "s_bcr", [16, TC], BF16)
            s_sel16b = sb(mctx, "s_sel16b", [16, 2048], BF16)
            s_sel16 = sb(mctx, "s_sel16", [16, 2048])


            def _setup_sync(sync):
                S.inc(sync.dma_start(out=s_ident[:, :], in_=i_ident), "ls", 16)
                S.inc(sync.dma_start(out=s_identb[:, :], in_=i_identb), "ls", 16)
                S.inc(sync.dma_start(out=s_ones[:, :], in_=i_ones), "ls", 16)
                S.inc(sync.dma_start(out=s_io8c[:, :], in_=i_iota8c), "ls", 16)
                S.inc(sync.dma_start(out=s_onesr[:, :], in_=i_onesr), "ls", 16)
                S.inc(sync.dma_start(out=s_sel16[:, :], in_=i_sel16), "ls", 16)
                S.inc(sync.dma_start(out=s_wxf[:, :], in_=i_wx_sl), "ls", 16)
                for i in range(NDT):
                    S.inc(sync.dma_start(out=s_wdtf[:, 64 * i:64 * (i + 1)],
                                         in_=i_wdt_sl[128 * i:128 * (i + 1), :]),
                          "ls", 16)
                    S.inc(sync.dma_start(out=s_bdt[:, i:i + 1],
                                         in_=i_bdt_sl[128 * i:128 * (i + 1), :]),
                          "ls", 16)
                    S.inc(sync.dma_start(out=s_onepD[:, i:i + 1],
                                         in_=i_d_sl[128 * i:128 * (i + 1), :]),
                          "ls", 16)
                    S.inc(sync.dma_start(out=s_negA[:, NST * i:NST * (i + 1)],
                                         in_=i_alog_sl[128 * i:128 * (i + 1), :]),
                          "ls", 16)
                    S.inc(sync.dma_start(out=s_routsl[:, EE * i:EE * (i + 1)],
                                         in_=i_rout_sl[128 * i:128 * (i + 1), :]),
                          "ls", 16)
            block.sync(_setup_sync)
            ls0 = S.cnt("ls")

            def _setup_vec(vector):
                S.wait(vector, "ls", ls0)
                S.inc(vector.tensor_copy(s_wxb[:, :], s_wxf[:, :]), "cv")
                S.inc(vector.tensor_copy(s_wdtb[:, :], s_wdtf[:, :]), "cv")
                S.inc(vector.tensor_copy(s_sel16b[:, :], s_sel16[:, :]), "cv")
                S.inc(vector.memset(s_carry[:, :], 0.0), "cv")
                S.inc(vector.tensor_scalar(s_onepD[:, :], s_onepD[:, :], 1.0,
                                           None, ALU.add), "cv")
            block.vector(_setup_vec)
            cv0 = S.cnt("cv")

            def _setup_sca(scalar):
                S.wait(scalar, "ls", ls0)
                S.inc(scalar.activation(s_negA[:, :], s_negA[:, :], AF.Exp), "cs")
                S.wait(scalar, "cs", S.cnt("cs"))
                S.inc(scalar.mul(s_negA[:, :], s_negA[:, :], -1.0), "cs")
            block.scalar(_setup_sca)

            for i in range(NDT):
                def _t(tensor, i=i):
                    S.wait(tensor, "cv", cv0)
                    S.wait(tensor, "cs", S.cnt("cs"))
                    S.inc(tensor.transpose(pst[:, 0:96],
                                           s_wxb[0:96, 128 * i:128 * (i + 1)],
                                           s_identb[0:96, 0:96]), "ct")
                block.tensor(_t)

                def _s(scalar, i=i):
                    S.wait(scalar, "ct", S.cnt("ct"))
                    S.inc(scalar.copy(s_wxt[i][:, 0:96], pst[:, 0:96]), "cs")
                block.scalar(_s)
            for i in range(NDT):
                def _t(tensor, i=i):
                    S.wait(tensor, "cs", S.cnt("cs"))
                    S.inc(tensor.transpose(pst[0:64, 0:128],
                                           s_wdtb[:, 64 * i:64 * (i + 1)],
                                           s_identb[:, :]), "ct")
                block.tensor(_t)

                def _s(scalar, i=i):
                    S.wait(scalar, "ct", S.cnt("ct"))
                    S.inc(scalar.copy(s_wdtt[i][0:64, 0:128], pst[0:64, 0:128]),
                          "cs")
                block.scalar(_s)

            lastcs_xraw = {0: 0, 1: 0}
            lastct_bcr = {0: 0}
            lastcv_ab = {0: 0, 1: 0}
            lastcv_bb = {0: 0, 1: 0}
            lg_after_xy = [0]

            def load_x_chunk(b, t0, par):
                def _f(sync):
                    S.wait(sync, "cs", lastcs_xraw[par])
                    for g in range(4):
                        S.inc(sync.dma_start(
                            out=s_xraw[par][:, 256 * g:256 * (g + 1)],
                            in_=i_x_dsl[b, t0 + 128 * g:t0 + 128 * (g + 1), :]),
                            "ls", 16)
                block.sync(_f)
                return S.cnt("ls")

            def transpose_x_chunk(ls_x, par):
                def _t(tensor):
                    S.wait(tensor, "ls", ls_x)
                    S.wait(tensor, "cs", S.cnt("cs"))
                    for i in range(NDT):
                        ps = ps1 if i == 0 else ps2
                        for g in range(4):
                            S.inc(tensor.transpose(
                                ps[:, 128 * g:128 * (g + 1)],
                                s_xraw[par][:, 256 * g + 128 * i:256 * g + 128 * (i + 1)],
                                s_ident[:, :]), "ct")
                block.tensor(_t)
                ct_tp = S.cnt("ct")

                def _s(scalar):
                    S.wait(scalar, "ct", ct_tp)
                    for i in range(NDT):
                        ps = ps1 if i == 0 else ps2
                        S.inc(scalar.copy(s_xdm[i][:, :], ps[:, :]), "cs")
                block.scalar(_s)
                lastcs_xraw[par] = S.cnt("cs")
                return S.cnt("cs")

            for b in range(BB):
                for c in range(NCH):
                    t0 = c * TC
                    col0 = b * TT_ + t0
                    par = (b * NCH + c) % 2
                    ls_x = load_x_chunk(b, t0, par)
                    cs_xdm = transpose_x_chunk(ls_x, par)

                    def _v(vector):
                        S.wait(vector, "cs", cs_xdm)
                        S.wait(vector, "ct", S.cnt("ct"))
                        for i in range(NDT):
                            S.inc(vector.tensor_copy(s_xbf[i][:, :],
                                                     s_xdm[i][:, :]), "cv")
                    block.vector(_v)
                    cv_xbf = S.cnt("cv")

                    def _t(tensor):
                        S.wait(tensor, "cv", cv_xbf)
                        S.wait(tensor, "cs", S.cnt("cs"))
                        for i in range(NDT):
                            S.inc(tensor.matmul(ps3[0:96, :], s_wxt[i][:, 0:96],
                                                s_xbf[i][:, :], start=(i == 0),
                                                stop=(i == NDT - 1)), "ct")
                    block.tensor(_t)
                    ct_xp = S.cnt("ct")

                    def _s(scalar):
                        S.wait(scalar, "ct", ct_xp)
                        S.wait(scalar, "la", S.cnt("la"))
                        S.inc(scalar.copy(s_stev[0:96, :], ps3[0:96, :]), "cs")
                        S.wait(scalar, "cs", S.cnt("cs"))
                        S.inc(scalar.dma_start(out=d_xp_in[0:96, col0:col0 + TC],
                                               in_=s_stev[0:96, :]), "la", 16)
                    block.scalar(_s)

            def _cc(gpsimd):
                S.wait(gpsimd, "la", S.cnt("la"))
                S.inc(gpsimd.collective_compute(
                    "AllReduce", ALU.add, replica_groups=RG,
                    ins=[d_xp_in.ap().opt()], outs=[d_xpr.ap().opt()]), "cc", 1)
            block.gpsimd(_cc)
            cc_xp = S.cnt("cc")

            for b in range(BB):
                for c in range(NCH):
                    t0 = c * TC
                    col0 = b * TT_ + t0
                    par = (b * NCH + c) % 2
                    ls_x = load_x_chunk(b, t0, par)
                    cs_xdm = transpose_x_chunk(ls_x, par)

                    def _s(scalar):
                        S.wait(scalar, "cv", S.cnt("cv"))
                        for i in range(NDT):
                            S.inc(scalar.activation(s_xpre[i][:, :], s_xdm[i][:, :],
                                                    AF.Copy,
                                                    scale=s_onepD[:, i:i + 1]),
                                  "cs")
                    block.scalar(_s)
                    cs_xpre = S.cnt("cs")

                    def _g(gpsimd, b=b, c=c):
                        S.wait(gpsimd, "cc", cc_xp)
                        S.wait(gpsimd, "cv", S.cnt("cv"))
                        if "dbg" in taps and b == 0 and c == 0:
                            S.inc(gpsimd.dma_start(out=tap_aps["t_xpr"],
                                                   in_=d_xpr[:, :]), "lg", 16)
                        S.inc(gpsimd.dma_start(out=s_dpf[0:64, :],
                                               in_=d_xpr[0:64, col0:col0 + TC]),
                              "lg", 16)
                    block.gpsimd(_g)
                    lg_dp = S.cnt("lg")

                    def _v(vector):
                        S.wait(vector, "lg", lg_dp)
                        S.inc(vector.tensor_copy(s_dpb[0:64, :], s_dpf[0:64, :]),
                              "cv")
                    block.vector(_v)
                    cv_dpb = S.cnt("cv")

                    def _t(tensor):
                        S.wait(tensor, "cv", cv_dpb)
                        S.wait(tensor, "cs", S.cnt("cs"))
                        for i in range(NDT):
                            ps = ps3 if i == 0 else ps4
                            S.inc(tensor.matmul(ps[:, :], s_wdtt[i][0:64, :],
                                                s_dpb[0:64, :], start=True,
                                                stop=True), "ct")
                    block.tensor(_t)
                    ct_delta = S.cnt("ct")

                    def _s(scalar):
                        S.wait(scalar, "ct", ct_delta)
                        S.wait(scalar, "cv", S.cnt("cv"))
                        for i in range(NDT):
                            ps = ps3 if i == 0 else ps4
                            S.inc(scalar.activation(s_expz[:, :], ps[:, :], AF.Exp,
                                                    bias=s_bdt[:, i:i + 1]), "cs")
                            S.wait(scalar, "cs", S.cnt("cs"))
                            S.inc(scalar.activation(s_delta[i][:, :], s_expz[:, :],
                                                    AF.Ln, bias=1.0), "cs")
                    block.scalar(_s)
                    cs_delta = S.cnt("cs")

                    def _v(vector):
                        S.wait(vector, "cs", max(cs_delta, cs_xdm))
                        for i in range(NDT):
                            S.inc(vector.tensor_mul(s_u[i][:, :], s_delta[i][:, :],
                                                    s_xdm[i][:, :]), "cv")
                    block.vector(_v)
                    cv_u = S.cnt("cv")
                    if "dbg" in taps and b == 0 and c == 0:
                        def _gd(gpsimd):
                            S.wait(gpsimd, "cv", cv_u)
                            S.inc(gpsimd.dma_start(out=tap_aps["t_delta"],
                                                   in_=s_delta[0][:, :]), "lg", 16)
                            S.inc(gpsimd.dma_start(out=tap_aps["t_u"],
                                                   in_=s_u[0][:, :]), "lg", 16)
                        block.gpsimd(_gd)

                    for h in range(2):
                        hpar = (b * NCH + c) * 2 + h
                        def _g(gpsimd, h=h, hpar=hpar):
                            S.wait(gpsimd, "cs", S.cnt("cs"))
                            S.inc(gpsimd.dma_start(
                                out=s_bcr[0:8, :],
                                in_=d_xpr[64 + 8 * h:72 + 8 * h,
                                          col0:col0 + TC]), "lg", 16)
                            S.inc(gpsimd.dma_start(
                                out=s_bcr[8:16, :],
                                in_=d_xpr[80 + 8 * h:88 + 8 * h,
                                          col0:col0 + TC]), "lg", 16)
                        block.gpsimd(_g)
                        lg_bcr = S.cnt("lg")

                        cs_bc0 = S.cnt("cs")

                        def _tb(tensor, lg_bcr=lg_bcr, cs_bc0=cs_bc0,
                                hpar=hpar):
                            S.wait(tensor, "lg", lg_bcr)
                            S.wait(tensor, "cv", S.cnt("cv"))
                            S.wait(tensor, "cs", cs_bc0)
                            for w in range(16):
                                if w >= 2:
                                    S.wait(tensor, "cs", cs_bc0 + w - 1)
                                ps = (ps1, ps2)[w % 2]
                                S.inc(tensor.matmul(
                                    ps[:, :], s_sel16b[:, 128 * w:128 * (w + 1)],
                                    s_bcr[:, :], start=True, stop=True),
                                    "ct")
                        block.tensor(_tb)
                        ct_bc0 = S.cnt("ct") - 16
                        lastct_bcr[0] = S.cnt("ct")

                        def _sb2(scalar, ct_bc0=ct_bc0, hpar=hpar):
                            S.wait(scalar, "cv", lastcv_bb[hpar % 2])
                            for w in range(16):
                                S.wait(scalar, "ct", ct_bc0 + w + 1)
                                dst = s_Bb[hpar % 2] if w < 8 else s_Cb[hpar % 2]
                                n = w % 8
                                ps = (ps1, ps2)[w % 2]
                                S.inc(scalar.copy(
                                    dst[:, n * TC:(n + 1) * TC], ps[:, :]),
                                    "cs")
                        block.scalar(_sb2)
                        lg_bc = S.cnt("cs")

                        for i in range(NDT):
                            un = 2 * h + i
                            ab = s_ab[un % 2]
                            bx = s_bx1
                            hh = s_hh1
                            ccol = ((i * BB + b) * 2 + h) * 8

                            def _s(scalar, i=i, h=h, ab=ab, un=un):
                                S.wait(scalar, "cs", cs_delta)
                                S.wait(scalar, "cv", lastcv_ab[un % 2])
                                for n in range(8):
                                    S.inc(scalar.activation(
                                        ab[:, n * TC:(n + 1) * TC],
                                        s_delta[i][:, :], AF.Exp,
                                        scale=s_negA[:, NST * i + 8 * h + n:
                                                     NST * i + 8 * h + n + 1]),
                                        "cs")
                            block.scalar(_s)
                            cs_ab = S.cnt("cs")

                            def _v(vector, i=i, h=h, ab=ab, bx=bx, hh=hh,
                                   ccol=ccol, cs_ab=cs_ab, lg_bc=lg_bc, b=b,
                                   c=c, hpar=hpar):
                                S.wait(vector, "cs", max(cs_ab, lg_bc))
                                if h == 0:
                                    S.wait(vector, "lg", lg_after_xy[0])
                                for n in range(8):
                                    S.inc(vector.tensor_mul(
                                        bx[:, n * TC:(n + 1) * TC], s_u[i][:, :],
                                        s_Bb[hpar % 2][:, n * TC:(n + 1) * TC]),
                                        "cv")
                                abst = ab[:, :].rearrange("p (n t) -> p n t",
                                                          t=TC)[:, :, 0]
                                bxst = bx[:, :].rearrange("p (n t) -> p n t",
                                                          t=TC)[:, :, 0]
                                S.inc(vector.tensor_mul(
                                    s_corr[:, :], abst,
                                    s_carry[:, ccol:ccol + 8]), "cv")
                                S.wait(vector, "cv", S.cnt("cv"))
                                S.inc(vector.tensor_add(bxst, bxst,
                                                        s_corr[:, :]), "cv")
                                S.wait(vector, "cv", S.cnt("cv"))
                                S.inc(vector.memset(abst, 0.0), "cv")
                                S.wait(vector, "cv", S.cnt("cv"))
                                S.inc(vector.tensor_tensor_scan(
                                    hh[:, :], ab[:, :], bx[:, :], 0.0,
                                    ALU.mult, ALU.add), "cv")
                                hhe = hh[:, :].rearrange("p (n t) -> p n t",
                                                         t=TC)[:, :, TC - 1]
                                S.wait(vector, "cv", S.cnt("cv"))
                                S.inc(vector.tensor_copy(
                                    s_carry[:, ccol:ccol + 8], hhe), "cv")
                                S.wait(vector, "cv", S.cnt("cv"))
                                S.inc(vector.tensor_mul(hh[:, :], hh[:, :],
                                                        s_Cb[hpar % 2][:, :]),
                                      "cv")
                                S.inc(vector.tensor_add(
                                    hh[:, 0:4 * TC], hh[:, 0:4 * TC],
                                    hh[:, 4 * TC:8 * TC]), "cv")
                                S.inc(vector.tensor_add(
                                    hh[:, 0:2 * TC], hh[:, 0:2 * TC],
                                    hh[:, 2 * TC:4 * TC]), "cv")
                                S.inc(vector.tensor_add(
                                    hh[:, 0:TC], hh[:, 0:TC],
                                    hh[:, TC:2 * TC]), "cv")
                                S.wait(vector, "cs", cs_xpre)
                                if h == 0:
                                    S.inc(vector.tensor_add(
                                        s_xy[i][:, :], s_xpre[i][:, :],
                                        hh[:, 0:TC]), "cv")
                                else:
                                    S.inc(vector.tensor_add(
                                        s_xy[i][:, :], s_xy[i][:, :],
                                        hh[:, 0:TC]), "cv")
                            block.vector(_v)
                            if ("dbg" in taps and b == 0 and c == 0
                                    and h == 0 and i == 0):
                                cv_u0 = S.cnt("cv")

                                def _gd2(gpsimd):
                                    S.wait(gpsimd, "cv", cv_u0)
                                    S.inc(gpsimd.dma_start(
                                        out=tap_aps["t_bx"],
                                        in_=bx[:, :]), "lg", 16)
                                    S.inc(gpsimd.dma_start(
                                        out=tap_aps["t_hh"],
                                        in_=hh[:, :]), "lg", 16)
                                block.gpsimd(_gd2)
                                lg_d0 = S.cnt("lg")

                                def _vw(vector):
                                    S.wait(vector, "lg", lg_d0)
                                block.vector(_vw)

                                def _sw(scalar):
                                    S.wait(scalar, "lg", lg_d0)
                                block.scalar(_sw)
                    cv_xy = S.cnt("cv")

                    def _g(gpsimd, b=b, c=c):
                        S.wait(gpsimd, "cv", cv_xy)
                        for i in range(NDT):
                            S.inc(gpsimd.dma_start(
                                out=d_xy_in[128 * i:128 * (i + 1),
                                            col0:col0 + TC],
                                in_=s_xy[i][:, :]), "lg", 16)
                            if "xy" in taps:
                                S.inc(gpsimd.dma_start(
                                    out=tap_aps["t_xy"][128 * i:128 * (i + 1),
                                                        col0:col0 + TC],
                                    in_=s_xy[i][:, :]), "lg", 16)
                    block.gpsimd(_g)
                    lg_after_xy[0] = S.cnt("lg")

                    def _s(scalar):
                        S.wait(scalar, "cv", cv_xy)
                        S.inc(scalar.activation(s_sq[:, :], s_xy[0][:, :],
                                                AF.Square), "cs")
                    block.scalar(_s)
                    cs_q0 = S.cnt("cs")

                    def _t(tensor):
                        S.wait(tensor, "cv", cv_xy)
                        S.wait(tensor, "la", S.cnt("la"))
                        for i in range(NDT):
                            S.inc(tensor.matmul(psf[0:8, :],
                                                s_routsl[:, EE * i:EE * (i + 1)],
                                                s_xy[i][:, :], start=(i == 0),
                                                stop=(i == 1)), "ct")
                        for i in range(NDT):
                            S.inc(tensor.matmul(psf[32:33, :], s_ones[:, 0:1],
                                                s_xy[i][:, :], start=(i == 0),
                                                stop=(i == 1)), "ct")
                        S.wait(tensor, "cs", cs_q0)
                        S.inc(tensor.matmul(psf[64:65, :], s_ones[:, 0:1],
                                            s_sq[:, :], start=True, stop=False),
                              "ct")
                    block.tensor(_t)
                    ct_q0 = S.cnt("ct")

                    def _s(scalar):
                        S.wait(scalar, "ct", ct_q0)
                        S.inc(scalar.activation(s_sq[:, :], s_xy[1][:, :],
                                                AF.Square), "cs")
                    block.scalar(_s)
                    cs_q1 = S.cnt("cs")

                    def _t(tensor):
                        S.wait(tensor, "cs", cs_q1)
                        S.inc(tensor.matmul(psf[64:65, :], s_ones[:, 0:1],
                                            s_sq[:, :], start=False, stop=True),
                              "ct")
                    block.tensor(_t)
                    ct_stats = S.cnt("ct")

                    def _s(scalar):
                        S.wait(scalar, "ct", ct_stats)
                        S.wait(scalar, "la", S.cnt("la"))
                        S.inc(scalar.copy(s_stev[0:8, :], psf[0:8, :]), "cs")
                        S.inc(scalar.copy(s_stev[32:33, :], psf[32:33, :]), "cs")
                        S.inc(scalar.copy(s_stev[64:65, :], psf[64:65, :]), "cs")
                        S.wait(scalar, "cs", S.cnt("cs"))
                        S.inc(scalar.dma_start(out=d_st_in[0:8, col0:col0 + TC],
                                               in_=s_stev[0:8, :]), "la", 16)
                        S.inc(scalar.dma_start(out=d_st_in[8:9, col0:col0 + TC],
                                               in_=s_stev[32:33, :]), "la", 16)
                        S.inc(scalar.dma_start(out=d_st_in[9:10, col0:col0 + TC],
                                               in_=s_stev[64:65, :]), "la", 16)
                    block.scalar(_s)

            def _cc2(gpsimd):
                S.wait(gpsimd, "lg", S.cnt("lg"))
                S.wait(gpsimd, "la", S.cnt("la"))
                S.inc(gpsimd.collective_compute(
                    "AllGather", ALU.bypass, replica_groups=RG,
                    ins=[d_xy_in.ap().opt()], outs=[d_xy.ap().opt()]), "cc", 1)
                S.inc(gpsimd.collective_compute(
                    "AllReduce", ALU.add, replica_groups=RG,
                    ins=[d_st_in.ap().opt()], outs=[d_st.ap().opt()]), "cc", 1)
            block.gpsimd(_cc2)

            bar = {k: S.cnt(k) for k in ("ls", "lg", "la", "cc", "ct", "cs", "cv")}
            for sec in (block.sync, block.vector, block.scalar, block.tensor,
                        block.gpsimd):
                def _barr(eng):
                    for k, v in bar.items():
                        eng.wait_ge(S.h[k], v)
                sec(_barr)

        if not do_ffn:
            def _out(gpsimd):
                for tt in range(NTOKT):
                    S.inc(gpsimd.dma_start(
                        out=o_out[128 * tt:128 * (tt + 1), :],
                        in_=d_xy.ap().rearrange("(d r) t -> d (r t)",
                                                r=8)[0:128, 0:DD]), "st", 16)
                S.wait(gpsimd, "st", S.cnt("st"))
            block.gpsimd(_out)
            return nc, tap_aps

        _ffn(keep)
    return nc, tap_aps


def _ffn(E):
    nc, block, S, sb = E["nc"], E["block"], E["S"], E["sb"]
    s_ident, s_identb, s_ones, s_io8c, s_onesr = (E[k] for k in
        ("s_ident", "s_identb", "s_ones", "s_io8c", "s_onesr"))
    ps1, ps2, ps3, ps4, pse, pst, psf = (E[k] for k in
        ("ps1", "ps2", "ps3", "ps4", "pse", "pst", "psf"))
    d_xy, d_st, d_cnt_in, d_cnt, d_sel, d_rows = (E[k] for k in
        ("d_xy", "d_st", "d_cnt_in", "d_cnt", "d_sel", "d_rows"))
    d_y1t, d_gath, d_outb, d_ysh = (E[k] for k in
        ("d_y1t", "d_gath", "d_outb", "d_ysh"))
    i_xyrl, i_strl_a, i_strl_b, i_strl_c = (E[k] for k in
        ("i_xyrl", "i_strl_a", "i_strl_b", "i_strl_c"))
    i_rout, i_swug, i_swdn, i_wug, i_wdn, i_roh = (E[k] for k in
        ("i_rout", "i_swug", "i_swdn", "i_wug", "i_wdn", "i_roh"))
    o_out = E["o_out"]
    tap_aps, taps = E["tap_aps"], E["taps"]
    stages = E.get("stages", ())

    with ExitStack() as fctx:
        s_y1tst = sb(fctx, "s_y1tst", [128, DD], BF16)
        s_zero = sb(fctx, "s_zero", [128, 512], BF16)
        s_xyrl = sb(fctx, "s_xyrl", [128, 16], I32)
        s_sia = sb(fctx, "s_sia", [8, 1], I32)
        s_sib = sb(fctx, "s_sib", [2, 1], I32)
        s_sic = sb(fctx, "s_sic", [2, 1], I32)
        s_soffc = sb(fctx, "s_soffc", [128, 16], I32)
        s_goffc = sb(fctx, "s_goffc", [128, 16], I32)
        s_w1c = sb(fctx, "s_w1c", [128, 8])
        s_w2c = sb(fctx, "s_w2c", [128, 8])

        def _idx(sync):
            S.inc(sync.dma_start(out=s_xyrl[:, :], in_=i_xyrl), "ls", 16)
            S.inc(sync.dma_start(out=s_sia[:, :], in_=i_strl_a), "ls", 16)
            S.inc(sync.dma_start(out=s_sib[:, :], in_=i_strl_b), "ls", 16)
            S.inc(sync.dma_start(out=s_sic[:, :], in_=i_strl_c), "ls", 16)
        block.sync(_idx)
        ls_idx = S.cnt("ls")

        def _z(vector):
            S.inc(vector.memset(s_zero[:, :], 0.0), "cv")
        block.vector(_z)
        cv_zero = S.cnt("cv")

        def _zg(gpsimd):
            S.wait(gpsimd, "cv", cv_zero)
            nrow = DUMP + 128
            for r0 in range(0, nrow, 128):
                nr = min(128, nrow - r0)
                for c0 in range(0, DD, 512):
                    S.inc(gpsimd.dma_start(
                        out=d_gath[r0:r0 + nr, c0:c0 + 512],
                        in_=s_zero[0:nr, :]), "lg", 16)
                    S.inc(gpsimd.dma_start(
                        out=d_outb[r0:r0 + nr, c0:c0 + 512],
                        in_=s_zero[0:nr, :]), "lg", 16)
        block.gpsimd(_zg)

        with ExitStack() as actx:
            s_y1 = [sb(fctx, f"s_y1_{j}", [128, TOK], BF16) for j in range(16)]

            def _gxy(gpsimd):
                S.wait(gpsimd, "ls", ls_idx)
                if "ag" in taps:
                    S.inc(gpsimd.dma_start(
                        out=tap_aps["t_ag"],
                        in_=d_xy.ap().rearrange("(d r) t -> d (r t)",
                                                r=8)[:, 0:1024]), "lg", 16)
                xyv = d_xy.ap()
                for j in range(16):
                    S.inc(gpsimd.indirect_dma_start(
                        s_y1[j][:, :], None, xyv,
                        bass.IndirectOffsetOnAxis(ap=s_xyrl[:, j:j + 1], axis=0)),
                        "lg", 16)
            block.gpsimd(_gxy)
            lg_xy = S.cnt("lg")

            with ExitStack() as bctx:
                s_t1 = sb(bctx, "s_t1", [128, TOK])
                s_meanb = sb(bctx, "s_meanb", [128, TOK])
                s_rstdb = sb(bctx, "s_rstdb", [128, TOK])
                s_raw8 = sb(bctx, "s_raw8", [8, TOK])
                s_st2 = sb(bctx, "s_st2", [2, TOK])
                s_ssq2 = sb(bctx, "s_ssq2", [2, TOK])
                s_mrow = sb(bctx, "s_mrow", [1, TOK])
                s_rrow = sb(bctx, "s_rrow", [1, TOK])
                s_rsr = sb(bctx, "s_rsr", [1, 8])
                s_cntr = sb(bctx, "s_cntr", [1, 8])
                s_rt = sb(bctx, "s_rt", [128, 128])
                s_rsumb = sb(bctx, "s_rsumb", [128, 8])
                s_nmc = sb(bctx, "s_nmc", [128, 8])
                s_rsc = sb(bctx, "s_rsc", [128, 8])
                s_lg = sb(bctx, "s_lg", [128, 8])
                s_srt = sb(bctx, "s_srt", [128, 8])
                s_idxs = sb(bctx, "s_idxs", [128, 8], U32)
                s_tmpu = sb(bctx, "s_tmpu", [128, 2], U32)
                s_ekl = sb(bctx, "s_ekl", [128, 2])
                s_c1 = sb(bctx, "s_c1", [128, 1])
                s_c2 = sb(bctx, "s_c2", [128, 1])
                s_c3 = sb(bctx, "s_c3", [128, 1])
                s_eb8 = sb(bctx, "s_eb8", [8, 2 * TOK])
                s_ind8 = sb(bctx, "s_ind8", [8, 2 * TOK])
                s_cum8 = sb(bctx, "s_cum8", [8, 2 * TOK])
                s_z8 = sb(bctx, "s_z8", [8, 2 * TOK])
                s_lpos = sb(bctx, "s_lpos", [1, 2 * TOK])
                s_gbrow = sb(bctx, "s_gbrow", [1, 2 * TOK])
                s_ml = sb(bctx, "s_ml", [1, 2 * TOK])
                s_mg = sb(bctx, "s_mg", [1, 2 * TOK])
                s_offr = sb(bctx, "s_offr", [1, 2 * TOK])
                s_dmpr = sb(bctx, "s_dmpr", [1, 2 * TOK])
                s_sofr = sb(bctx, "s_sofr", [1, 2 * TOK])
                s_gofr = sb(bctx, "s_gofr", [1, 2 * TOK])
                s_ef = sb(bctx, "s_ef", [1, 2 * TOK])
                s_mycnt = sb(bctx, "s_mycnt", [8, 1])
                s_c88 = sb(bctx, "s_c88", [8, 8])
                s_cT = sb(bctx, "s_cT", [8, 8])
                s_cumr = sb(bctx, "s_cumr", [8, 8])
                s_rohb = sb(bctx, "s_rohb", [8, 8])
                s_rohr = sb(bctx, "s_rohr", [1, 8])
                s_incl = sb(bctx, "s_incl", [8, 1])
                s_gb8 = sb(bctx, "s_gb8", [8, 1])
                s_sm = sb(bctx, "s_sm", [128, 16])

                def _gst(gpsimd):
                    S.wait(gpsimd, "ls", ls_idx)
                    stv = d_st.ap()
                    S.inc(gpsimd.indirect_dma_start(
                        s_raw8[:, :], None, stv,
                        bass.IndirectOffsetOnAxis(ap=s_sia[:, 0:1], axis=0)),
                        "lg", 16)
                    S.inc(gpsimd.indirect_dma_start(
                        s_st2[:, :], None, stv,
                        bass.IndirectOffsetOnAxis(ap=s_sib[:, 0:1], axis=0)),
                        "lg", 16)
                    S.inc(gpsimd.indirect_dma_start(
                        s_ssq2[:, :], None, stv,
                        bass.IndirectOffsetOnAxis(ap=s_sic[:, 0:1], axis=0)),
                        "lg", 16)
                block.gpsimd(_gst)
                lg_st = S.cnt("lg")

                if "st" in taps:
                    def _gtap(gpsimd):
                        S.wait(gpsimd, "lg", lg_st)
                        S.wait(gpsimd, "cc", S.cnt("cc"))
                        S.inc(gpsimd.dma_start(out=tap_aps["t_st"],
                                               in_=d_st[:, :]), "lg", 16)
                        S.inc(gpsimd.dma_start(out=tap_aps["t_raw8"],
                                               in_=s_raw8[:, :]), "lg", 16)
                        S.inc(gpsimd.dma_start(out=tap_aps["t_st2"],
                                               in_=s_st2[:, :]), "lg", 16)
                        S.inc(gpsimd.dma_start(out=tap_aps["t_xyg"],
                                               in_=s_y1[0][:, :]), "lg", 16)
                    block.gpsimd(_gtap)
                    lg_tap = S.cnt("lg")

                    def _vtap(vector):
                        S.wait(vector, "lg", lg_tap)
                    block.vector(_vtap)

                def _rt(sync):
                    S.inc(sync.dma_start(
                        out=s_rt[:, :].rearrange("p (j e) -> p j e", e=8),
                        in_=i_rout.rearrange("(j p) e -> p j e", p=128)),
                        "ls", 16)
                block.sync(_rt)
                ls_rt = S.cnt("ls")

                def _t(tensor):
                    S.wait(tensor, "ls", ls_rt)
                    for j in range(16):
                        S.inc(tensor.matmul(ps3[0:8, 0:1],
                                            s_rt[:, 8 * j:8 * (j + 1)],
                                            s_ones[:, 0:1], start=(j == 0),
                                            stop=(j == 15)), "ct")
                block.tensor(_t)
                ct_rs = S.cnt("ct")

                def _s(scalar):
                    S.wait(scalar, "ct", ct_rs)
                    S.inc(scalar.copy(s_gb8[:, :], ps3[0:8, 0:1]), "cs")
                block.scalar(_s)
                cs_rs = S.cnt("cs")

                def _t(tensor):
                    S.wait(tensor, "cs", cs_rs)
                    S.inc(tensor.transpose(ps4[0:1, 0:8], s_gb8[:, 0:1],
                                           s_ident[0:8, 0:8]), "ct")
                block.tensor(_t)
                ct_rs2 = S.cnt("ct")

                def _s(scalar):
                    S.wait(scalar, "ct", ct_rs2)
                    S.inc(scalar.copy(s_rsr[0:1, 0:8], ps4[0:1, 0:8]), "cs")
                block.scalar(_s)

                def _t2(tensor):
                    S.wait(tensor, "cs", S.cnt("cs"))
                    S.inc(tensor.matmul(ps1[:, 0:8], s_onesr[0:1, :],
                                        s_rsr[0:1, 0:8], start=True,
                                        stop=True), "ct")
                block.tensor(_t2)
                ct_rsb = S.cnt("ct")

                def _s2(scalar):
                    S.wait(scalar, "ct", ct_rsb)
                    S.inc(scalar.copy(s_rsumb[:, :], ps1[:, 0:8]), "cs")
                block.scalar(_s2)
                lg_rsb = S.cnt("cs")

                def _v(vector):
                    S.wait(vector, "lg", lg_st)
                    S.inc(vector.tensor_scalar(s_mrow[:, :], s_st2[0:1, :],
                                               1.0 / DD, None, ALU.mult), "cv")
                    S.wait(vector, "cv", S.cnt("cv"))
                    S.inc(vector.tensor_scalar(s_rrow[:, :], s_ssq2[0:1, :],
                                               1.0 / DD, None, ALU.mult), "cv")
                    S.wait(vector, "cv", S.cnt("cv"))
                    S.inc(vector.tensor_mul(s_t1[0:1, :], s_mrow[:, :],
                                            s_mrow[:, :]), "cv")
                    S.wait(vector, "cv", S.cnt("cv"))
                    S.inc(vector.tensor_sub(s_rrow[:, :], s_rrow[:, :],
                                            s_t1[0:1, :]), "cv")
                block.vector(_v)
                cv_var = S.cnt("cv")

                def _v2(vector):
                    S.wait(vector, "cv", cv_var)
                    S.inc(vector.tensor_scalar(s_rrow[:, :], s_rrow[:, :],
                                               1e-5, None, ALU.add), "cv")
                    S.wait(vector, "cv", S.cnt("cv"))
                    S.inc(vector.tensor_copy(s_t1[0:1, :], s_rrow[:, :]), "cv")
                block.vector(_v2)
                cv_eps = S.cnt("cv")

                def _s(scalar):
                    S.wait(scalar, "cv", cv_eps)
                    S.inc(scalar.activation(s_rrow[:, :], s_rrow[:, :],
                                            AF.Ln), "cs")
                    S.wait(scalar, "cs", S.cnt("cs"))
                    S.inc(scalar.mul(s_rrow[:, :], s_rrow[:, :], -0.5), "cs")
                    S.wait(scalar, "cs", S.cnt("cs"))
                    S.inc(scalar.activation(s_rrow[:, :], s_rrow[:, :],
                                            AF.Exp), "cs")
                block.scalar(_s)
                la_mr = S.cnt("cs")

                cs_mb0 = S.cnt("cs")

                def _t3(tensor):
                    S.wait(tensor, "cs", cs_mb0)
                    for w in range(4):
                        if w >= 2:
                            S.wait(tensor, "cs", cs_mb0 + w - 1)
                        src = s_mrow if w < 2 else s_rrow
                        q = w % 2
                        ps = ps1 if w % 2 == 0 else ps2
                        S.inc(tensor.matmul(
                            ps[:, :], s_onesr[0:1, :],
                            src[0:1, 512 * q:512 * (q + 1)], start=True,
                            stop=True), "ct")
                block.tensor(_t3)
                ct_mb0 = S.cnt("ct") - 4

                def _s3(scalar):
                    for w in range(4):
                        S.wait(scalar, "ct", ct_mb0 + w + 1)
                        dst = s_meanb if w < 2 else s_rstdb
                        q = w % 2
                        ps = ps1 if w % 2 == 0 else ps2
                        S.inc(scalar.copy(dst[:, 512 * q:512 * (q + 1)],
                                          ps[:, :]), "cs")
                block.scalar(_s3)
                lg_mrb = S.cnt("cs")

                if "st" in taps:
                    def _gtap2(gpsimd):
                        S.wait(gpsimd, "cs", lg_mrb)
                        S.inc(gpsimd.dma_start(out=tap_aps["t_mr"][0:1, :],
                                               in_=s_mrow[:, :]), "lg", 16)
                        S.inc(gpsimd.dma_start(out=tap_aps["t_mr"][1:2, :],
                                               in_=s_rrow[:, :]), "lg", 16)
                        S.inc(gpsimd.dma_start(out=tap_aps["t_mb"],
                                               in_=s_meanb[:, :]), "lg", 16)
                        S.inc(gpsimd.dma_start(out=tap_aps["t_rb"],
                                               in_=s_rstdb[:, :]), "lg", 16)
                    block.gpsimd(_gtap2)
                    lg_tap2 = S.cnt("lg")

                    def _vtap2(vector):
                        S.wait(vector, "lg", lg_tap2)
                    block.vector(_vtap2)

                    def _stap2(scalar):
                        S.wait(scalar, "lg", lg_tap2)
                    block.scalar(_stap2)

                def _t(tensor):
                    S.wait(tensor, "cs", S.cnt("cs"))
                    for tt in range(8):
                        S.inc(tensor.transpose(
                            ps3[:, tt:tt + 1],
                            s_mrow[0:1, 128 * tt:128 * (tt + 1)],
                            s_ident[0:1, 0:1]), "ct")
                    for tt in range(8):
                        S.inc(tensor.transpose(
                            ps4[:, tt:tt + 1],
                            s_rrow[0:1, 128 * tt:128 * (tt + 1)],
                            s_ident[0:1, 0:1]), "ct")
                block.tensor(_t)
                ct_cols = S.cnt("ct")

                def _s(scalar):
                    S.wait(scalar, "ct", ct_cols)
                    S.inc(scalar.copy(s_nmc[:, :], ps3[:, 0:8]), "cs")
                    S.inc(scalar.copy(s_rsc[:, :], ps4[:, 0:8]), "cs")
                block.scalar(_s)
                cs_cols = S.cnt("cs")

                def _v(vector):
                    S.wait(vector, "cs", cs_cols)
                    S.inc(vector.tensor_scalar(s_nmc[:, :], s_nmc[:, :], -1.0,
                                               None, ALU.mult), "cv")
                block.vector(_v)
                cv_nmc = S.cnt("cv")

                def _v(vector):
                    S.wait(vector, "lg", lg_xy)
                    S.wait(vector, "cs", lg_mrb)
                    for j in range(16):
                        S.inc(vector.tensor_sub(s_t1[:, :], s_y1[j][:, :],
                                                s_meanb[:, :]), "cv")
                        S.wait(vector, "cv", S.cnt("cv"))
                        S.inc(vector.tensor_mul(s_y1[j][:, :], s_t1[:, :],
                                                s_rstdb[:, :]), "cv")
                block.vector(_v)
                cv_ln1 = S.cnt("cv")

                if "y1" in taps:
                    def _g(gpsimd):
                        S.wait(gpsimd, "cv", cv_ln1)
                        for j in range(16):
                            S.inc(gpsimd.dma_start(
                                out=tap_aps["t_y1"][128 * j:128 * (j + 1), :],
                                in_=s_y1[j][:, :]), "lg", 16)
                    block.gpsimd(_g)

                for tt in range(8):
                    def _t(tensor, tt=tt):
                        S.wait(tensor, "lg", lg_st)
                        S.wait(tensor, "cs", S.cnt("cs"))
                        S.inc(tensor.transpose(
                            ps1[:, 0:8], s_raw8[:, 128 * tt:128 * (tt + 1)],
                            s_ident[0:8, 0:8]), "ct")
                    block.tensor(_t)
                    ct_l = S.cnt("ct")

                    def _s(scalar, ct_l=ct_l):
                        S.wait(scalar, "ct", ct_l)
                        S.wait(scalar, "cv", S.cnt("cv"))
                        S.inc(scalar.copy(s_lg[:, :], ps1[:, 0:8]), "cs")
                    block.scalar(_s)
                    cs_l = S.cnt("cs")

                    def _v(vector, tt=tt, cs_l=cs_l):
                        S.wait(vector, "cs", cs_l)
                        S.wait(vector, "cs", lg_rsb)
                        S.wait(vector, "cv", cv_nmc)
                        S.inc(vector.scalar_tensor_tensor(
                            s_lg[:, :], s_rsumb[:, :], s_nmc[:, tt:tt + 1],
                            s_lg[:, :], ALU.mult, ALU.add), "cv")
                        S.wait(vector, "cv", S.cnt("cv"))
                        S.inc(vector.tensor_scalar(s_lg[:, :], s_lg[:, :],
                                                   s_rsc[:, tt:tt + 1], None,
                                                   ALU.mult), "cv")
                        S.wait(vector, "cv", S.cnt("cv"))
                        S.inc(vector.max(s_srt[:, :], s_lg[:, :]), "cv")
                        S.wait(vector, "cv", S.cnt("cv"))
                        S.inc(vector.max_index(s_idxs[:, :], s_srt[:, :],
                                               s_lg[:, :]), "cv")
                        S.wait(vector, "cv", S.cnt("cv"))
                        S.inc(vector.tensor_scalar(s_tmpu[:, 0:2],
                                                   s_idxs[:, 0:2], EXPBITS,
                                                   None, ALU.bitwise_or), "cv")
                        S.wait(vector, "cv", S.cnt("cv"))
                        S.inc(vector.tensor_scalar(s_ekl[:, :],
                                                   s_tmpu[:, 0:2].bitcast(F32),
                                                   EXP23, None, ALU.subtract),
                              "cv")
                        S.inc(vector.tensor_sub(s_c1[:, :], s_srt[:, 1:2],
                                                s_srt[:, 0:1]), "cv")
                        S.wait(vector, "cv", S.cnt("cv"))
                        S.inc(vector.tensor_copy(s_c3[:, :], s_c1[:, :]), "cv")
                    block.vector(_v)
                    cv_g = S.cnt("cv")

                    def _s(scalar, cv_g=cv_g):
                        S.wait(scalar, "cv", cv_g)
                        S.inc(scalar.activation(s_c2[:, :], s_c1[:, :], AF.Exp),
                              "cs")
                    block.scalar(_s)
                    cs_eg = S.cnt("cs")

                    def _v(vector, tt=tt, cs_eg=cs_eg):
                        S.wait(vector, "cs", cs_eg)
                        S.inc(vector.tensor_scalar(s_c3[:, :], s_c2[:, :], 1.0,
                                                   None, ALU.add), "cv")
                        S.wait(vector, "cv", S.cnt("cv"))
                        S.inc(vector.reciprocal(s_w1c[:, tt:tt + 1],
                                                s_c3[:, :]), "cv")
                        S.wait(vector, "cv", S.cnt("cv"))
                        S.inc(vector.tensor_scalar(s_w2c[:, tt:tt + 1],
                                                   s_w1c[:, tt:tt + 1], -1.0,
                                                   1.0, ALU.mult, ALU.add), "cv")
                        S.wait(vector, "cv", S.cnt("cv"))
                    block.vector(_v)
                    cv_w = S.cnt("cv")

                    def _vd(vector, cv_w=cv_w):
                        S.wait(vector, "cv", cv_w)
                        S.inc(vector.tensor_copy(s_c3[:, :], s_ekl[:, 0:1]),
                              "cv")
                        S.inc(vector.tensor_copy(s_c3[:, :], s_ekl[:, 1:2]),
                              "cv")
                    block.vector(_vd)
                    cv_wd = S.cnt("cv")

                    def _s(scalar, tt=tt, cv_wd=cv_wd):
                        S.wait(scalar, "cv", cv_wd)
                        S.inc(scalar.dma_start(
                            out=d_sel[128 * tt:128 * (tt + 1), :],
                            in_=s_ekl[:, :]), "la", 16)
                    block.scalar(_s)
                la_sel = S.cnt("la")

                def _g(gpsimd):
                    S.wait(gpsimd, "la", la_sel)
                    flat = d_sel.ap().rearrange("t k -> (t k)").unsqueeze(0)
                    S.inc(gpsimd.dma_start(out=s_ef[:, :], in_=flat), "lg", 16)
                block.gpsimd(_g)
                lg_ef = S.cnt("lg")

                cs_eb0 = S.cnt("cs")

                def _t4(tensor):
                    S.wait(tensor, "lg", lg_ef)
                    S.wait(tensor, "cs", cs_eb0)
                    for q in range(4):
                        if q >= 2:
                            S.wait(tensor, "cs", cs_eb0 + q - 1)
                        ps = ps1 if q % 2 == 0 else ps2
                        S.inc(tensor.matmul(
                            ps[0:8, :], s_onesr[0:1, 0:8],
                            s_ef[0:1, 512 * q:512 * (q + 1)], start=True,
                            stop=True), "ct")
                block.tensor(_t4)
                ct_eb0 = S.cnt("ct") - 4

                def _s4(scalar):
                    for q in range(4):
                        S.wait(scalar, "ct", ct_eb0 + q + 1)
                        ps = ps1 if q % 2 == 0 else ps2
                        S.inc(scalar.copy(s_eb8[:, 512 * q:512 * (q + 1)],
                                          ps[0:8, :]), "cs")
                block.scalar(_s4)
                cs_eb = S.cnt("cs")

                def _v(vector):
                    S.wait(vector, "lg", lg_ef)
                    S.wait(vector, "cs", cs_eb)
                    S.inc(vector.memset(s_z8[:, :], 0.0), "cv")
                    S.wait(vector, "cv", S.cnt("cv"))
                    S.inc(vector.tensor_scalar(s_ind8[:, :], s_eb8[:, :],
                                               s_io8c[:, 0:1], None,
                                               ALU.is_equal), "cv")
                    S.wait(vector, "cv", S.cnt("cv"))
                    S.inc(vector.tensor_tensor_scan(
                        s_cum8[:, :], s_ind8[:, :], s_z8[:, :], 0.0,
                        ALU.add, ALU.add), "cv")
                    S.wait(vector, "cv", S.cnt("cv"))
                    S.inc(vector.tensor_copy(
                        s_mycnt[:, :], s_cum8[:, 2 * TOK - 1:2 * TOK]), "cv")
                    S.wait(vector, "cv", S.cnt("cv"))
                    S.inc(vector.tensor_scalar(s_cum8[:, :], s_cum8[:, :], 1.0,
                                               None, ALU.subtract), "cv")
                    S.wait(vector, "cv", S.cnt("cv"))
                    S.inc(vector.tensor_mul(s_cum8[:, :], s_cum8[:, :],
                                            s_ind8[:, :]), "cv")
                    S.wait(vector, "cv", S.cnt("cv"))
                    S.inc(vector.tensor_copy(s_gb8[:, :], s_mycnt[:, :]), "cv")
                block.vector(_v)
                cv_cum = S.cnt("cv")

                cs_base = S.cnt("cs")

                def _t(tensor):
                    S.wait(tensor, "cv", cv_cum)
                    for q in range(4):
                        if q > 0:
                            S.wait(tensor, "cs", cs_base + q)
                        S.inc(tensor.matmul(ps1[0:1, 0:512], s_ones[0:8, 0:1],
                                            s_cum8[:, 512 * q:512 * (q + 1)],
                                            start=True, stop=True), "ct")
                block.tensor(_t)
                ct_lp0 = S.cnt("ct") - 4

                def _s(scalar):
                    for q in range(4):
                        S.wait(scalar, "ct", ct_lp0 + q + 1)
                        S.inc(scalar.copy(s_lpos[0:1, 512 * q:512 * (q + 1)],
                                          ps1[0:1, 0:512]), "cs")
                block.scalar(_s)
                cs_lp = S.cnt("cs")

                def _t(tensor):
                    S.wait(tensor, "cv", cv_cum)
                    S.wait(tensor, "cs", S.cnt("cs"))
                    S.inc(tensor.transpose(ps2[0:1, 0:8], s_mycnt[:, 0:1],
                                           s_ident[0:8, 0:8]), "ct")
                block.tensor(_t)
                ct_cnt = S.cnt("ct")

                def _s(scalar):
                    S.wait(scalar, "ct", ct_cnt)
                    S.inc(scalar.copy(s_cntr[0:1, 0:8], ps2[0:1, 0:8]), "cs")
                    S.wait(scalar, "cs", S.cnt("cs"))
                    S.inc(scalar.dma_start(out=d_cnt_in[0:1, :],
                                           in_=s_cntr[0:1, 0:8]), "la", 16)
                block.scalar(_s)
                la_cnt = S.cnt("la")

                def _g(gpsimd):
                    S.wait(gpsimd, "la", la_cnt)
                    S.inc(gpsimd.collective_compute(
                        "AllGather", ALU.bypass, replica_groups=RG,
                        ins=[d_cnt_in.ap().opt()], outs=[d_cnt.ap().opt()]),
                        "cc", 1)
                    S.wait(gpsimd, "cc", S.cnt("cc"))
                    S.inc(gpsimd.dma_start(out=s_c88[:, :], in_=d_cnt[:, :]),
                          "lg", 16)
                    S.inc(gpsimd.dma_start(out=s_rohr[:, :], in_=i_roh),
                          "lg", 16)
                block.gpsimd(_g)
                lg_c88 = S.cnt("lg")

                def _t5(tensor):
                    S.wait(tensor, "lg", lg_c88)
                    S.wait(tensor, "cs", S.cnt("cs"))
                    S.inc(tensor.matmul(ps3[0:8, 0:8], s_onesr[0:1, 0:8],
                                        s_rohr[0:1, :], start=True,
                                        stop=True), "ct")
                block.tensor(_t5)
                ct_rohb = S.cnt("ct")

                def _s5(scalar):
                    S.wait(scalar, "ct", ct_rohb)
                    S.inc(scalar.copy(s_rohb[:, :], ps3[0:8, 0:8]), "cs")
                block.scalar(_s5)
                cs_rohb = S.cnt("cs")

                def _t(tensor):
                    S.wait(tensor, "lg", lg_c88)
                    S.wait(tensor, "cs", S.cnt("cs"))
                    S.inc(tensor.transpose(ps2[0:8, 0:8], s_c88[:, :],
                                           s_ident[0:8, 0:8]), "ct")
                block.tensor(_t)
                ct_c88 = S.cnt("ct")

                def _s(scalar):
                    S.wait(scalar, "ct", ct_c88)
                    S.inc(scalar.copy(s_cT[:, :], ps2[0:8, 0:8]), "cs")
                block.scalar(_s)
                cs_c88 = S.cnt("cs")

                def _v(vector):
                    S.wait(vector, "cs", max(cs_c88, cs_rohb))
                    S.inc(vector.tensor_tensor_scan(
                        s_cumr[:, :], s_cT[:, :], s_z8[:, 0:8], 0.0,
                        ALU.add, ALU.add), "cv")
                    S.wait(vector, "cv", S.cnt("cv"))
                    S.inc(vector.tensor_mul(s_cumr[:, :], s_cumr[:, :],
                                            s_rohb[:, :]), "cv")
                    S.wait(vector, "cv", S.cnt("cv"))
                    S.inc(vector.reduce_sum(s_incl[:, :], s_cumr[:, :],
                                            axis=AX.X), "cv")
                    S.wait(vector, "cv", S.cnt("cv"))
                    S.inc(vector.tensor_sub(s_gb8[:, :], s_incl[:, :],
                                            s_mycnt[:, :]), "cv")
                    S.wait(vector, "cv", S.cnt("cv"))
                    S.inc(vector.tensor_scalar(s_ind8[:, :], s_ind8[:, :],
                                               s_gb8[:, 0:1], None, ALU.mult),
                          "cv")
                    S.wait(vector, "cv", S.cnt("cv"))
                    S.inc(vector.tensor_copy(s_incl[:, :], s_gb8[:, :]), "cv")
                block.vector(_v)
                cv_gb = S.cnt("cv")

                cs_base2 = S.cnt("cs")

                def _t(tensor):
                    S.wait(tensor, "cv", cv_gb)
                    for q in range(4):
                        if q > 0:
                            S.wait(tensor, "cs", cs_base2 + q)
                        S.inc(tensor.matmul(ps1[0:1, 0:512], s_ones[0:8, 0:1],
                                            s_ind8[:, 512 * q:512 * (q + 1)],
                                            start=True, stop=True), "ct")
                block.tensor(_t)
                ct_gb0 = S.cnt("ct") - 4

                def _s(scalar):
                    for q in range(4):
                        S.wait(scalar, "ct", ct_gb0 + q + 1)
                        S.inc(scalar.copy(s_gbrow[0:1, 512 * q:512 * (q + 1)],
                                          ps1[0:1, 0:512]), "cs")
                block.scalar(_s)
                cs_gb = S.cnt("cs")

                def _v(vector):
                    S.wait(vector, "cs", max(cs_lp, cs_gb))
                    S.inc(vector.tensor_add(s_gbrow[:, :], s_gbrow[:, :],
                                            s_lpos[:, :]), "cv")
                    S.wait(vector, "cv", S.cnt("cv"))
                    S.inc(vector.tensor_scalar(s_ml[:, :], s_lpos[:, :],
                                               float(CLOC), None, ALU.is_lt),
                          "cv")
                    S.wait(vector, "cv", S.cnt("cv"))
                    S.inc(vector.tensor_scalar(s_mg[:, :], s_gbrow[:, :],
                                               float(CAP), None, ALU.is_lt),
                          "cv")
                    S.wait(vector, "cv", S.cnt("cv"))
                    S.inc(vector.tensor_mul(s_mg[:, :], s_mg[:, :],
                                            s_ml[:, :]), "cv")
                    S.wait(vector, "cv", S.cnt("cv"))
                    S.inc(vector.tensor_scalar(s_offr[:, :], s_ef[:, :],
                                               float(CLOC), None, ALU.mult),
                          "cv")
                    S.wait(vector, "cv", S.cnt("cv"))
                    S.inc(vector.tensor_add(s_offr[:, :], s_offr[:, :],
                                            s_lpos[:, :]), "cv")
                    S.wait(vector, "cv", S.cnt("cv"))
                    S.inc(vector.tensor_scalar(s_dmpr[:, :], s_offr[:, :],
                                               float(DUMP), None,
                                               ALU.subtract), "cv")
                    S.wait(vector, "cv", S.cnt("cv"))
                    S.inc(vector.tensor_mul(s_sofr[:, :], s_ml[:, :],
                                            s_dmpr[:, :]), "cv")
                    S.inc(vector.tensor_mul(s_gofr[:, :], s_mg[:, :],
                                            s_dmpr[:, :]), "cv")
                    S.wait(vector, "cv", S.cnt("cv"))
                    S.inc(vector.tensor_scalar(s_sofr[:, :], s_sofr[:, :],
                                               float(DUMP), None, ALU.add),
                          "cv")
                    S.inc(vector.tensor_scalar(s_gofr[:, :], s_gofr[:, :],
                                               float(DUMP), None, ALU.add),
                          "cv")
                    S.wait(vector, "cv", S.cnt("cv"))
                    S.inc(vector.tensor_copy(s_dmpr[:, :], s_sofr[:, :]), "cv")
                    S.inc(vector.tensor_copy(s_dmpr[:, :], s_gofr[:, :]), "cv")
                block.vector(_v)
                cv_offs = S.cnt("cv")

                if "route" in taps:
                    def _s(scalar):
                        S.wait(scalar, "cv", cv_offs)
                        for ri, row in enumerate((s_sofr, s_gofr, s_ef, s_lpos,
                                                  s_gbrow, s_ml, s_mg)):
                            S.inc(scalar.dma_start(
                                out=tap_aps["t_route"][ri:ri + 1, :],
                                in_=row[:, :]), "la", 16)
                    block.scalar(_s)

                for src_row, dst in ((s_sofr, s_soffc), (s_gofr, s_goffc)):
                    def _t(tensor, src_row=src_row):
                        S.wait(tensor, "cv", cv_offs)
                        S.wait(tensor, "cs", S.cnt("cs"))
                        v = src_row[0:1, :].rearrange("p (t k) -> p k t", k=2)
                        for col in range(16):
                            tt, k = col // 2, col % 2
                            S.inc(tensor.transpose(
                                ps2[:, col:col + 1],
                                v[:, k, 128 * tt:128 * (tt + 1)],
                                s_ident[0:1, 0:1]), "ct")
                    block.tensor(_t)
                    ct_o = S.cnt("ct")

                    def _s(scalar, ct_o=ct_o):
                        S.wait(scalar, "ct", ct_o)
                        S.inc(scalar.copy(s_sm[:, :], ps2[:, 0:16]), "cs")
                    block.scalar(_s)
                    cs_o = S.cnt("cs")

                    def _v(vector, dst=dst, cs_o=cs_o):
                        S.wait(vector, "cs", cs_o)
                        S.inc(vector.tensor_scalar(dst[:, :].bitcast(F32),
                                                   s_sm[:, :], EXP23, None,
                                                   ALU.add), "cv")
                        S.wait(vector, "cv", S.cnt("cv"))
                        S.inc(vector.tensor_scalar(dst[:, :], dst[:, :], MANT,
                                                   None, ALU.bitwise_and), "cv")
                        S.wait(vector, "cv", S.cnt("cv"))
                        S.inc(vector.tensor_copy(s_sm[:, :],
                                                 dst[:, :].bitcast(F32)), "cv")
                    block.vector(_v)
                cv_offc = S.cnt("cv")

                if "route_only" in stages:
                    def _out(gpsimd):
                        S.wait(gpsimd, "cv", cv_offc)
                        for tt in range(NTOKT):
                            S.inc(gpsimd.dma_start(
                                out=o_out[128 * tt:128 * (tt + 1), :],
                                in_=d_xy.ap().rearrange(
                                    "(d r) t -> d (r t)", r=8)[0:128, 0:DD]),
                                  "st", 16)
                        S.wait(gpsimd, "st", S.cnt("st"))
                    block.gpsimd(_out)
                    return

            lg_zero_done = S.cnt("lg")
            for tt in range(NTOKT):
                for half in range(2):
                    def _t(tensor, tt=tt, half=half):
                        S.wait(tensor, "cv", cv_offc)
                        S.wait(tensor, "cs", S.cnt("cs"))
                        for j in range(8):
                            dt = 8 * half + j
                            S.inc(tensor.transpose(
                                pst[:, 128 * j:128 * (j + 1)],
                                s_y1[dt][:, 128 * tt:128 * (tt + 1)],
                                s_identb[:, :]), "ct")
                    block.tensor(_t)
                    ct_y1t = S.cnt("ct")

                    def _s(scalar, half=half, ct_y1t=ct_y1t):
                        S.wait(scalar, "ct", ct_y1t)
                        S.wait(scalar, "lg", S.cnt("lg"))
                        S.inc(scalar.copy(
                            s_y1tst[:, 1024 * half:1024 * (half + 1)],
                            pst[:, :]), "cs")
                    block.scalar(_s)
                cs_y1t = S.cnt("cs")

                def _g(gpsimd, tt=tt, cs_y1t=cs_y1t):
                    S.wait(gpsimd, "cs", cs_y1t)
                    S.inc(gpsimd.dma_start(
                        out=d_y1t[128 * tt:128 * (tt + 1), :],
                        in_=s_y1tst[:, :]), "lg", 16)
                    for k in range(2):
                        S.inc(gpsimd.indirect_dma_start(
                            d_gath.ap(),
                            bass.IndirectOffsetOnAxis(
                                ap=s_soffc[:, 2 * tt + k:2 * tt + k + 1],
                                axis=0),
                            s_y1tst[:, :], None), "lg", 16)
                block.gpsimd(_g)
            lg_scat = S.cnt("lg")

            s_h = [sb(fctx, f"s_h{i}", [128, TOK], BF16) for i in range(NJT)]
            with ExitStack() as uctx:
                s_wg = [sb(uctx, f"s_wg{p}", [128, DD], BF16) for p in range(2)]
                s_wu = [sb(uctx, f"s_wu{p}", [128, DD], BF16) for p in range(2)]
                s_g2 = [sb(uctx, f"s_g2{i}", [128, 512]) for i in range(2)]
                lastcv_ps, lastcs_ps = {}, {}
                lastcs_wu = {0: 0, 1: 0}

                for p in range(NJT):
                    pb = p % 2

                    def _g(gpsimd, p=p, pb=pb):
                        S.wait(gpsimd, "cs", lastcs_wu[pb])
                        S.inc(gpsimd.dma_start(
                            out=s_wg[pb][:, :], in_=i_swug[p]), "lg", 16)
                        S.inc(gpsimd.dma_start(
                            out=s_wu[pb][:, :], in_=i_swug[NJT + p]),
                              "lg", 16)
                    block.gpsimd(_g)
                    lg_w = S.cnt("lg")

                    for c2 in range(2):
                        ub = (2 * p + c2) % 2
                        pg, pu = ((ps1, ps2), (ps3, ps4))[ub]
                        gk = "sup%d" % ub

                        def _t(tensor, pb=pb, lg_w=lg_w, c2=c2, pg=pg, pu=pu,
                               gk=gk):
                            S.wait(tensor, "lg", lg_w)
                            S.wait(tensor, "cv", lastcv_ps.get(gk, 0))
                            S.wait(tensor, "cs", lastcs_ps.get(gk, 0))
                            for j in range(16):
                                S.inc(tensor.matmul(
                                    pg[:, :], s_wg[pb][:, 128 * j:128 * (j + 1)],
                                    s_y1[j][:, 512 * c2:512 * (c2 + 1)],
                                    start=(j == 0), stop=(j == 15)), "ct")
                            for j in range(16):
                                S.inc(tensor.matmul(
                                    pu[:, :], s_wu[pb][:, 128 * j:128 * (j + 1)],
                                    s_y1[j][:, 512 * c2:512 * (c2 + 1)],
                                    start=(j == 0), stop=(j == 15)), "ct")
                        block.tensor(_t)
                        ct_up = S.cnt("ct")

                        def _s(scalar, ct_up=ct_up, pg=pg):
                            S.wait(scalar, "ct", ct_up)
                            S.wait(scalar, "cv", S.cnt("cv"))
                            S.inc(scalar.activation(s_g2[(2 * p + c2) % 2][:, :],
                                                    pg[:, :], AF.Silu), "cs")
                        block.scalar(_s)
                        cs_si = S.cnt("cs")
                        lastcs_ps[gk] = cs_si

                        def _v(vector, c2=c2, p=p, cs_si=cs_si, pu=pu):
                            S.wait(vector, "cs", cs_si)
                            S.inc(vector.tensor_mul(
                                s_h[p][:, 512 * c2:512 * (c2 + 1)],
                                s_g2[(2 * p + c2) % 2][:, :], pu[:, :]), "cv")
                        block.vector(_v)
                        lastcv_ps[gk] = S.cnt("cv")
                    lastcs_wu[pb] = S.cnt("cs")
                cv_h = S.cnt("cv")

                pass

        with ExitStack() as dctx:
            s_wdh = [sb(dctx, f"s_wdh{p}", [128, JH // 2], BF16)
                     for p in range(2)]
            s_yst = sb(dctx, "s_yst", [128, TOK], BF16)
            s_gT = [sb(dctx, f"s_gT{j}", [128, CLOC], BF16)
                    for j in range(16)]
            s_he = [[sb(dctx, f"s_he{par}_{i}", [128, CLOC], BF16)
                     for i in range(NIT)] for par in range(2)]
            s_wg2 = [sb(dctx, f"s_wg2{p}", [128, DD], BF16) for p in range(2)]
            s_wu2 = [sb(dctx, f"s_wu2{p}", [128, DD], BF16) for p in range(2)]
            s_wd2q = [[sb(dctx, f"s_wd2q{par}_{p}", [128, 512], BF16)
                       for p in range(NIT)] for par in range(2)]
            s_ge2 = [sb(dctx, f"s_ge2{i}", [128, CLOC]) for i in range(2)]
            s_ost512 = [sb(dctx, f"s_ost512{i}", [128, 512], BF16)
                        for i in range(3)]


            cs_ph0 = S.cnt("cs")
            lastcs_sdn = [cs_ph0]
            lastcs_w2 = {0: cs_ph0, 1: cs_ph0}
            lastcs_gt = [cs_ph0]
            lastcs_psq = {0: cs_ph0, 1: cs_ph0}
            lastcv_psq = {0: 0, 1: 0}
            upcnt = [0]
            lastcs_dnev = [cs_ph0]
            cs_dn_done = {0: cs_ph0, 1: cs_ph0}
            lg_wdset = {}
            la_ostp = {i: S.cnt("la") for i in range(3)}
            ostp = [0]
            moe_state = {}
            NSET = 4 * EE

            def _wdset_load(gpsimd, si):
                e2, q2 = si // 4, si % 4
                for p in range(NIT):
                    S.inc(gpsimd.dma_start(
                        out=s_wd2q[si % 2][p][:, :],
                        in_=i_wdn[e2][128 * p:128 * (p + 1),
                                      512 * q2:512 * (q2 + 1)]), "lg", 16)
                lg_wdset[si] = S.cnt("lg")

            def _g0(gpsimd):
                for si in (0, 1):
                    _wdset_load(gpsimd, si)
            block.gpsimd(_g0)

            pending_gw = []

            def flush_gw():
                while pending_gw:
                    si2, ctg = pending_gw.pop(0)

                    def _gw(gpsimd, si2=si2, ctg=ctg):
                        S.wait(gpsimd, "ct", ctg)
                        _wdset_load(gpsimd, si2)
                    block.gpsimd(_gw)



            NJH = NJT // 2

            def emit_sdown(dt):
                for half in range(2):
                    def _l(gpsimd, dt=dt, half=half):
                        S.wait(gpsimd, "ct",
                               moe_state.get(("ctwd", half), 0))
                        S.inc(gpsimd.dma_start(
                            out=s_wdh[half][:, :],
                            in_=i_swdn[dt][:, (JH // 2) * half:
                                           (JH // 2) * (half + 1)]),
                              "lg", 16)
                    block.gpsimd(_l)
                    moe_state[("lgwd", half)] = S.cnt("lg")
                flush_gw()

                for half in range(2):
                    lg_wd = moe_state[("lgwd", half)]

                    def _t(tensor, half=half, lg_wd=lg_wd):
                        S.wait(tensor, "lg", lg_wd)
                        S.wait(tensor, "cv", cv_h)
                        if half == 0:
                            S.wait(tensor, "cs", lastcs_sdn[0])
                        for i in range(NJH):
                            ig = NJH * half + i
                            for c2 in range(2):
                                S.inc(tensor.matmul(
                                    pse[:, 512 * c2:512 * (c2 + 1)],
                                    s_wdh[half][:, 128 * i:128 * (i + 1)],
                                    s_h[ig][:, 512 * c2:512 * (c2 + 1)],
                                    start=(ig == 0),
                                    stop=(ig == NJT - 1)), "ct")
                    block.tensor(_t)
                    ct_dn = S.cnt("ct")
                    moe_state[("ctwd", half)] = ct_dn

                def _s(scalar, dt=dt, ct_dn=ct_dn):
                    S.wait(scalar, "ct", ct_dn)
                    S.wait(scalar, "la", S.cnt("la"))
                    S.inc(scalar.copy(s_yst[:, 0:512], pse[:, 0:512]), "cs")
                    S.inc(scalar.copy(s_yst[:, 512:1024],
                                      pse[:, 512:1024]), "cs")
                    S.wait(scalar, "cs", S.cnt("cs"))
                    S.inc(scalar.dma_start(
                        out=d_ysh[128 * dt:128 * (dt + 1), :],
                        in_=s_yst[:, :]), "la", 16)
                block.scalar(_s)
                lastcs_sdn[0] = S.cnt("cs")

            def emit_moe_gt(e):
                def _ls(sync, e=e):
                    S.wait(sync, "lg", lg_scat)
                    S.wait(sync, "cs", lastcs_gt[0])
                    for j in range(16):
                        S.inc(sync.dma_start(
                            out=s_gT[j][:, :],
                            in_=d_gath[CLOC * e:CLOC * (e + 1),
                                       128 * j:128 * (j + 1)],
                            transpose=True), "ls", 16)
                block.sync(_ls)
                moe_state[("lsg", e)] = S.cnt("ls")

            def emit_moe_up(e, p):
                pb = p % 2
                par = e % 2

                def _g(gpsimd, e=e, p=p, pb=pb):
                    S.wait(gpsimd, "cs", lastcs_w2[pb])
                    S.inc(gpsimd.dma_start(
                        out=s_wg2[pb][:, :], in_=i_wug[e][p]), "lg", 16)
                    S.inc(gpsimd.dma_start(
                        out=s_wu2[pb][:, :], in_=i_wug[e][NIT + p]),
                          "lg", 16)
                block.gpsimd(_g)
                lg_w2 = S.cnt("lg")
                ls_g = moe_state[("lsg", e)]
                flush_gw()

                ub2 = upcnt[0] % 2
                upcnt[0] += 1
                pg2, pu2 = ((ps1, ps2), (ps3, ps4))[ub2]
                cs_gate = lastcs_psq[ub2]
                if ub2 == 1:
                    cs_gate = max(cs_gate, lastcs_dnev[0])

                def _t(tensor, pb=pb, lg_w2=lg_w2, ls_g=ls_g,
                       pg2=pg2, pu2=pu2, ub2=ub2, cs_gate=cs_gate):
                    S.wait(tensor, "lg", lg_w2)
                    S.wait(tensor, "ls", ls_g)
                    S.wait(tensor, "cv", lastcv_psq[ub2])
                    S.wait(tensor, "cs", cs_gate)
                    for j in range(16):
                        S.inc(tensor.matmul(
                            pg2[:, 0:CLOC],
                            s_wg2[pb][:, 128 * j:128 * (j + 1)],
                            s_gT[j][:, :], start=(j == 0),
                            stop=(j == 15)), "ct")
                    for j in range(16):
                        S.inc(tensor.matmul(
                            pu2[:, 0:CLOC],
                            s_wu2[pb][:, 128 * j:128 * (j + 1)],
                            s_gT[j][:, :], start=(j == 0),
                            stop=(j == 15)), "ct")
                block.tensor(_t)
                ct_up2 = S.cnt("ct")

                def _s(scalar, ct_up2=ct_up2, pb=pb, pg2=pg2, ub2=ub2,
                       cv_gate=lastcv_psq[ub2]):
                    S.wait(scalar, "ct", ct_up2)
                    S.wait(scalar, "cv", cv_gate)
                    S.inc(scalar.activation(s_ge2[ub2][:, 0:CLOC],
                                            pg2[:, 0:CLOC], AF.Silu), "cs")
                block.scalar(_s)
                cs_si2 = S.cnt("cs")
                lastcs_psq[ub2] = cs_si2
                lastcs_w2[pb] = cs_si2

                def _v(vector, p=p, cs_si2=cs_si2, pb=pb, par=par,
                       pu2=pu2, ub2=ub2):
                    S.wait(vector, "cs", cs_si2)
                    S.wait(vector, "cs", cs_dn_done[par])
                    S.inc(vector.tensor_mul(s_he[par][p][:, :],
                                            s_ge2[ub2][:, 0:CLOC],
                                            pu2[:, 0:CLOC]), "cv")
                block.vector(_v)
                lastcv_psq[ub2] = S.cnt("cv")
                if p == NIT - 1:
                    moe_state[("cvhe", e)] = S.cnt("cv")
                    lastcs_gt[0] = S.cnt("cs")

            STB = ((0, 128, "ps3"), (128, 128, "ps4"), (256, 64, "psf"))

            def emit_moe_dn(e, q):
                flush_gw()
                si = 4 * e + q
                par = si % 2
                hpar = e % 2
                lg_set = lg_wdset[si]
                cv_he = moe_state[("cvhe", e)]
                psb = {"ps3": ps3, "ps4": ps4, "psf": psf}

                def _t(tensor, e=e, par=par, hpar=hpar,
                       lg_set=lg_set, cv_he=max(cv_he, lastcv_psq[1]),
                       cs_gate=max(lastcs_dnev[0], lastcs_psq[1])):
                    S.wait(tensor, "lg", lg_set)
                    S.wait(tensor, "cv", cv_he)
                    S.wait(tensor, "cs", cs_gate)
                    for p in range(NIT):
                        for (r0, nr, pn) in STB:
                            S.inc(tensor.matmul(
                                psb[pn][0:nr, :],
                                s_he[hpar][p][:, r0:r0 + nr],
                                s_wd2q[par][p][:, :],
                                start=(p == 0), stop=(p == NIT - 1)), "ct")
                block.tensor(_t)
                ct_g = S.cnt("ct")

                def _s(scalar, e=e, q=q, si=si, ct_g=ct_g):
                    S.wait(scalar, "ct", ct_g)
                    for (r0, nr, pn) in STB:
                        op = ostp[0]
                        ostp[0] = (ostp[0] + 1) % 3
                        S.wait(scalar, "la", la_ostp[op])
                        S.inc(scalar.copy(s_ost512[op][0:nr, :],
                                          psb[pn][0:nr, :]), "cs")
                        S.wait(scalar, "cs", S.cnt("cs"))
                        S.inc(scalar.dma_start(
                            out=d_outb[CLOC * e + r0:CLOC * e + r0 + nr,
                                       512 * q:512 * (q + 1)],
                            in_=s_ost512[op][0:nr, :]), "la", 16)
                        la_ostp[op] = S.cnt("la")
                block.scalar(_s)
                lastcs_dnev[0] = S.cnt("cs")
                if q == 3:
                    cs_dn_done[e % 2] = S.cnt("cs")
                if si + 2 < NSET:
                    pending_gw.append((si + 2, ct_g))

            moe_steps = []
            for e in range(EE):
                moe_steps.append(("gt", e))
                ups = [("up", e, p) for p in range(NIT)]
                dns = ([("dn", e - 1, q) for q in range(4)]
                       if e > 0 else [])
                mix = []
                ui = di = 0
                for slot_i in range(len(ups) + len(dns)):
                    if di < len(dns) and slot_i % 4 == 2:
                        mix.append(dns[di]); di += 1
                    elif ui < len(ups):
                        mix.append(ups[ui]); ui += 1
                    else:
                        mix.append(dns[di]); di += 1
                moe_steps.extend(mix)
            moe_steps.extend([("dn", EE - 1, q) for q in range(4)])

            k = 0

            def emit_one():
                nonlocal k
                st_ = moe_steps[k]
                if st_[0] == "gt":
                    emit_moe_gt(st_[1])
                elif st_[0] == "up":
                    emit_moe_up(st_[1], st_[2])
                else:
                    emit_moe_dn(st_[1], st_[2])
                k += 1

            per_dt = (len(moe_steps) + 15) // 16
            for dt in range(16):
                emit_sdown(dt)
                for _ in range(per_dt):
                    if k < len(moe_steps):
                        emit_one()
            while k < len(moe_steps):
                emit_one()
            la_moe = S.cnt("la")
            la_ysh = la_moe

        with ExitStack() as octx:
            s_ta = sb(octx, "s_ta", [128, DD], BF16)
            s_gA = sb(octx, "s_gA", [128, DD], BF16)
            s_gB = sb(octx, "s_gB", [128, DD], BF16)
            s_sh = sb(octx, "s_sh", [128, DD], BF16)
            s_z = sb(octx, "s_z", [128, DD])
            s_sq2 = sb(octx, "s_sq2", [128, DD])
            s_m2 = sb(octx, "s_m2", [128, 1])
            s_ss2 = sb(octx, "s_ss2", [128, 1])
            s_msq = sb(octx, "s_msq", [128, 1])
            s_v2 = sb(octx, "s_v2", [128, 1])
            s_nm2 = sb(octx, "s_nm2", [128, 1])
            s_orow = sb(octx, "s_orow", [128, DD])

            for tt in range(NTOKT):
                def _g(gpsimd, tt=tt):
                    S.wait(gpsimd, "la", la_moe)
                    S.wait(gpsimd, "cv", S.cnt("cv"))
                    S.inc(gpsimd.dma_start(
                        out=s_ta[:, :],
                        in_=d_y1t[128 * tt:128 * (tt + 1), :]), "lg", 16)
                    S.inc(gpsimd.indirect_dma_start(
                        s_gA[:, :], None, d_outb.ap(),
                        bass.IndirectOffsetOnAxis(
                            ap=s_goffc[:, 2 * tt:2 * tt + 1], axis=0)),
                        "lg", 16)
                    S.inc(gpsimd.indirect_dma_start(
                        s_gB[:, :], None, d_outb.ap(),
                        bass.IndirectOffsetOnAxis(
                            ap=s_goffc[:, 2 * tt + 1:2 * tt + 2], axis=0)),
                        "lg", 16)
                block.gpsimd(_g)
                lg_cmb = S.cnt("lg")

                def _ls(sync, tt=tt):
                    S.wait(sync, "la", la_moe)
                    S.wait(sync, "cv", S.cnt("cv"))
                    S.inc(sync.dma_start(
                        out=s_sh[:, :],
                        in_=d_ysh[0:DD, 128 * tt:128 * (tt + 1)],
                        transpose=True), "ls", 16)
                block.sync(_ls)
                ls_sh = S.cnt("ls")

                def _v(vector, tt=tt, lg_cmb=lg_cmb, ls_sh=ls_sh):
                    S.wait(vector, "lg", lg_cmb)
                    S.wait(vector, "ls", ls_sh)
                    S.wait(vector, "cs", S.cnt("cs"))
                    S.inc(vector.tensor_scalar(s_z[:, :], s_gA[:, :],
                                               s_w1c[:, tt:tt + 1], None,
                                               ALU.mult), "cv")
                    S.wait(vector, "cv", S.cnt("cv"))
                    S.inc(vector.scalar_tensor_tensor(
                        s_z[:, :], s_gB[:, :], s_w2c[:, tt:tt + 1], s_z[:, :],
                        ALU.mult, ALU.add), "cv")
                    S.wait(vector, "cv", S.cnt("cv"))
                    S.inc(vector.tensor_add(s_z[:, :], s_z[:, :],
                                            s_ta[:, :]), "cv")
                    S.wait(vector, "cv", S.cnt("cv"))
                    S.inc(vector.tensor_add(s_z[:, :], s_z[:, :],
                                            s_sh[:, :]), "cv")
                    S.wait(vector, "cv", S.cnt("cv"))
                    S.inc(vector.reduce_sum(s_m2[:, :], s_z[:, :],
                                            axis=AX.X), "cv")
                    S.wait(vector, "cv", S.cnt("cv"))
                    S.inc(vector.tensor_scalar(s_m2[:, :], s_m2[:, :],
                                               1.0 / DD, None, ALU.mult), "cv")
                block.vector(_v)
                cv_z = S.cnt("cv")

                def _s(scalar, cv_z=cv_z):
                    S.wait(scalar, "cv", cv_z)
                    S.inc(scalar.activation(s_sq2[:, :], s_z[:, :], AF.Square,
                                            accum_out=s_ss2[:, 0:1]), "cs")
                block.scalar(_s)
                cs_ss = S.cnt("cs")

                def _v(vector, cs_ss=cs_ss):
                    S.wait(vector, "cs", cs_ss)
                    S.inc(vector.tensor_mul(s_msq[:, :], s_m2[:, :],
                                            s_m2[:, :]), "cv")
                    S.wait(vector, "cv", S.cnt("cv"))
                    S.inc(vector.tensor_scalar(s_v2[:, :], s_ss2[:, :],
                                               1.0 / DD, None, ALU.mult), "cv")
                    S.wait(vector, "cv", S.cnt("cv"))
                    S.inc(vector.tensor_sub(s_v2[:, :], s_v2[:, :],
                                            s_msq[:, :]), "cv")
                block.vector(_v)
                cv_v2 = S.cnt("cv")

                def _v3(vector, cv_v2=cv_v2):
                    S.wait(vector, "cv", cv_v2)
                    S.inc(vector.tensor_scalar(s_v2[:, :], s_v2[:, :], 1e-5,
                                               None, ALU.add), "cv")
                    S.wait(vector, "cv", S.cnt("cv"))
                    S.inc(vector.tensor_copy(s_msq[:, :], s_v2[:, :]), "cv")
                block.vector(_v3)
                cv_v2b = S.cnt("cv")

                def _s(scalar, cv_v2b=cv_v2b):
                    S.wait(scalar, "cv", cv_v2b)
                    S.inc(scalar.activation(s_v2[:, :], s_v2[:, :], AF.Ln), "cs")
                    S.wait(scalar, "cs", S.cnt("cs"))
                    S.inc(scalar.mul(s_v2[:, :], s_v2[:, :], -0.5), "cs")
                    S.wait(scalar, "cs", S.cnt("cs"))
                    S.inc(scalar.activation(s_v2[:, :], s_v2[:, :], AF.Exp), "cs")
                block.scalar(_s)
                cs_r2 = S.cnt("cs")

                def _v(vector, cs_r2=cs_r2):
                    S.wait(vector, "cs", cs_r2)
                    S.inc(vector.scalar_tensor_tensor(
                        s_nm2[:, :], s_m2[:, :], -1.0, s_v2[:, :],
                        ALU.mult, ALU.mult), "cv")
                    S.wait(vector, "cv", S.cnt("cv"))
                block.vector(_v)
                cv_nm2 = S.cnt("cv")

                def _v4(vector, cv_nm2=cv_nm2):
                    S.wait(vector, "cv", cv_nm2)
                    S.inc(vector.tensor_scalar(s_orow[:, :], s_z[:, :],
                                               s_v2[:, 0:1], s_nm2[:, 0:1],
                                               ALU.mult, ALU.add), "cv")
                block.vector(_v4)
                cv_out = S.cnt("cv")

                def _s(scalar, tt=tt, cv_out=cv_out):
                    S.wait(scalar, "cv", cv_out)
                    S.inc(scalar.dma_start(
                        out=o_out[128 * tt:128 * (tt + 1), :],
                        in_=s_orow[:, :]), "st", 16)
                block.scalar(_s)

            def _fin(scalar):
                S.wait(scalar, "st", S.cnt("st"))
            block.scalar(_fin)


def _host_inputs(inputs):
    import ml_dtypes
    x = np.ascontiguousarray(inputs["L0_attn_x"], dtype=np.float32)
    alog = np.ascontiguousarray(inputs["L0_attn_A_log"], dtype=np.float32)
    dsk = np.ascontiguousarray(inputs["L0_attn_D"], dtype=np.float32)
    wx = np.ascontiguousarray(inputs["L0_attn_W_x"], dtype=np.float32)
    wdt = np.ascontiguousarray(inputs["L0_attn_W_dt"], dtype=np.float32)
    bdt = np.ascontiguousarray(inputs["L0_attn_b_dt"], dtype=np.float32)
    rout = np.ascontiguousarray(inputs["L0_ffn_router"], dtype=np.float32)
    wug = np.ascontiguousarray(inputs["L0_ffn_w_up_gate"], dtype=np.float32)
    wdn = np.ascontiguousarray(inputs["L0_ffn_w_down"], dtype=np.float32)
    swug = np.ascontiguousarray(inputs["L0_ffn_shared_w_up_gate"],
                                dtype=np.float32)
    swdn = np.ascontiguousarray(inputs["L0_ffn_shared_w_down"],
                                dtype=np.float32)

    wugP = np.ascontiguousarray(
        wug.reshape(EE, 16, 128, 2 * NIT, 128).transpose(0, 3, 2, 1, 4)
           .reshape(EE, 2 * NIT, 128, 16 * 128))
    swugP = np.ascontiguousarray(
        swug.reshape(16, 128, 2 * NJT, 128).transpose(2, 1, 0, 3)
            .reshape(2 * NJT, 128, 16 * 128))
    swdnP = np.ascontiguousarray(
        swdn.reshape(NJT, 128, 16, 128).transpose(2, 1, 0, 3)
            .reshape(16, 128, NJT * 128))

    ident = np.eye(128, dtype=np.float32)
    identb = np.eye(128, dtype=ml_dtypes.bfloat16)
    ones = np.ones((128, 1), dtype=np.float32)
    iota8c = np.arange(8, dtype=np.float32).reshape(8, 1)

    in_maps = []
    for r in range(NCORE):
        dsl = slice(DSL * r, DSL * (r + 1))
        xyrl = np.empty((128, 16), dtype=np.int32)
        for j in range(16):
            xyrl[:, j] = (128 * j + np.arange(128)) * 8 + r
        strl = (np.arange(16, dtype=np.int32) * 8 + r).reshape(16, 1)
        roh = np.zeros((1, 8), dtype=np.float32)
        roh[0, r] = 1.0
        m = {
            "x_dsl": np.ascontiguousarray(x[:, :, dsl]),
            "wx_sl": np.ascontiguousarray(wx[:, dsl]),
            "wdt_sl": np.ascontiguousarray(wdt[dsl, :]),
            "bdt_sl": np.ascontiguousarray(bdt[dsl].reshape(DSL, 1)),
            "alog_sl": np.ascontiguousarray(alog[dsl, :]),
            "d_sl": np.ascontiguousarray(dsk[dsl].reshape(DSL, 1)),
            "rout_sl": np.ascontiguousarray(rout[dsl, :]),
            "rout": rout,
            "wug": wugP, "wdn": wdn, "swug": swugP, "swdn": swdnP,
            "identf": ident, "identb": identb, "onesf": ones,
            "xyrl": xyrl,
            "strl_a": np.ascontiguousarray(strl[0:8]),
            "strl_b": np.ascontiguousarray(strl[8:10]),
            "strl_c": np.ascontiguousarray(np.array([[9 * 8 + r], [9 * 8 + r]],
                                                    dtype=np.int32)),
            "roh": roh, "iota8c": iota8c,
            "onesr": np.ones((1, 128), dtype=np.float32),
            "sel16": np.repeat(np.eye(16, dtype=np.float32), 128,
                               axis=1).reshape(16, 2048),
        }
        in_maps.append(m)
    return in_maps


def kernel(**inputs):
    nc, _ = build()
    in_maps = _host_inputs(inputs)
    res = run_bass_kernel_spmd(nc, in_maps, core_ids=list(range(NCORE)))
    out = np.empty((BB, TT_, DD), dtype=np.float32)
    for r in range(NCORE):
        b = r // 2
        t0 = (TT_ // 2) * (r % 2)
        out[b, t0:t0 + TOK, :] = np.asarray(res.results[r]["out_tok"])
    return out
